# version 14
# baseline (speedup 1.0000x reference)
import math
from dataclasses import dataclass

import numpy as np
import ml_dtypes

import concourse.bass as bass
import concourse.mybir as mybir
from concourse.bass_utils import run_bass_kernel_spmd
from concourse.bass_core_rust import IndirectOffsetOnAxis

F32 = mybir.dt.float32
BF16 = mybir.dt.bfloat16
I32 = mybir.dt.int32
AF = mybir.ActivationFunctionType
ALU = mybir.AluOpType
AX = mybir.AxisListType

ENGS = ("pe", "act", "dve", "pool", "sp")
DMA_SEMS = ("w1", "w2", "wst", "hwr", "hrd", "xld", "out", "cst", "cstp", "awr")


@dataclass
class Cfg:
    D: int = 1024
    S: int = 4096
    E: int = 8
    HF: int = 4096
    TT: int = 1024
    QT: int = 512
    CAP: int = 3072

    @property
    def NDC(self): return self.D // 128
    @property
    def NTC(self): return self.S // 128
    @property
    def NQT(self): return self.S // self.QT
    @property
    def HC(self): return self.HF // 128
    @property
    def NP1(self): return self.HF // 512
    @property
    def NDH(self): return max(1, self.D // 512)
    @property
    def DHW(self): return min(512, self.D)
    @property
    def NTCT(self): return self.TT // 128
    @property
    def NTT(self): return self.S // self.TT
    @property
    def NCH(self): return self.CAP // self.TT
    @property
    def NSG(self): return self.E * self.NCH
    @property
    def RW(self): return self.D + 2 * self.E
    @property
    def scale(self): return 1.0 / math.sqrt(self.D)


class Planner:
    def __init__(self):
        self.q = {e: [] for e in ENGS}
        self.cnt = {}
        self.last_event = {e: None for e in ENGS}

    def sem(self, name):
        self.cnt[name] = 0

    def __post_init_events(self):
        pass

    def op(self, eng, emit, waits=(), incs=(), tag=""):
        w = []
        for x in waits:
            if x is None:
                continue
            s, v = x
            assert s in self.cnt, s
            w.append((s, int(v)))
        out = []
        for s, n in incs:
            if s not in self.cnt:
                self.cnt[s] = 0
            self.cnt[s] += n
            out.append((s, self.cnt[s]))
        if out:
            self.last_event[eng] = out[0]
        self.q[eng].append([tuple(w), emit, tuple(incs), tag])
        return out

    def inc_last(self, eng, sem):
        assert self.q[eng], eng
        last = self.q[eng][-1]
        assert not last[2], last
        if sem not in self.cnt:
            self.cnt[sem] = 0
        self.cnt[sem] += 1
        last[2] = ((sem, 1),)
        self.last_event[eng] = (sem, self.cnt[sem])
        return 1

    def check(self):
        for e in ENGS:
            for waits, _, _, tag in self.q[e]:
                for s, v in waits:
                    assert v <= self.cnt[s], \
                        f"unreachable wait {s}>={v} (final {self.cnt[s]}) at {e}:{tag}"
        cur = {s: 0 for s in self.cnt}
        ptr = {e: 0 for e in ENGS}
        total = sum(len(self.q[e]) for e in ENGS)
        done = 0
        while done < total:
            progressed = False
            for e in ENGS:
                while ptr[e] < len(self.q[e]):
                    waits, _, incs, tag = self.q[e][ptr[e]]
                    if all(cur[s] >= v for s, v in waits):
                        for s, n in incs:
                            cur[s] += n
                        ptr[e] += 1
                        done += 1
                        progressed = True
                    else:
                        break
            if not progressed:
                details = []
                for e in ENGS:
                    if ptr[e] < len(self.q[e]):
                        waits, _, _, tag = self.q[e][ptr[e]]
                        blocked = [(s, v, cur[s]) for s, v in waits if cur[s] < v]
                        details.append(f"{e}@{ptr[e]} tag={tag} blocked_on={blocked}")
                raise RuntimeError("DEADLOCK in plan:\n" + "\n".join(details))


class Builder:
    def __init__(self, cfg: Cfg):
        self.cfg = cfg
        nc = bass.Bass("TRN2", target_bir_lowering=False)
        self.nc = nc
        self.p = Planner()
        for s in ENGS:
            self.p.sem(s)

        c = cfg
        dt_ = nc.dram_tensor
        self.x_d = dt_("x", [c.S, c.D], F32, kind="ExternalInput")
        self.wq_d = dt_("wq", [c.D, c.D], F32, kind="ExternalInput")
        self.wk_d = dt_("wk", [c.D, c.D], F32, kind="ExternalInput")
        self.wv_d = dt_("wv", [c.D, c.D], F32, kind="ExternalInput")
        self.wp_d = dt_("wp", [c.D, c.D], F32, kind="ExternalInput")
        self.bq_d = dt_("bq", [c.D], F32, kind="ExternalInput")
        self.bk_d = dt_("bk", [c.D], F32, kind="ExternalInput")
        self.bv_d = dt_("bv", [c.D], F32, kind="ExternalInput")
        self.bp_d = dt_("bp", [c.D], F32, kind="ExternalInput")
        self.gw_d = dt_("gw", [c.E, c.D], F32, kind="ExternalInput")
        self.gb_d = dt_("gb", [c.E], F32, kind="ExternalInput")
        self.w1_d = dt_("w1", [c.E, c.D, c.HF], F32, kind="ExternalInput")
        self.b1_d = dt_("b1", [c.E, c.HF], F32, kind="ExternalInput")
        self.w2_d = dt_("w2", [c.E, c.HF, c.D], F32, kind="ExternalInput")
        self.b2_d = dt_("b2", [c.E, c.D], F32, kind="ExternalInput")
        self.y_d = dt_("y", [c.S, c.D], F32, kind="ExternalOutput")
        self.attn_d = dt_("attn_scr", [c.S, c.D], F32, kind="Internal")
        self.y_d_scr = dt_("y_scr", [128, c.NDC * c.S], BF16, kind="Internal")
        self.h_d = dt_("h_scr", [c.NSG, c.HF, c.TT], BF16, kind="Internal")
        self.xrow_d = dt_("xrow_scr", [c.S, c.RW], BF16, kind="Internal")
        self.xg_d = dt_("xg_scr", [c.E * c.CAP, c.RW], BF16, kind="Internal")
        self.outsl_d = dt_("outsl_scr", [c.E * c.CAP, c.D], F32, kind="Internal")
        self.w1b_d = dt_("w1b_scr", [c.E, c.NP1, 128, c.NDC * 4 * 128], BF16, kind="Internal")
        self.wqb_d = dt_("wqb_scr", [c.D, c.D], BF16, kind="Internal")
        self.wkb_d = dt_("wkb_scr", [c.D, c.D], BF16, kind="Internal")
        self.wvb_d = dt_("wvb_scr", [c.D, c.D], BF16, kind="Internal")
        self.wpb_d = dt_("wpb_scr", [c.D, c.D], BF16, kind="Internal")
        self.w2b_d = dt_("w2b_scr", [c.E, c.NDH, c.NP1, 128, 4 * c.DHW], BF16, kind="Internal")

        self.id_bf_d = nc.inline_tensor(np.eye(128, dtype=ml_dtypes.bfloat16), "id_bf_c")
        self.id_f_d = nc.inline_tensor(np.eye(128, dtype=np.float32), "id_f_c")
        self.ones_f_d = nc.inline_tensor(np.ones((1, 128), dtype=np.float32), "ones_f_c")
        m = np.zeros((128, 4, 512), dtype=np.float32)
        for d in range(4):
            for kp in range(128):
                qf0 = kp + 128 * d
                if qf0 < 512:
                    m[kp, d, qf0:] = 1.0
        self.mask_d = nc.inline_tensor(m.reshape(128, 2048).astype(ml_dtypes.bfloat16), "mask_c")
        ecap = np.zeros((128, c.NTC * c.E), dtype=np.float32)
        for e in range(c.E):
            ecap[:, e::c.E] = e * c.CAP
        self.ecap_d = nc.inline_tensor(ecap, "ecap_c")

        ARENA = 208960
        BASE = 16512
        ah = nc.alloc_sbuf_tensor("arena", [128, ARENA], mybir.dt.uint8)
        assert nc.lookup_mloc(ah).addr == BASE, nc.lookup_mloc(ah).addr
        self.ARENA = ARENA
        A_raw = nc.alloc_sbuf_tensor_at

        def A(name, shape, dtype, offset):
            return A_raw(name, shape, dtype, offset=offset + BASE)
        self._off = 0

        def at(name, shape, dtype):
            self._off = (self._off + 63) // 64 * 64
            t = A(name, shape, dtype, offset=self._off)
            self._off += int(np.prod(shape[1:])) * mybir.dt.size(dtype)
            return t

        self.id_bf = at("id_bf_s", [128, 128], BF16)
        self.id_f = at("id_f_s", [128, 128], F32)
        self.ones_f = at("ones_f_s", [1, 128], F32)
        self.masks = at("masks_s", [128, 2048], BF16)
        self.bqkT = at("bqkT", [128, 2 * c.NDC], F32)
        self.b1T = at("b1T", [128, c.HC * c.E], F32)
        self.b2_bf = at("b2_bf", [c.E, c.D], BF16)
        self.bp_row = at("bp_row", [1, c.D], F32)
        self.bv_row = at("bv_row", [1, c.D], F32)
        self.gb_s = at("gb_s", [c.E, 1], F32)
        self.gate_wT = at("gate_wT", [128, c.NDC * c.E], BF16)
        self.w_gates_tm = at("w_gates_tm", [128, c.NTC * c.E], F32)
        self.w_fm = at("w_fm", [c.E, c.S], BF16)
        self.negb = at("negb", [128, c.NDC], F32)
        self.scr = at("scr", [128, 64], F32)
        self.consts = at("consts", [128, 2], F32)
        self.scr1 = at("scr1", [1, 64], F32)
        self.recip64 = at("recip64", [64, 512], F32)
        self.sumrow = at("sumrow", [1, 512], F32)
        self.sumrow2 = at("sumrow2", [1, 512], F32)

        A0 = self._off = (self._off + 63) // 64 * 64
        self.xn = A("xn", [128, c.NDC * c.S], BF16, offset=A0)
        o = A0
        def atA(name, shape, dtype):
            nonlocal o
            o = (o + 63) // 64 * 64
            t = A(name, shape, dtype, offset=o)
            o += int(np.prod(shape[1:])) * mybir.dt.size(dtype)
            return t
        self.xga = [atA(f"xga{i}", [128, c.NDC * c.TT], BF16) for i in range(2)]
        self.w1buf = [atA(f"w1buf{i}", [128, c.NDC, 4, 128], BF16) for i in range(2)]
        self.h_rd = [atA(f"h_rd{i}", [128, c.TT], BF16) for i in range(4)]
        self.h_wr = [atA(f"h_wr{i}", [128, c.TT], BF16) for i in range(2)]
        self.wsl = [atA(f"wsl{i}", [128, (c.TT // 128) * 2 * c.E], BF16) for i in range(2)]
        self._off = max(o, A0 + c.NDC * c.S * 2)

        self.w2buf = [at(f"w2buf{i}", [128, 4, c.DHW], BF16) for i in range(2)]
        self.y_stage = [at(f"y_stage{i}", [64, 512], BF16) for i in range(2)]
        self.x_rd = [at(f"x_rd{i}", [128, c.D], F32) for i in range(2)]
        self.attn_rd = [at(f"attn_rd{i}", [128, c.D], F32) for i in range(2)]
        self.y_tc = [at(f"y_tc{i}", [128, c.NDC * 128], BF16) for i in range(2)]
        self.out_stage = [at(f"out_stage{i}", [128, 512], F32) for i in range(4)]
        self.xrb = [at(f"xrb{i}", [128, c.RW], BF16) for i in range(2)]
        self.idxs_i = at("idxs_i", [128, c.NTC * c.E], I32)
        self.s12_i = at("s12_i", [128, 2 * c.NTC], I32)
        self.sc_a = at("sc_a", [128, c.NTC * c.E], F32)
        self.sc_b = at("sc_b", [128, c.NTC * c.E], F32)
        self.sc_c = at("sc_c", [128, c.NTC * c.E], F32)
        self.sc_d = at("sc_d", [128, c.NTC * c.E], F32)
        self.sc_e = at("sc_e", [128, c.NTC * c.E], F32)
        self.sc_f = at("sc_f", [128, c.NTC * c.E], F32)
        self.sc_sm = at("sc_sm", [128, 256], F32)
        self.pp_a = at("pp_a", [8, 256], F32)
        self.pp_b = at("pp_b", [8, 256], F32)
        self.ecap_s = at("ecap_s", [128, c.NTC * c.E], F32)
        self.whi_bf = at("whi_bf", [128, c.NTC * c.E], BF16)
        self.wlo_bf = at("wlo_bf", [128, c.NTC * c.E], BF16)
        self.w2x = at("w2x", [128, c.NTC * 2 * c.E], BF16)
        self.wslf = [at(f"wslf{i}", [128, c.NTCT], F32) for i in range(2)]

        C0 = self._off = (self._off + 63) // 64 * 64
        self.x_tm = [A(f"x_tm{i}", [128, c.D], F32, offset=C0 + i * c.D * 4) for i in range(2)]
        o = C0 + 2 * c.D * 4
        self.xn_tm = [A(f"xn_tm{i}", [128, c.D], BF16, offset=o + i * c.D * 2) for i in range(2)]
        o += 2 * c.D * 2
        self.sq = [A(f"sq{i}", [128, c.D], F32, offset=o + i * c.D * 4) for i in range(2)]
        o += 2 * c.D * 4
        self.b1_stage = A("b1_stage", [c.E, c.HF], F32, offset=o); o += c.HF * 4
        self.gw_stage = A("gw_stage", [c.E, c.D], F32, offset=o); o += c.D * 4
        self.bqk_stage = A("bqk_stage", [8, 256], F32, offset=o); o += 1024
        p0_end = o
        self.gates_fm = A("gates_fm", [c.E, c.S], F32, offset=C0)
        o = C0 + c.S * 4
        self.g_tm = A("g_tm", [128, c.NTC * c.E], F32, offset=o); o += c.NTC * c.E * 4
        self.tk = []
        for i in range(4):
            self.tk.append(A(f"tk{i}", [128, c.NTC * c.E], F32, offset=o)); o += c.NTC * c.E * 4
        self.tks = A("tks", [128, 4 * c.NTC], F32, offset=o); o += 4 * c.NTC * 4
        p1_end = o
        self.Q_g = A("Q_g", [128, c.S], BF16, offset=C0)
        self.K_g = A("K_g", [128, c.S], BF16, offset=C0 + c.S * 2)
        o = C0 + 2 * c.S * 2
        vsz = ((c.S // 128) * 65 * 2 + 63) // 64 * 64
        self.v_tm = [A(f"v_tm{i}", [128, (c.S // 128) * 65], BF16, offset=o + i * vsz) for i in range(2)]
        o += 2 * vsz
        self.att_sb = [A(f"att_sb{i}", [128, 1024], BF16, offset=o + i * 2048) for i in range(4)]
        o += 8192
        self.wqT_g = A("wqT_g", [128, c.D], BF16, offset=o); o += c.D * 2
        self.wkT_g = A("wkT_g", [128, c.D], BF16, offset=o); o += c.D * 2
        self.wvT_g = A("wvT_g", [128, c.D], BF16, offset=o); o += c.D * 2
        self.w_stage = [A(f"w_stage{i}", [128, c.D], BF16, offset=o + i * c.D * 2) for i in range(2)]
        o += 2 * c.D * 2
        p2_end = o
        self.wpT = A("wpT", [128, c.NDC * c.D], BF16, offset=C0)
        o = C0 + c.NDC * c.D * 2
        self.attn_stage = [A(f"attn_stage{i}", [128, c.D], F32, offset=o + i * c.D * 4) for i in range(2)]
        o += 2 * c.D * 4
        p3_end = o
        self.out_acc = A("out_acc", [128, c.NTCT * c.D], F32, offset=C0)
        p4_end = C0 + c.NTCT * c.D * 4
        self.sbuf_top = max(p0_end, p1_end, p2_end, p3_end, p4_end)
        assert self.sbuf_top <= ARENA, f"SBUF over budget: {self.sbuf_top} > {ARENA}"

        self.Q = [nc.alloc_psum_tensor(f"psq{i}", [128, 1024], F32) for i in range(4)]

    def op(self, eng, emit, waits=(), incs=(), tag=""):
        return self.p.op(eng, emit, waits, incs, tag)

    def one(self, eng, emit, waits=(), tag=""):
        return self.op(eng, emit, waits, [(eng, 1)], tag)[0]

    def dma(self, eng, sem, emit, waits=(), tag=""):
        return self.op(eng, emit, waits, [(sem, 16)], tag)[0]

    def barrier(self, tag="bar", exclude=()):
        snap = dict(self.p.cnt)
        self._nbar = getattr(self, "_nbar", 0) + 1
        bs = f"bar{self._nbar}"
        self.op("pool", lambda eng: eng.drain(), (), [(bs, 1)], tag=tag + "_a")
        waits = [(bs, 1)]
        for e in ENGS:
            if e != "pool" and self.p.q[e] and not self.p.q[e][-1][2]:
                self.p.inc_last(e, e)
            if self.p.last_event[e] is not None:
                waits.append(self.p.last_event[e])
        waits += [(s, v) for s, v in snap.items()
                  if s not in ENGS and not s.startswith("bar") and v > 0
                  and s not in exclude]
        wm = {}
        for s, v in waits:
            wm[s] = max(wm.get(s, 0), v)
        waits = sorted(wm.items())
        for e in ENGS:
            self.op(e, lambda eng: eng.nop(), waits, (), tag=tag + "_b")

    def build(self):
        self.p0_consts()
        self.p0_ln()
        self.barrier("after_ln")
        self.p1_gates()
        self.barrier("after_gates", exclude=("scp0", "scp1", "xb0", "xb1"))
        self.make_hgen()
        self.mk_relay()
        self.p2_attn()
        self.relay_drip(10 ** 9)
        self.barrier("after_attn")
        self.p3_proj()
        self.barrier("after_proj")
        self.p4_ffn()
        self.op("sp", lambda eng: eng.nop(),
                [(s, v) for s, v in self.p.cnt.items() if s not in ENGS and v > 0],
                [("sp", 1)], tag="final_wait")
        self.p.check()
        self.emit()
        return self.nc

    def p0_consts(self):
        c = self.cfg
        for dst, srcd in ((self.wqb_d, self.wq_d), (self.wkb_d, self.wk_d),
                          (self.wvb_d, self.wv_d), (self.wpb_d, self.wp_d)):
            self.dma("pool", "rlw", (lambda dst, srcd: (lambda eng: eng.dma_start(
                dst[:], srcd[:])))(dst, srcd), [], tag="rlw")
        loads = [
            (self.id_bf[:], self.id_bf_d[:]),
            (self.id_f[:], self.id_f_d[:]),
            (self.ones_f[:], self.ones_f_d[:]),
            (self.masks[:], self.mask_d[:]),
            (self.bp_row[:], self.bp_d[:].unsqueeze(0)),
            (self.bv_row[:], self.bv_d[:].unsqueeze(0)),
            (self.gb_s[:], self.gb_d[:].unsqueeze(1)),
            (self.b1_stage[:], self.b1_d[:]),
            (self.gw_stage[:], self.gw_d[:]),
            (self.bqk_stage[0:c.NDC, 0:128], self.bq_d[:].rearrange("(a b) -> a b", b=128)),
            (self.bqk_stage[0:c.NDC, 128:256], self.bk_d[:].rearrange("(a b) -> a b", b=128)),
            (self.ecap_s[:], self.ecap_d[:]),
        ]
        for dst, src in loads:
            self.dma("sp", "cst", (lambda d, s: (lambda eng: eng.dma_start(d, s)))(dst, src), tag="cst")
        self.op("pool", lambda eng: eng.memset(self.consts[:, 0:1], 1e-5), (), [("ms0", 1)], tag="c_eps")
        self.op("pool", lambda eng: eng.memset(self.consts[:, 1:2], 1.0), (), [("ms1", 1)], tag="c_one")
        self.ev_consts = ("ms0", 1)
        self.dma("pool", "cstp", lambda eng: eng.dma_start(self.b2_bf[:], self.b2_d[:]), tag="b2")
        cst_all = ("cst", self.p.cnt["cst"])

        tp = self.Q[2]
        for j, colbase in ((0, 0), (1, c.NDC)):
            evt = self.op("pe", (lambda j: (lambda eng: eng.transpose(
                self.Q[2][0:128, 0:c.NDC], self.bqk_stage[0:c.NDC, 128 * j:128 * (j + 1)],
                self.id_f[0:c.NDC, 0:c.NDC])))(j),
                [cst_all, ("act", self.p.cnt["act"])], [("pe", 1)], tag=f"bqk_tr{j}")[0]
            self.one("act", (lambda cb: (lambda eng: eng.copy(
                self.bqkT[:, cb:cb + c.NDC], self.Q[2][0:128, 0:c.NDC])))(colbase), [evt], tag=f"bqk_ev{j}")
        for hc in range(c.HC):
            evt = self.op("pe", (lambda hc: (lambda eng: eng.transpose(
                self.Q[2][0:128, 8:8 + c.E], self.b1_stage[0:c.E, hc * 128:(hc + 1) * 128],
                self.id_f[0:c.E, 0:c.E])))(hc),
                [cst_all, ("act", self.p.cnt["act"])], [("pe", 1)], tag=f"b1_tr{hc}")[0]
            self.one("act", (lambda hc: (lambda eng: eng.copy(
                self.b1T[:, hc * c.E:(hc + 1) * c.E], self.Q[2][0:128, 8:8 + c.E])))(hc), [evt], tag=f"b1_ev{hc}")
        for dc in range(c.NDC):
            evt = self.op("pe", (lambda dc: (lambda eng: eng.transpose(
                self.Q[2][0:128, 16:16 + c.E], self.gw_stage[0:c.E, dc * 128:(dc + 1) * 128],
                self.id_f[0:c.E, 0:c.E])))(dc),
                [cst_all, ("act", self.p.cnt["act"])], [("pe", 1)], tag=f"gw_tr{dc}")[0]
            self.one("act", (lambda dc: (lambda eng: eng.copy(
                self.gate_wT[:, dc * c.E:(dc + 1) * c.E], self.Q[2][0:128, 16:16 + c.E])))(dc),
                [evt], tag=f"gw_ev{dc}")

    def p0_ln(self):
        c = self.cfg
        NP = 1.0 / c.D
        ev_dve = [None, None]
        ev_sq = [None, None]
        ev_evac = [None, None]
        ev_var = [None, None]
        ev_trans = [None, None]
        ev_xrw = [None, None]
        for tc in range(c.NTC):
            par = tc % 2
            waits = [ev_dve[par], ev_sq[par]]
            evx = self.dma("sp" if par == 0 else "pool", f"xl{par}",
                (lambda tc, par: (lambda eng: eng.dma_start(
                    self.x_tm[par][:], self.x_d[tc * 128:(tc + 1) * 128, :])))(tc, par),
                waits, tag=f"ln_x{tc}")
            o = par * 8
            m_sl, nm_sl, v_sl, sd_sl, rs_sl = (self.scr[:, o + i:o + i + 1] for i in range(5))
            e1 = self.one("dve", (lambda par, m_sl: (lambda eng: eng.reduce_sum(
                m_sl, self.x_tm[par][:], axis=AX.X)))(par, m_sl), [evx], tag=f"ln_mean{tc}")
            e2 = self.one("act", (lambda m_sl, nm_sl: (lambda eng: eng.mul(nm_sl, m_sl, -NP)))(m_sl, nm_sl),
                          [e1], tag=f"ln_negm{tc}")
            e3 = self.one("act", (lambda par, nm_sl: (lambda eng: eng.activation(
                self.sq[par][:], self.x_tm[par][:], AF.Square, bias=nm_sl)))(par, nm_sl),
                [e2, ev_var[par]], tag=f"ln_sq{tc}")
            ev_sq[par] = e3
            e4 = self.one("dve", (lambda par, v_sl: (lambda eng: eng.reduce_sum(
                v_sl, self.sq[par][:], axis=AX.X)))(par, v_sl), [e3], tag=f"ln_var{tc}")
            ev_var[par] = e4
            e5 = self.one("act", (lambda v_sl, sd_sl: (lambda eng: eng.activation(
                sd_sl, v_sl, AF.Sqrt, bias=self.consts[:, 0:1], scale=NP)))(v_sl, sd_sl),
                [e4, self.ev_consts if tc == 0 else None], tag=f"ln_std{tc}")
            e6 = self.one("dve", (lambda sd_sl, rs_sl: (lambda eng: eng.reciprocal(rs_sl, sd_sl)))(sd_sl, rs_sl),
                          [e5], tag=f"ln_rstd{tc}")
            e7 = self.one("dve", (lambda par, nm_sl, rs_sl: (lambda eng: eng.tensor_scalar(
                self.xn_tm[par][:], self.x_tm[par][:], nm_sl, rs_sl, ALU.add, ALU.mult)))(par, nm_sl, rs_sl),
                [e6, ev_trans[par], ev_xrw[par]], tag=f"ln_xn{tc}")
            ev_dve[par] = e7
            ev_xrw[par] = self.dma("sp", f"xw{par}", (lambda tc, par: (lambda eng: eng.dma_start(
                self.xrow_d[tc * 128:(tc + 1) * 128, 0:c.D], self.xn_tm[par][:])))(tc, par),
                [e7], tag=f"xrw{tc}")
            pv = self.Q[par][:].bitcast(BF16)
            for dc in range(c.NDC):
                self.op("pe", (lambda par, dc, pv: (lambda eng: eng.transpose(
                    pv[:, dc * 128:(dc + 1) * 128],
                    self.xn_tm[par][:, dc * 128:(dc + 1) * 128], self.id_bf[:])))(par, dc, pv),
                    [e7, ev_evac[par]] if dc == 0 else (),
                    [("pe", 1)] if dc == c.NDC - 1 else (), tag=f"ln_tr{tc}_{dc}")
            evt = ("pe", self.p.cnt["pe"])
            ev_trans[par] = evt
            dst = self.xn[:].rearrange("p (dc s) -> p dc s", dc=c.NDC)[:, :, tc * 128:(tc + 1) * 128]
            src = pv[:, 0:c.NDC * 128].rearrange("p (dc t) -> p dc t", dc=c.NDC)
            ev_evac[par] = self.one("act", (lambda dst, src: (lambda eng: eng.copy(dst, src)))(dst, src),
                                    [evt], tag=f"ln_ev{tc}")

    def p1_gates(self):
        c = self.cfg
        ns = c.S // 512
        ev_ev = [None, None]
        for s in range(ns):
            par = s % 2
            gp = self.Q[1][0:c.E, par * 512:(par + 1) * 512]
            for dc in range(c.NDC):
                self.op("pe", (lambda s, dc, gp: (lambda eng: eng.matmul(
                    gp, self.gate_wT[:, dc * c.E:(dc + 1) * c.E],
                    self.xn[:, dc * c.S + s * 512: dc * c.S + (s + 1) * 512],
                    start=(dc == 0), stop=(dc == c.NDC - 1))))(s, dc, gp),
                    [ev_ev[par]] if dc == 0 else (),
                    [("pe", 1)] if dc == c.NDC - 1 else (), tag=f"g_mm{s}_{dc}")
            evt = ("pe", self.p.cnt["pe"])
            ev_ev[par] = self.one("act", (lambda s, gp: (lambda eng: eng.activation(
                self.gates_fm[:, s * 512:(s + 1) * 512], gp, AF.Identity, bias=self.gb_s[:])))(s, gp),
                [evt], tag=f"g_ev{s}")
        ga0, ga1 = ev_ev[0], ev_ev[1]
        ev_tre = [None, None]
        for tc in range(c.NTC):
            par = tc % 2
            sl = 32 + par * 512
            evt = self.op("pe", (lambda tc, sl: (lambda eng: eng.transpose(
                self.Q[2][0:128, sl:sl + c.E], self.gates_fm[0:c.E, tc * 128:(tc + 1) * 128],
                self.id_f[0:c.E, 0:c.E])))(tc, sl),
                [ga0, ga1, ev_tre[par]], [("pe", 1)], tag=f"g_tr{tc}")[0]
            ev_tre[par] = self.one("act", (lambda tc, sl: (lambda eng: eng.copy(
                self.g_tm[:, tc * c.E:(tc + 1) * c.E], self.Q[2][0:128, sl:sl + c.E])))(tc, sl),
                [evt], tag=f"g_tre{tc}")
        gtm = [ev_tre[0], ev_tre[1]]
        NE, NT = c.E, c.NTC
        g3 = self.g_tm[:].rearrange("p (t e) -> p t e", e=NE)
        mm1 = self.tks[:, 0:NT]
        mm2 = self.tks[:, NT:2 * NT]
        em2 = self.tks[:, 2 * NT:3 * NT]
        rden = self.tks[:, 3 * NT:4 * NT]
        d_, eq_, msk_, wun_ = (t[:] for t in self.tk[0:4])
        d3 = d_.rearrange("p (t e) -> p t e", e=NE)
        e1 = self.one("dve", lambda eng: eng.reduce_max(mm1, g3, axis=AX.X), gtm, tag="tk_m1")
        e2 = self.one("dve", lambda eng: eng.tensor_tensor(
            d3, g3, mm1.unsqueeze(2).broadcast_to([128, NT, NE]), ALU.subtract), [e1], tag="tk_d")
        e3 = self.one("dve", lambda eng: eng.tensor_single_scalar(eq_, d_, 0.0, ALU.is_equal), [e2], tag="tk_eq")
        e4 = self.one("dve", lambda eng: eng.scalar_tensor_tensor(
            msk_, eq_, -1e30, d_, ALU.mult, ALU.add), [e3], tag="tk_msk")
        e5 = self.one("dve", lambda eng: eng.reduce_max(
            mm2, msk_.rearrange("p (t e) -> p t e", e=NE), axis=AX.X), [e4], tag="tk_m2")
        e6 = self.one("act", lambda eng: eng.activation(eq_, d_, AF.Exp), [e2, e4], tag="tk_expd")
        expd = eq_
        e7 = self.one("dve", lambda eng: eng.tensor_tensor(
            msk_.rearrange("p (t e) -> p t e", e=NE), d3,
            mm2.unsqueeze(2).broadcast_to([128, NT, NE]), ALU.is_ge), [e5], tag="tk_ge")
        ge_ = msk_
        e8 = self.one("dve", lambda eng: eng.tensor_tensor(wun_, expd, ge_, ALU.mult), [e6, e7], tag="tk_wun")
        e9 = self.one("act", lambda eng: eng.activation(em2, mm2, AF.Exp), [e5], tag="tk_em2")
        e10 = self.one("act", lambda eng: eng.add(em2, em2, 1.0), [e9], tag="tk_den")
        e11 = self.one("dve", lambda eng: eng.reciprocal(rden, em2), [e10], tag="tk_rden")
        e12 = self.one("dve", lambda eng: eng.tensor_tensor(
            self.w_gates_tm[:].rearrange("p (t e) -> p t e", e=NE),
            wun_.rearrange("p (t e) -> p t e", e=NE),
            rden.unsqueeze(2).broadcast_to([128, NT, NE]), ALU.mult), [e8, e11], tag="tk_w")
        ev_tre = [None, None]
        for tc in range(c.NTC):
            par = tc % 2
            sl = 48 + par * 512
            evt = self.op("pe", (lambda tc, sl: (lambda eng: eng.transpose(
                self.Q[2][0:c.E, sl:sl + 128], self.w_gates_tm[:, tc * c.E:(tc + 1) * c.E],
                self.id_f[:])))(tc, sl), [e12, ev_tre[par]], [("pe", 1)], tag=f"w_tr{tc}")[0]
            ev_tre[par] = self.one("act", (lambda tc, sl: (lambda eng: eng.copy(
                self.w_fm[0:c.E, tc * 128:(tc + 1) * 128], self.Q[2][0:c.E, sl:sl + 128])))(tc, sl),
                [evt], tag=f"w_tre{tc}")
        eh1 = self.one("dve", lambda eng: eng.tensor_copy(self.whi_bf[:], self.w_gates_tm[:]),
                       [e12], tag="wsp1")
        eh2 = self.one("dve", lambda eng: eng.tensor_copy(self.sc_a[:], self.whi_bf[:]),
                       [eh1], tag="wsp2")
        eh3 = self.one("dve", lambda eng: eng.tensor_sub(self.sc_b[:], self.w_gates_tm[:], self.sc_a[:]),
                       [eh2], tag="wsp3")
        eh4 = self.one("dve", lambda eng: eng.tensor_copy(self.wlo_bf[:], self.sc_b[:]),
                       [eh3], tag="wsp4")
        v2 = self.w2x[:].rearrange("p (t e two) -> p t e two", e=c.E, two=2)
        eh5 = self.one("dve", lambda eng: eng.tensor_copy(
            v2[:, :, :, 0], self.whi_bf[:].rearrange("p (t e) -> p t e", e=c.E)), [eh4], tag="wsp5")
        eh6 = self.one("dve", lambda eng: eng.tensor_copy(
            v2[:, :, :, 1], self.wlo_bf[:].rearrange("p (t e) -> p t e", e=c.E)), [eh5], tag="wsp6")
        for tc in range(c.NTC):
            self.dma("sp", "ww", (lambda tc: (lambda eng: eng.dma_start(
                self.xrow_d[tc * 128:(tc + 1) * 128, c.D:c.D + 2 * c.E],
                self.w2x[:, tc * 2 * c.E:(tc + 1) * 2 * c.E])))(tc),
                [eh6], tag=f"wwr{tc}")
        self.p15_route(eh6)


    def mk_relay(self):
        c = self.cfg
        self.n_relay = c.E * c.NP1 + c.E * c.NDH * c.NP1

        def gen():
            for e in range(c.E):
                for hp in range(c.NP1):
                    yield ("w1", e, hp, 0)
                for dh in range(c.NDH):
                    for hp in range(c.NP1):
                        yield ("w2", e, hp, dh)
        self._relay_it = gen()

    def relay_drip(self, n):
        c = self.cfg
        for _ in range(n):
            try:
                kind, e, hp, dh = next(self._relay_it)
            except StopIteration:
                return
            if kind == "w1":
                self.dma("pool", "rl", (lambda e, hp: (lambda eng: eng.dma_start(
                    self.w1b_d[e, hp].rearrange("p (dc hcl f) -> p dc hcl f", dc=c.NDC, f=128),
                    self.w1_d[e, :, hp * 512:(hp + 1) * 512].rearrange(
                        "(dc p) (hcl f) -> p dc hcl f", p=128, f=128))))(e, hp),
                    [], tag=f"rl1_{e}_{hp}")
            else:
                self.dma("pool", "rl", (lambda e, hp, dh: (lambda eng: eng.dma_start(
                    self.w2b_d[e, dh, hp].rearrange("p (hcl f) -> p hcl f", f=c.DHW),
                    self.w2_d[e, hp * 512:(hp + 1) * 512,
                              dh * c.DHW:(dh + 1) * c.DHW].rearrange(
                        "(hcl p) f -> p hcl f", p=128))))(e, hp, dh),
                    [], tag=f"rl2_{e}_{dh}_{hp}")

    def p15_route(self, e12):
        c = self.cfg
        NE, NT = c.E, c.NTC
        BIG = 60000.0
        A3 = lambda t: t[:].rearrange("p (t e) -> p t e", e=NE)
        ev = e12

        def dve(emit, extra=(), tag=""):
            nonlocal ev
            ev = self.one("dve", emit, [ev] + list(extra), tag=tag)
            return ev
        dve(lambda eng: eng.tensor_single_scalar(self.sc_d[:], self.w_gates_tm[:], 0.0, ALU.is_gt), tag="rt_mk")
        dve(lambda eng: eng.tensor_copy(self.sc_a[:], self.sc_d[:]), tag="rt_cp")
        cur, nxt = self.sc_a, self.sc_b
        k = 1
        while k < NT:
            dve((lambda cur, nxt, k: (lambda eng: eng.tensor_copy(
                A3(nxt)[:, 0:k, :], A3(cur)[:, 0:k, :])))(cur, nxt, k), tag=f"rt_s{k}a")
            dve((lambda cur, nxt, k: (lambda eng: eng.tensor_tensor(
                A3(nxt)[:, k:NT, :], A3(cur)[:, k:NT, :], A3(cur)[:, 0:NT - k, :],
                ALU.add)))(cur, nxt, k), tag=f"rt_s{k}b")
            cur, nxt = nxt, cur
            k *= 2
        incl = cur
        dve((lambda incl: (lambda eng: eng.tensor_sub(self.sc_c[:], incl[:], self.sc_d[:])))(incl), tag="rt_ex")
        ept = self.op("pe", (lambda incl: (lambda eng: eng.transpose(
            self.Q[2][0:NE, 256:384], incl[:, (NT - 1) * NE:NT * NE], self.id_f[:])))(incl),
            [ev, ("act", self.p.cnt["act"])], [("pe", 1)], tag="rt_tr1")[0]
        ea = self.one("act", lambda eng: eng.copy(self.pp_a[0:NE, 0:128], self.Q[2][0:NE, 256:384]),
                      [ept], tag="rt_ev1")
        ev = ea
        cur, nxt = self.pp_a, self.pp_b
        k = 1
        while k < 128:
            dve((lambda cur, nxt, k: (lambda eng: eng.tensor_copy(
                nxt[0:NE, 0:k], cur[0:NE, 0:k])))(cur, nxt, k), tag=f"rt_p{k}a")
            dve((lambda cur, nxt, k: (lambda eng: eng.tensor_tensor(
                nxt[0:NE, k:128], cur[0:NE, k:128], cur[0:NE, 0:128 - k],
                ALU.add)))(cur, nxt, k), tag=f"rt_p{k}b")
            cur, nxt = nxt, cur
            k *= 2
        inclp, other = cur, nxt
        evm = self.one("pool", (lambda other: (lambda eng: eng.memset(other[0:NE, 0:1], 0.0)))(other),
                       [ev], tag="rt_ms")
        dve((lambda inclp, other: (lambda eng: eng.tensor_copy(
            other[0:NE, 1:128], inclp[0:NE, 0:127])))(inclp, other), [evm], tag="rt_sh")
        ep2 = self.op("pe", (lambda other: (lambda eng: eng.transpose(
            self.Q[2][0:128, 384:384 + NE], other[0:NE, 0:128], self.id_f[0:NE, 0:NE])))(other),
            [ev, ("act", self.p.cnt["act"])], [("pe", 1)], tag="rt_tr2")[0]
        ev = self.one("act", lambda eng: eng.copy(self.sc_sm[:, 0:NE], self.Q[2][0:128, 384:384 + NE]),
                      [ep2], tag="rt_ev2")
        dve(lambda eng: eng.tensor_tensor(
            A3(self.sc_b), A3(self.sc_c),
            self.sc_sm[:, 0:NE].unsqueeze(1).broadcast_to([128, NT, NE]), ALU.add), tag="rt_pt")
        dve(lambda eng: eng.tensor_single_scalar(self.sc_c[:], self.sc_b[:], float(c.CAP), ALU.is_lt), tag="rt_ft")
        dve(lambda eng: eng.tensor_tensor(self.sc_c[:], self.sc_c[:], self.sc_d[:], ALU.mult), tag="rt_vl")
        dve(lambda eng: eng.tensor_tensor(self.sc_b[:], self.sc_b[:], self.ecap_s[:], ALU.add), tag="rt_sg")
        dve(lambda eng: eng.reduce_max(self.sc_sm[:, 32:32 + NT], A3(self.w_gates_tm), axis=AX.X), tag="rt_rm")
        dve(lambda eng: eng.tensor_tensor(
            A3(self.sc_e), A3(self.w_gates_tm),
            self.sc_sm[:, 32:32 + NT].unsqueeze(2).broadcast_to([128, NT, NE]), ALU.is_ge), tag="rt_s1")
        dve(lambda eng: eng.tensor_sub(self.sc_f[:], self.sc_d[:], self.sc_e[:]), tag="rt_s2")
        for k2, sel in ((0, self.sc_e), (1, self.sc_f)):
            dve((lambda sel: (lambda eng: eng.tensor_tensor(sel[:], sel[:], self.sc_c[:], ALU.mult)))(sel),
                tag=f"rt_a{k2}")
            dve((lambda sel: (lambda eng: eng.tensor_tensor(self.sc_a[:], sel[:], self.sc_b[:], ALU.mult)))(sel),
                tag=f"rt_q{k2}")
            dve(lambda eng: eng.reduce_sum(self.sc_sm[:, 64:64 + NT], A3(self.sc_a), axis=AX.X), tag=f"rt_sa{k2}")
            dve((lambda sel: (lambda eng: eng.reduce_sum(
                self.sc_sm[:, 96:96 + NT], A3(sel), axis=AX.X)))(sel), tag=f"rt_n{k2}")
            dve(lambda eng: eng.tensor_scalar(
                self.sc_sm[:, 128:128 + NT], self.sc_sm[:, 96:96 + NT], -1.0, -BIG,
                ALU.add, ALU.mult), tag=f"rt_u{k2}")
            dve(lambda eng: eng.tensor_tensor(
                self.sc_sm[:, 64:64 + NT], self.sc_sm[:, 64:64 + NT],
                self.sc_sm[:, 128:128 + NT], ALU.add), tag=f"rt_ss{k2}")
            dve((lambda k2: (lambda eng: eng.tensor_copy(
                self.s12_i[:, k2 * NT:(k2 + 1) * NT], self.sc_sm[:, 64:64 + NT])))(k2), tag=f"rt_ci{k2}")
        dve(lambda eng: eng.tensor_scalar(self.sc_b[:], self.sc_b[:], -BIG, None, ALU.add), tag="rt_i1")
        dve(lambda eng: eng.tensor_tensor(self.sc_b[:], self.sc_b[:], self.sc_c[:], ALU.mult), tag="rt_i2")
        dve(lambda eng: eng.tensor_scalar(self.sc_b[:], self.sc_b[:], BIG, None, ALU.add), tag="rt_i3")
        dve(lambda eng: eng.tensor_copy(self.idxs_i[:], self.sc_b[:]), tag="rt_i4")
        ev_idx = ev
        def mk_reg(eng):
            self._bc_reg = eng.to_reg(c.E * c.CAP - 1)
            return eng.nop()
        self.op("pool", mk_reg, (), [("ms8", 1)], tag="rt_reg")
        wr_all = [(s, self.p.cnt[s]) for s in ("xw0", "xw1", "ww") if s in self.p.cnt]
        ev_rnd = [None, None]
        for tc in range(c.NTC):
            par = tc % 2
            exb = self.dma("sp", f"xb{par}", (lambda tc, par: (lambda eng: eng.dma_start(
                self.xrb[par][:], self.xrow_d[tc * 128:(tc + 1) * 128, :])))(tc, par),
                wr_all + [ev_rnd[par]], tag=f"xrl{tc}")
            for k2 in range(2):
                self.dma("pool", f"scp{par}", (lambda tc, par, k2: (lambda eng: eng.indirect_dma_start(
                    self.xg_d[:],
                    IndirectOffsetOnAxis(self.s12_i[:, k2 * NT + tc:k2 * NT + tc + 1], 0),
                    self.xrb[par][:], None,
                    bounds_check=self._bc_reg, oob_is_err=False)))(tc, par, k2),
                    [exb, ev_idx] if k2 == 0 else [], tag=f"sct{tc}_{k2}")
            ev_rnd[par] = (f"scp{par}", self.p.cnt[f"scp{par}"])

    def make_hgen(self):
        c = self.cfg
        nh = c.TT // 512
        self.ev_hwr_ev = {}
        self.h_pos = -1
        st = {
            "u": 0,
            "ev_gelu": [None, None],
            "ev_hwr_free": [None, None],
            "ev_w1_free": [None, None],
            "piece_loaded": {},
            "chunk_loaded": {},
            "ev_xga_free": [None, None],
            "q3_guard": [],
        }
        self.hst = st
        self.ev_wslf_ev = {}

        def load_piece(P):
            if P >= c.NSG * c.NP1 or P in st["piece_loaded"]:
                return
            sgi2, hp2 = divmod(P, c.NP1)
            e2 = sgi2 // c.NCH
            bi = P % 2
            ev = self.dma("pool", f"w1{bi}", (lambda e2, hp2, bi: (lambda eng: eng.dma_start(
                self.w1buf[bi][:],
                self.w1b_d[e2, hp2].rearrange(
                    "p (dc hcl f) -> p dc hcl f", dc=c.NDC, f=128))))(e2, hp2, bi),
                [st["ev_w1_free"][bi], ("rl", 16 * self.n_relay)], tag=f"w1d{sgi2}_{hp2}")
            st["piece_loaded"][P] = ev

        def load_chunk(sgi2):
            if sgi2 >= c.NSG or sgi2 in st["chunk_loaded"]:
                return
            bi = sgi2 % 2
            rb = sgi2 * c.TT
            sem = f"xg{bi}"
            for dc in range(c.NDC):
                self.dma("sp", sem, (lambda sgi2, dc, bi, rb: (lambda eng: eng.dma_start(
                    self.xga[bi][:, dc * c.TT:(dc + 1) * c.TT],
                    self.xg_d[rb:rb + c.TT, dc * 128:(dc + 1) * 128], transpose=True)))(sgi2, dc, bi, rb),
                    [st["ev_xga_free"][bi]] if dc == 0 else [], tag=f"xgl{sgi2}_{dc}")
            self.dma("sp", sem, (lambda sgi2, bi, rb: (lambda eng: eng.dma_start(
                self.wsl[bi][:].rearrange("p (a w) -> p a w", w=2 * c.E),
                self.xg_d[rb:rb + c.TT, c.D:c.D + 2 * c.E].rearrange(
                    "(a p) w -> p a w", p=128))))(sgi2, bi, rb),
                [self.ev_wslf_ev.get(sgi2 - 2)], tag=f"wsl{sgi2}")
            st["chunk_loaded"][sgi2] = (sem, self.p.cnt[sem])
        self.load_chunk = load_chunk

        def units():
            for sgi in range(c.NSG):
                for hp in range(c.NP1):
                    for hcl in range(4):
                        for half in range(nh):
                            yield sgi, hp, hcl, half
        it = units()

        def pull():
            try:
                sgi, hp, hcl, half = next(it)
            except StopIteration:
                return False
            e = sgi // c.NCH
            hc = hp * 4 + hcl
            P = sgi * c.NP1 + hp
            load_chunk(sgi)
            load_chunk(sgi + 1)
            load_piece(P)
            load_piece(P + 1)
            bi = P % 2
            cbi = sgi % 2
            u = st["u"]; st["u"] += 1
            qp = u % 2
            hQ = self.Q[3][:, qp * 512:qp * 512 + 512]
            for dc in range(c.NDC):
                wts = []
                if dc == 0:
                    wts = [st["piece_loaded"][P], st["chunk_loaded"][sgi],
                           st["ev_gelu"][qp]] + st["q3_guard"]
                    st["q3_guard"] = []
                self.op("pe", (lambda hcl, half, dc, bi, cbi, hQ: (lambda eng: eng.matmul(
                    hQ, self.w1buf[bi][:, dc, hcl, :],
                    self.xga[cbi][:, dc * c.TT + half * 512: dc * c.TT + (half + 1) * 512],
                    start=(dc == 0), stop=(dc == c.NDC - 1))))(hcl, half, dc, bi, cbi, hQ),
                    wts, [("pe", 1)] if dc == c.NDC - 1 else (),
                    tag=f"h_mm{sgi}_{hc}_{half}_{dc}")
            evh = ("pe", self.p.cnt["pe"])
            if hcl == 3 and half == nh - 1:
                st["ev_w1_free"][bi] = evh
            if hp == c.NP1 - 1 and hcl == 3 and half == nh - 1:
                st["ev_xga_free"][cbi] = evh
            wbi = hc % 2
            ege = self.one("act", (lambda e, hc, half, wbi, hQ: (lambda eng: eng.activation(
                self.h_wr[wbi][:, half * 512:(half + 1) * 512], hQ, AF.Gelu,
                bias=self.b1T[:, hc * c.E + e:hc * c.E + e + 1])))(e, hc, half, wbi, hQ),
                [evh, st["ev_hwr_free"][wbi] if half == 0 else None], tag=f"gelu{sgi}_{hc}_{half}")
            st["ev_gelu"][qp] = ege
            if half == nh - 1:
                self.dma("act", f"hw{wbi}", (lambda sgi, hc, wbi: (lambda eng: eng.dma_start(
                    self.h_d[sgi, hc * 128:(hc + 1) * 128, :], self.h_wr[wbi][:, 0:c.TT])))(sgi, hc, wbi),
                    [ege], tag=f"hwr{sgi}_{hc}")
                st["ev_hwr_free"][wbi] = (f"hw{wbi}", self.p.cnt[f"hw{wbi}"])
                self.ev_hwr_ev[(sgi, hc)] = st["ev_hwr_free"][wbi]
                if hc == c.HC - 1:
                    self.h_pos = sgi
            return True
        self.pull_h = pull

    def pull_h_n(self, n):
        for _ in range(n):
            if not self.pull_h():
                return

    def pull_h_until(self, gi):
        while self.h_pos < gi:
            if not self.pull_h():
                raise RuntimeError("h generator exhausted early")

    def p2_attn(self):
        c = self.cfg
        self.op("pool", lambda eng: eng.memset(self.v_tm[0][:], 1.0), (), [("ms2", 1)], tag="vt_ms0")
        self.op("pool", lambda eng: eng.memset(self.v_tm[1][:], 1.0), (), [("ms3", 1)], tag="vt_ms1")
        ev_vtm_war = [("ms2", 1), ("ms3", 1)]
        ev_wtr_ev = [None, None]
        ev_stage_free = [None, None]
        ev_exp_q = [None, None]
        ev_attsb = [None, None, None, None]
        ev_y_free = None
        ev_qk_last_ev = None
        ev_ystage = [None, None]
        wd = {0: self.wqb_d, 1: self.wkb_d, 2: self.wvb_d}
        for g in range(c.NDC):
            wT = {0: self.wqT_g, 1: self.wkT_g, 2: self.wvT_g}
            ev_wT = {}
            for j in range(3):
                par = j % 2
                evs = self.dma("act", f"ws{par}", (lambda g, j, par: (lambda eng: eng.dma_start(
                    self.w_stage[par][:], wd[j][g * 128:(g + 1) * 128, :])))(g, j, par),
                    [ev_stage_free[par], ("rlw", 64)], tag=f"wst{g}_{j}")
                pv = self.Q[2][:].bitcast(BF16)
                lastev = None
                for dc in range(c.NDC):
                    half = dc % 2
                    sl = half * 1024
                    wts = []
                    if dc == 0:
                        wts = [evs, ev_y_free]
                    wts.append(ev_wtr_ev[half])
                    self.op("pe", (lambda par, sl, dc, pv: (lambda eng: eng.transpose(
                        pv[:, sl:sl + 128], self.w_stage[par][:, dc * 128:(dc + 1) * 128],
                        self.id_bf[:])))(par, sl, dc, pv), wts, [("pe", 1)], tag=f"wtr{g}_{j}_{dc}")
                    evt = ("pe", self.p.cnt["pe"])
                    lastev = self.one("act", (lambda dc, sl, pv, wTj=wT[j]: (lambda eng: eng.copy(
                        wTj[:, dc * 128:(dc + 1) * 128], pv[:, sl:sl + 128])))(dc, sl, pv),
                        [evt], tag=f"wtre{g}_{j}_{dc}")
                    ev_wtr_ev[half] = lastev
                ev_wT[j] = lastev
                ev_stage_free[par] = lastev
            self.relay_drip(-(-self.n_relay // c.NDC))
            ev_vev = [None, None]
            for kc in range(c.S // 128):
                par = kc % 2
                vp = self.Q[par][:, 0:128]
                for dc in range(c.NDC):
                    wts = []
                    if dc == 0:
                        wts = [ev_wT[2], ev_vev[par], ev_y_free if kc == 0 else None]
                        if kc < 2:
                            wts.append(ev_exp_q[par])
                    self.op("pe", (lambda kc, dc, vp: (lambda eng: eng.matmul(
                        vp, self.xn[:, dc * c.S + kc * 128: dc * c.S + (kc + 1) * 128],
                        self.wvT_g[:, dc * 128:(dc + 1) * 128],
                        start=(dc == 0), stop=False)))(kc, dc, vp), wts, (), tag=f"v_mm{g}_{kc}_{dc}")
                self.op("pe", (lambda g, vp: (lambda eng: eng.matmul(
                    vp, self.ones_f[:], self.bv_row[:, g * 128:(g + 1) * 128],
                    start=False, stop=True)))(g, vp), (), [("pe", 1)], tag=f"v_bv{g}_{kc}")
                evt = ("pe", self.p.cnt["pe"])
                wv = [evt]
                if kc < 2:
                    wv += ev_vtm_war
                self.one("dve", (lambda kc, vp: (lambda eng: eng.tensor_copy(
                    self.v_tm[0][:, kc * 65:kc * 65 + 64], vp[:, 0:64])))(kc, vp), wv, tag=f"v_evA{g}_{kc}")
                ev_vev[par] = self.one("dve", (lambda kc, vp: (lambda eng: eng.tensor_copy(
                    self.v_tm[1][:, kc * 65:kc * 65 + 64], vp[:, 64:128])))(kc, vp), [evt], tag=f"v_evB{g}_{kc}")
            ev_vready = ev_vev[0] if (c.S // 128) % 2 == 1 else ev_vev[1]
            QK = {0: (self.wqT_g, self.Q_g, 0), 1: (self.wkT_g, self.K_g, c.NDC)}
            nsl = c.S // 512
            ev_qk_ev = ev_qk_last_ev
            for j in (0, 1):
                wTj, dstT, bcol = QK[j]
                for hb in range((nsl + 3) // 4):
                    sls = list(range(hb * 4, min(nsl, hb * 4 + 4)))
                    for dc in range(c.NDC):
                        for i, s in enumerate(sls):
                            pq = self.Q[i // 2][:, (i % 2) * 512:(i % 2) * 512 + 512]
                            wts = []
                            if dc == 0 and i == 0:
                                wts = [ev_wT[j], ev_qk_ev, ev_exp_q[0], ev_exp_q[1],
                                       ev_vev[0], ev_vev[1]]
                            self.op("pe", (lambda j, dc, s, pq, wTj: (lambda eng: eng.matmul(
                                pq, wTj[:, dc * 128:(dc + 1) * 128],
                                self.xn[:, dc * c.S + s * 512: dc * c.S + (s + 1) * 512],
                                start=(dc == 0), stop=(dc == c.NDC - 1))))(j, dc, s, pq, wTj),
                                wts, [("pe", 1)] if (dc == c.NDC - 1 and i == len(sls) - 1) else (),
                                tag=f"qk{g}_{j}_{hb}_{dc}_{s}")
                    evt = ("pe", self.p.cnt["pe"])
                    for i in range(0, len(sls), 2):
                        n2 = min(2, len(sls) - i)
                        pq = self.Q[i // 2][:, 0:512 * n2]
                        dst = dstT[:, sls[i] * 512: (sls[i] + n2) * 512]
                        ev_qk_ev = self.one("act", (lambda pq, dst, g, bcol: (lambda eng: eng.activation(
                            dst, pq, AF.Identity, bias=self.bqkT[:, bcol + g:bcol + g + 1])))(pq, dst, g, bcol),
                            [evt], tag=f"qk_ev{g}_{j}_{hb}_{i}")
            ev_qk = ev_qk_ev
            ev_qk_last_ev = ev_qk_ev
            qm = self.scr[:, 16:17]
            km = self.scr[:, 17:18]
            r1 = self.scr1[:, 0:1]
            r2 = self.scr1[:, 1:2]
            eq1 = self.one("dve", lambda eng: eng.tensor_reduce(
                qm, self.Q_g[:], AX.X, ALU.max, apply_absolute_value=True), [ev_qk], tag=f"b_q{g}")
            ek1 = self.one("dve", lambda eng: eng.tensor_reduce(
                km, self.K_g[:], AX.X, ALU.max, apply_absolute_value=True), [eq1], tag=f"b_k{g}")
            et1 = self.op("pe", lambda eng: eng.transpose(
                self.Q[2][0:1, 0:128], qm, self.id_f[:]),
                [eq1, ev_y_free], [("pe", 1)], tag=f"b_tq{g}")[0]
            er1 = self.one("dve", lambda eng: eng.reduce_max(r1, self.Q[2][0:1, 0:128], axis=AX.X),
                           [et1], tag=f"b_rq{g}")
            et2 = self.op("pe", lambda eng: eng.transpose(
                self.Q[2][0:1, 128:256], km, self.id_f[:]), [ek1, er1], [("pe", 1)], tag=f"b_tk{g}")[0]
            er2 = self.one("dve", lambda eng: eng.reduce_max(r2, self.Q[2][0:1, 128:256], axis=AX.X),
                           [et2], tag=f"b_rk{g}")
            es1 = self.one("act", lambda eng: eng.activation(r1, r1, AF.Square), [er1], tag=f"b_sq{g}")
            es2 = self.one("act", lambda eng: eng.activation(r2, r2, AF.Square), [er2], tag=f"b_sk{g}")
            ea = self.one("dve", lambda eng: eng.tensor_add(r1, r1, r2), [es1, es2], tag=f"b_add{g}")
            en = self.one("act", lambda eng: eng.mul(r1, r1, -1.0), [ea], tag=f"b_neg{g}")
            eb = self.op("pe", lambda eng: eng.matmul(
                self.Q[2][0:128, 256:257], self.ones_f[:], r1[:, 0:1], start=True, stop=True),
                [en], [("pe", 1)], tag=f"b_bc{g}")[0]
            ebb = self.one("act", (lambda g: (lambda eng: eng.copy(
                self.negb[:, g:g + 1], self.Q[2][0:128, 256:257])))(g), [eb], tag=f"b_cp{g}")
            last_mul = ev_y_free
            for qt in range(c.NQT):
                kcm = 4 * (qt + 1)
                pend = []

                def emit_av(kc, ap, ee, kcm=None):
                    avw = [ee]
                    if kc == 0:
                        avw += [ev_vready, last_mul]
                    self.op("pe", (lambda kc, ap, kcm: (lambda eng: eng.matmul(
                        self.Q[2][0:65, 0:512], self.v_tm[0][:, kc * 65:(kc + 1) * 65],
                        self.att_sb[ap][:, 0:512], start=(kc == 0), stop=(kc == kcm - 1))))(kc, ap, kcm),
                        avw, (), tag=f"avA{g}_{kc}")
                    self.op("pe", (lambda kc, ap, kcm: (lambda eng: eng.matmul(
                        self.Q[2][0:65, 512:1024], self.v_tm[1][:, kc * 65:(kc + 1) * 65],
                        self.att_sb[ap][:, 512:1024], start=(kc == 0), stop=(kc == kcm - 1))))(kc, ap, kcm),
                        (), [("pe", 1)], tag=f"avB{g}_{kc}")
                    ev_attsb[ap] = ("pe", self.p.cnt["pe"])

                for kc in range(kcm):
                    kp = kc % 2
                    ap = kc % 4
                    sp_ = self.Q[kp]
                    w0 = [ev_exp_q[kp]]
                    if qt == 0 and kc == 0:
                        w0 += [ev_qk, ebb]
                    if qt > 0 and kc == 0:
                        w0.append(ev_epi_q0)
                    self.op("pe", (lambda qt, kc, sp_: (lambda eng: eng.matmul(
                        sp_[:, 0:512], self.K_g[0:64, kc * 128:(kc + 1) * 128],
                        self.Q_g[0:64, qt * 512:(qt + 1) * 512], start=True, stop=True,
                        tile_position=(0, 0))))(qt, kc, sp_), w0, (), tag=f"scA{g}_{qt}_{kc}")
                    self.op("pe", (lambda qt, kc, sp_: (lambda eng: eng.matmul(
                        sp_[:, 512:1024], self.K_g[64:128, kc * 128:(kc + 1) * 128],
                        self.Q_g[64:128, qt * 512:(qt + 1) * 512], start=True, stop=True,
                        tile_position=(64, 0))))(qt, kc, sp_), (), [("pe", 1)], tag=f"scB{g}_{qt}_{kc}")
                    ev_sc = ("pe", self.p.cnt["pe"])
                    wts = [ev_sc, ev_attsb[ap]]
                    if qt == 0 and kc == 0:
                        wts.append(ebb)
                    ee = self.one("act", (lambda g, ap, sp_: (lambda eng: eng.activation(
                        self.att_sb[ap][:], sp_[:], AF.Exp, bias=self.negb[:, g:g + 1],
                        scale=self.cfg.scale)))(g, ap, sp_), wts, tag=f"exp{g}_{qt}_{kc}")
                    ev_exp_q[kp] = ee
                    dnum = kc - 4 * qt
                    if dnum >= 0:
                        ee = self.one("dve", (lambda ap, dnum: (lambda eng: eng.tensor_tensor(
                            self.att_sb[ap][:].rearrange("p (h q) -> p h q", h=2),
                            self.att_sb[ap][:].rearrange("p (h q) -> p h q", h=2),
                            self.masks[:, dnum * 512:(dnum + 1) * 512].unsqueeze(1)
                                .broadcast_to([128, 2, 512]), ALU.mult)))(ap, dnum),
                            [ee], tag=f"msk{g}_{qt}_{kc}")
                    pend.append((kc, ap, ee))
                    if len(pend) > 2:
                        emit_av(*pend.pop(0), kcm=kcm)
                while pend:
                    emit_av(*pend.pop(0), kcm=kcm)
                ev_avB = ("pe", self.p.cnt["pe"])
                bslot = self.Q[0][0:64, 0:512]
                erA = self.one("act", lambda eng: eng.activation(
                    self.sumrow[:], self.Q[2][64:65, 0:512], AF.Ln), [ev_avB, last_mul], tag=f"nrA{g}_{qt}")
                ebcA = self.op("pe", lambda eng: eng.matmul(
                    bslot, self.ones_f[:, 0:64], self.sumrow[:], start=True, stop=True),
                    [erA, ev_exp_q[0]], [("pe", 1)], tag=f"nbA{g}_{qt}")[0]
                ecA = self.one("act", lambda eng: eng.activation(
                    self.recip64[:], bslot, AF.Exp, scale=-1.0), [ebcA], tag=f"ncA{g}_{qt}")
                mulA = self.one("dve", lambda eng: eng.tensor_mul(
                    self.y_stage[0][:], self.Q[2][0:64, 0:512], self.recip64[:]),
                    [ecA, ev_ystage[0]], tag=f"nmA{g}_{qt}")
                self.dma("sp", "yw0", (lambda g, qt: (lambda eng: eng.dma_start(
                    self.y_d_scr[0:64, g * c.S + qt * 512:g * c.S + (qt + 1) * 512],
                    self.y_stage[0][:])))(g, qt), [mulA], tag=f"ywA{g}_{qt}")
                ev_ystage[0] = ("yw0", self.p.cnt["yw0"])
                erB = self.one("act", lambda eng: eng.activation(
                    self.sumrow2[:], self.Q[2][64:65, 512:1024], AF.Ln), [ev_avB], tag=f"nrB{g}_{qt}")
                ebcB = self.op("pe", lambda eng: eng.matmul(
                    bslot, self.ones_f[:, 0:64], self.sumrow2[:], start=True, stop=True),
                    [erB, ecA], [("pe", 1)], tag=f"nbB{g}_{qt}")[0]
                ecB = self.one("act", lambda eng: eng.activation(
                    self.recip64[:], bslot, AF.Exp, scale=-1.0), [ebcB, mulA], tag=f"ncB{g}_{qt}")
                last_mul = self.one("dve", lambda eng: eng.tensor_mul(
                    self.y_stage[1][:], self.Q[2][0:64, 512:1024], self.recip64[:]),
                    [ecB, ev_ystage[1]], tag=f"nmB{g}_{qt}")
                self.dma("sp", "yw1", (lambda g, qt: (lambda eng: eng.dma_start(
                    self.y_d_scr[64:128, g * c.S + qt * 512:g * c.S + (qt + 1) * 512],
                    self.y_stage[1][:])))(g, qt), [last_mul], tag=f"ywB{g}_{qt}")
                ev_ystage[1] = ("yw1", self.p.cnt["yw1"])
                ev_epi_q0 = ecB
                ev_y_free = last_mul
            ev_vtm_war = [ev_attsb[0], ev_attsb[1]]

    def p3_proj(self):
        c = self.cfg
        ev_wtr_ev = [None, None]
        ev_stage_free = [None, None]
        for dcH in range(c.NDC):
            par = dcH % 2
            evs = self.dma("act", f"ws{par}", (lambda dcH, par: (lambda eng: eng.dma_start(
                self.w_stage[par][:], self.wpb_d[dcH * 128:(dcH + 1) * 128, :])))(dcH, par),
                [ev_stage_free[par], ("rlw", 64)], tag=f"wp_st{dcH}")
            self.pull_h_n(1)
            pv = self.Q[2][:].bitcast(BF16)
            lastev = None
            for fc in range(c.NDC):
                half = fc % 2
                sl = half * 1024
                wts = [ev_wtr_ev[half]]
                if fc == 0:
                    wts.append(evs)
                self.op("pe", (lambda par, fc, sl, pv: (lambda eng: eng.transpose(
                    pv[:, sl:sl + 128], self.w_stage[par][:, fc * 128:(fc + 1) * 128],
                    self.id_bf[:])))(par, fc, sl, pv), wts, [("pe", 1)], tag=f"wp_tr{dcH}_{fc}")
                evt = ("pe", self.p.cnt["pe"])
                lastev = self.one("act", (lambda dcH, fc, sl, pv: (lambda eng: eng.copy(
                    self.wpT[:, fc * c.D + dcH * 128:fc * c.D + (dcH + 1) * 128],
                    pv[:, sl:sl + 128])))(dcH, fc, sl, pv), [evt], tag=f"wp_tre{dcH}_{fc}")
                ev_wtr_ev[half] = lastev
            ev_stage_free[par] = lastev
        ev_wp = lastev
        ev_ev = [None, None]
        ev_ytc = [None, None]
        ev_xp_free = [None, None]
        self.ev_p3_xrd = [None, None]
        yv = self.y_d_scr[:].rearrange("p (fc s) -> p fc s", fc=c.NDC)
        for tc in range(c.NTC):
            par = tc % 2
            prev_aw = ev_ev[par]
            pp = self.Q[par]
            if tc % 2 == 0:
                self.pull_h_n(1)
            eyt = self.dma("sp", f"yt{par}", (lambda tc, par: (lambda eng: eng.dma_start(
                self.y_tc[par][:].rearrange("p (fc t) -> p fc t", fc=c.NDC),
                yv[:, :, tc * 128:(tc + 1) * 128])))(tc, par), [ev_ytc[par]], tag=f"ytd{tc}")
            for dh in range(c.NDH):
                po = pp[:, dh * 512:dh * 512 + c.DHW]
                for fc in range(c.NDC):
                    wts = []
                    if fc == 0 and dh == 0:
                        wts = [ev_wp, prev_aw, eyt]
                    self.op("pe", (lambda tc, dh, fc, po, par: (lambda eng: eng.matmul(
                        po, self.y_tc[par][:, fc * 128:(fc + 1) * 128],
                        self.wpT[:, fc * c.D + dh * c.DHW:fc * c.D + (dh + 1) * c.DHW],
                        start=(fc == 0), stop=False)))(tc, dh, fc, po, par), wts, (), tag=f"pj{tc}_{dh}_{fc}")
                self.op("pe", (lambda dh, po: (lambda eng: eng.matmul(
                    po, self.ones_f[:], self.bp_row[:, dh * c.DHW:(dh + 1) * c.DHW],
                    start=False, stop=False)))(dh, po), (), (), tag=f"pj_bp{tc}_{dh}")
                self.op("pe", (lambda tc, dh, po: (lambda eng: eng.matmul(
                    po, self.w_fm[:, tc * 128:(tc + 1) * 128],
                    self.b2_bf[:, dh * c.DHW:(dh + 1) * c.DHW],
                    start=False, stop=True)))(tc, dh, po), (),
                    [("pe", 1)] if dh == c.NDH - 1 else (), tag=f"pj_b2{tc}_{dh}")
            evt = ("pe", self.p.cnt["pe"])
            ev_ytc[par] = evt
            exl = self.dma("sp", f"xp{par}", (lambda tc, par: (lambda eng: eng.dma_start(
                self.x_rd[par][:], self.x_d[tc * 128:(tc + 1) * 128, :])))(tc, par),
                [ev_xp_free[par]], tag=f"pj_x{tc}")
            eev = self.one("dve", (lambda par, pp: (lambda eng: eng.tensor_add(
                self.attn_stage[par][:], pp[:, 0:c.D], self.x_rd[par][:])))(par, pp),
                [evt, prev_aw, exl], tag=f"pj_ev{tc}")
            ev_xp_free[par] = eev
            self.ev_p3_xrd[par] = eev
            self.dma("act", f"aw{par}", (lambda tc, par: (lambda eng: eng.dma_start(
                self.attn_d[tc * 128:(tc + 1) * 128, :], self.attn_stage[par][:])))(tc, par),
                [eev], tag=f"pj_wr{tc}")
            ev_ev[par] = (f"aw{par}", self.p.cnt[f"aw{par}"])

    def p4_ffn(self):
        c = self.cfg
        ev_w2_free = [None, None]
        ev_hrd_free = [None, None, None, None]
        ev_psum_tc = [None] * c.NTCT
        ev_stop_tc = [None] * c.NTCT
        ev_ostage = [None, None, None, None]
        self.ow_marks = []
        hrd_ix = 0
        w2_ix = 0
        for sgi in range(c.NSG):
            e = sgi // c.NCH
            q3g = [ev_psum_tc[6], ev_psum_tc[7]] if c.NTCT > 6 else []
            self.hst["q3_guard"] = [x for x in q3g if x is not None]
            cbi = sgi % 2
            self.load_chunk(sgi)
            wv3 = self.wsl[cbi][:].rearrange("p (a w) -> p a w", w=2 * c.E)
            ev_wslf = self.one("dve", (lambda e, cbi, wv3: (lambda eng: eng.tensor_add(
                self.wslf[cbi][:], wv3[:, :, 2 * e:2 * e + 1].squeeze(2),
                wv3[:, :, 2 * e + 1:2 * e + 2].squeeze(2))))(e, cbi, wv3),
                [self.hst["chunk_loaded"][sgi]], tag=f"wslf{sgi}")
            self.ev_wslf_ev[sgi] = ev_wslf
            self.pull_h_until(min(sgi + 1, c.NSG - 1))
            ge_last = [self.hst["ev_gelu"][0], self.hst["ev_gelu"][1]]
            for dh in range(c.NDH):
                for hp2 in range(c.NP1):
                    bi = w2_ix % 2
                    self.dma("pool", f"w2{bi}", (lambda e, hp2, dh, bi: (lambda eng: eng.dma_start(
                        self.w2buf[bi][:],
                        self.w2b_d[e, dh, hp2].rearrange(
                            "p (hcl f) -> p hcl f", f=c.DHW))))(e, hp2, dh, bi),
                        [ev_w2_free[bi], ("rl", 16 * self.n_relay)], tag=f"w2d{sgi}_{dh}_{hp2}")
                    evw2 = (f"w2{bi}", self.p.cnt[f"w2{bi}"])
                    w2_ix += 1
                    for hcl in range(4):
                        hc = hp2 * 4 + hcl
                        rbi = hrd_ix % 4
                        self.dma("sp", f"hr{rbi}", (lambda sgi, hc, rbi: (lambda eng: eng.dma_start(
                            self.h_rd[rbi][:], self.h_d[sgi, hc * 128:(hc + 1) * 128, :])))(sgi, hc, rbi),
                            [self.ev_hwr_ev[(sgi, hc)], ev_hrd_free[rbi]], tag=f"hrd{sgi}_{dh}_{hc}")
                        evhr = (f"hr{rbi}", self.p.cnt[f"hr{rbi}"])
                        hrd_ix += 1
                        for tc in range(c.NTCT):
                            op = self.Q[tc // 2][:, (tc % 2) * 512:(tc % 2) * 512 + c.DHW]
                            wts = []
                            if tc == 0:
                                wts = [evw2, evhr]
                            if hc == 0:
                                wts.append(ev_psum_tc[tc])
                                if tc >= 6:
                                    wts.append(ge_last[tc - 6])
                            do_inc = (tc == c.NTCT - 1) or (hc == c.HC - 1)
                            self.op("pe", (lambda hcl, tc, rbi, bi, op, hc: (lambda eng: eng.matmul(
                                op, self.h_rd[rbi][:, tc * 128:(tc + 1) * 128],
                                self.w2buf[bi][:, hcl, :],
                                start=(hc == 0), stop=(hc == c.HC - 1))))(hcl, tc, rbi, bi, op, hc),
                                wts,
                                [("pe", 1)] if do_inc else (),
                                tag=f"o_mm{sgi}_{dh}_{hc}_{tc}")
                            if hc == c.HC - 1:
                                ev_stop_tc[tc] = ("pe", self.p.cnt["pe"])
                        evo = ("pe", self.p.cnt["pe"])
                        if hcl == 3:
                            ev_w2_free[bi] = evo
                        ev_hrd_free[rbi] = evo
                order = [6, 7, 0, 1, 2, 3, 4, 5] if c.NTCT == 8 else list(range(c.NTCT))
                for k, tc in enumerate(order):
                    op = self.Q[tc // 2][:, (tc % 2) * 512:(tc % 2) * 512 + c.DHW]
                    o = k % 4
                    wcol = self.wslf[cbi][:, tc:tc + 1]
                    ev = self.one("dve", (lambda op, wcol, o: (lambda eng: eng.tensor_scalar(
                        self.out_stage[o][:], op, wcol, None, ALU.mult)))(op, wcol, o),
                        [ev_stop_tc[tc], ev_ostage[o], ev_wslf], tag=f"stt{sgi}_{dh}_{tc}")
                    ev_psum_tc[tc] = ev
                    self.dma("act", f"ow{o}", (lambda sgi, dh, tc, o: (lambda eng: eng.dma_start(
                        self.outsl_d[sgi * c.TT + tc * 128: sgi * c.TT + (tc + 1) * 128,
                                     dh * c.DHW:(dh + 1) * c.DHW],
                        self.out_stage[o][:])))(sgi, dh, tc, o), [ev], tag=f"osw{sgi}_{dh}_{tc}")
                    ev_ostage[o] = (f"ow{o}", self.p.cnt[f"ow{o}"])
            if (sgi + 1) % c.NCH == 0:
                self.ow_marks.append([(f"ow{o}", self.p.cnt[f"ow{o}"])
                                      for o in range(4) if f"ow{o}" in self.p.cnt])
        ow_all = [(s, self.p.cnt[s]) for s in ("ow0", "ow1", "ow2", "ow3") if s in self.p.cnt]
        aw_all = [(s, self.p.cnt[s]) for s in ("aw0", "aw1") if s in self.p.cnt]
        lanes = [self.attn_rd[0], self.attn_rd[1], self.x_rd[0], self.x_rd[1]]
        ev_lane = [None, None, self.ev_p3_xrd[0], self.ev_p3_xrd[1]]
        for tc in range(c.NTC):
            ln = tc % 4
            buf = lanes[ln]
            ea = self.dma("sp", f"al{ln}", (lambda tc, buf: (lambda eng: eng.dma_start(
                buf[:], self.attn_d[tc * 128:(tc + 1) * 128, :])))(tc, buf),
                aw_all + [ev_lane[ln]], tag=f"fa{tc}")
            evg = ea
            for e in range(c.E):
                self.dma("pool", f"gp{ln}", (lambda tc, buf, e: (lambda eng: eng.indirect_dma_start(
                    buf[:], None,
                    self.outsl_d[:],
                    IndirectOffsetOnAxis(self.idxs_i[:, tc * c.E + e: tc * c.E + e + 1], 0),
                    bounds_check=self._bc_reg, oob_is_err=False,
                    compute_op=ALU.add)))(tc, buf, e),
                    self.ow_marks[e] + [evg], tag=f"gad{tc}_{e}")
                evg = (f"gp{ln}", self.p.cnt[f"gp{ln}"])
            self.dma("sp", f"os{ln}", (lambda tc, buf: (lambda eng: eng.dma_start(
                self.y_d[tc * 128:(tc + 1) * 128, :], buf[:])))(tc, buf),
                [evg], tag=f"fout{tc}")
            ev_lane[ln] = (f"os{ln}", self.p.cnt[f"os{ln}"])

    def emit(self):
        nc = self.nc
        semh = self.semh = {s: nc.alloc_semaphore(s) for s in self.p.cnt}
        waited = {e: {} for e in ENGS}

        def run(eng_obj, ename):
            seen = waited[ename]
            for waits, emitfn, incs, tag in self.p.q[ename]:
                for s, v in waits:
                    if seen.get(s, -1) >= v:
                        continue
                    eng_obj.wait_ge(semh[s], v)
                    seen[s] = v
                inst = emitfn(eng_obj)
                for s, n in incs:
                    inst.then_inc(semh[s], n)

        with nc.Block() as blk:
            blk.tensor(lambda eng: run(eng, "pe"))
            blk.scalar(lambda eng: run(eng, "act"))
            blk.vector(lambda eng: run(eng, "dve"))
            blk.gpsimd(lambda eng: run(eng, "pool"))
            blk.sync(lambda eng: run(eng, "sp"))


def build_program(cfg: Cfg):
    return Builder(cfg).build()


_IN_KEYS = {
    "x": "L0_attn_x", "wq": "L0_attn_Wq", "bq": "L0_attn_bq", "wk": "L0_attn_Wk",
    "bk": "L0_attn_bk", "wv": "L0_attn_Wv", "bv": "L0_attn_bv", "wp": "L0_attn_Wp",
    "bp": "L0_attn_bp", "gw": "L0_ffn_gate_w", "gb": "L0_ffn_gate_b",
    "w1": "L0_ffn_w1", "b1": "L0_ffn_b1", "w2": "L0_ffn_w2", "b2": "L0_ffn_b2",
}


def kernel(**inputs):
    cfg = Cfg()
    n = 8
    x = np.ascontiguousarray(np.asarray(inputs["L0_attn_x"], dtype=np.float32))
    common = {k: np.ascontiguousarray(np.asarray(inputs[v], dtype=np.float32))
              for k, v in _IN_KEYS.items() if k != "x"}
    nc = build_program(cfg)
    in_maps = []
    for ci in range(n):
        m = dict(common)
        m["x"] = np.ascontiguousarray(x[ci])
        in_maps.append(m)
    res = run_bass_kernel_spmd(nc, in_maps, core_ids=list(range(n)))
    out = np.stack([res.results[ci]["y"] for ci in range(n)], axis=0)
    return out.astype(np.float32)


# version 15
# speedup vs baseline: 1.1062x; 1.1062x over previous
import math
from dataclasses import dataclass

import numpy as np
import ml_dtypes

import concourse.bass as bass
import concourse.mybir as mybir
from concourse.bass_utils import run_bass_kernel_spmd
from concourse.bass_core_rust import IndirectOffsetOnAxis

F32 = mybir.dt.float32
BF16 = mybir.dt.bfloat16
I32 = mybir.dt.int32
AF = mybir.ActivationFunctionType
ALU = mybir.AluOpType
AX = mybir.AxisListType

ENGS = ("pe", "act", "dve", "pool", "sp")
DMA_SEMS = ("w1", "w2", "wst", "hwr", "hrd", "xld", "out", "cst", "cstp", "awr")


@dataclass
class Cfg:
    D: int = 1024
    S: int = 4096
    E: int = 8
    HF: int = 4096
    TT: int = 1024
    QT: int = 512
    CAP: int = 3072

    @property
    def NDC(self): return self.D // 128
    @property
    def NTC(self): return self.S // 128
    @property
    def NQT(self): return self.S // self.QT
    @property
    def HC(self): return self.HF // 128
    @property
    def NP1(self): return self.HF // 512
    @property
    def NDH(self): return max(1, self.D // 512)
    @property
    def DHW(self): return min(512, self.D)
    @property
    def NTCT(self): return self.TT // 128
    @property
    def NTT(self): return self.S // self.TT
    @property
    def NCH(self): return self.CAP // self.TT
    @property
    def NSG(self): return self.E * self.NCH
    @property
    def RW(self): return self.D + 2 * self.E
    @property
    def scale(self): return 1.0 / math.sqrt(self.D)


class Planner:
    def __init__(self):
        self.q = {e: [] for e in ENGS}
        self.cnt = {}
        self.last_event = {e: None for e in ENGS}

    def sem(self, name):
        self.cnt[name] = 0

    def __post_init_events(self):
        pass

    def op(self, eng, emit, waits=(), incs=(), tag=""):
        w = []
        for x in waits:
            if x is None:
                continue
            s, v = x
            assert s in self.cnt, s
            w.append((s, int(v)))
        out = []
        for s, n in incs:
            if s not in self.cnt:
                self.cnt[s] = 0
            self.cnt[s] += n
            out.append((s, self.cnt[s]))
        if out:
            self.last_event[eng] = out[0]
        self.q[eng].append([tuple(w), emit, tuple(incs), tag])
        return out

    def inc_last(self, eng, sem):
        assert self.q[eng], eng
        last = self.q[eng][-1]
        assert not last[2], last
        if sem not in self.cnt:
            self.cnt[sem] = 0
        self.cnt[sem] += 1
        last[2] = ((sem, 1),)
        self.last_event[eng] = (sem, self.cnt[sem])
        return 1

    def check(self):
        for e in ENGS:
            for waits, _, _, tag in self.q[e]:
                for s, v in waits:
                    assert v <= self.cnt[s], \
                        f"unreachable wait {s}>={v} (final {self.cnt[s]}) at {e}:{tag}"
        cur = {s: 0 for s in self.cnt}
        ptr = {e: 0 for e in ENGS}
        total = sum(len(self.q[e]) for e in ENGS)
        done = 0
        while done < total:
            progressed = False
            for e in ENGS:
                while ptr[e] < len(self.q[e]):
                    waits, _, incs, tag = self.q[e][ptr[e]]
                    if all(cur[s] >= v for s, v in waits):
                        for s, n in incs:
                            cur[s] += n
                        ptr[e] += 1
                        done += 1
                        progressed = True
                    else:
                        break
            if not progressed:
                details = []
                for e in ENGS:
                    if ptr[e] < len(self.q[e]):
                        waits, _, _, tag = self.q[e][ptr[e]]
                        blocked = [(s, v, cur[s]) for s, v in waits if cur[s] < v]
                        details.append(f"{e}@{ptr[e]} tag={tag} blocked_on={blocked}")
                raise RuntimeError("DEADLOCK in plan:\n" + "\n".join(details))


class Builder:
    def __init__(self, cfg: Cfg):
        self.cfg = cfg
        nc = bass.Bass("TRN2", target_bir_lowering=False)
        self.nc = nc
        self.p = Planner()
        for s in ENGS:
            self.p.sem(s)

        c = cfg
        dt_ = nc.dram_tensor
        self.x_d = dt_("x", [c.S, c.D], F32, kind="ExternalInput")
        self.wq_d = dt_("wq", [c.D, c.D], F32, kind="ExternalInput")
        self.wk_d = dt_("wk", [c.D, c.D], F32, kind="ExternalInput")
        self.wv_d = dt_("wv", [c.D, c.D], F32, kind="ExternalInput")
        self.wp_d = dt_("wp", [c.D, c.D], F32, kind="ExternalInput")
        self.bq_d = dt_("bq", [c.D], F32, kind="ExternalInput")
        self.bk_d = dt_("bk", [c.D], F32, kind="ExternalInput")
        self.bv_d = dt_("bv", [c.D], F32, kind="ExternalInput")
        self.bp_d = dt_("bp", [c.D], F32, kind="ExternalInput")
        self.gw_d = dt_("gw", [c.E, c.D], F32, kind="ExternalInput")
        self.gb_d = dt_("gb", [c.E], F32, kind="ExternalInput")
        self.w1_d = dt_("w1", [c.E, c.D, c.HF], F32, kind="ExternalInput")
        self.b1_d = dt_("b1", [c.E, c.HF], F32, kind="ExternalInput")
        self.w2_d = dt_("w2", [c.E, c.HF, c.D], F32, kind="ExternalInput")
        self.b2_d = dt_("b2", [c.E, c.D], F32, kind="ExternalInput")
        self.y_d = dt_("y", [c.S, c.D], F32, kind="ExternalOutput")
        self.attn_d = dt_("attn_scr", [c.S, c.D], F32, kind="Internal")
        self.y_d_scr = dt_("y_scr", [128, c.NDC * c.S], BF16, kind="Internal")
        self.h_d = dt_("h_scr", [c.NSG, c.HF, c.TT], BF16, kind="Internal")
        self.xrow_d = dt_("xrow_scr", [c.S, c.RW], BF16, kind="Internal")
        self.xg_d = dt_("xg_scr", [c.E * c.CAP, c.RW], BF16, kind="Internal")
        self.outsl_d = dt_("outsl_scr", [c.E * c.CAP, c.D], F32, kind="Internal")
        self.w1b_d = dt_("w1b_scr", [c.E, c.NP1, 128, c.NDC * 4 * 128], BF16, kind="Internal")
        self.wqb_d = dt_("wqb_scr", [c.D, c.D], BF16, kind="Internal")
        self.wkb_d = dt_("wkb_scr", [c.D, c.D], BF16, kind="Internal")
        self.wvb_d = dt_("wvb_scr", [c.D, c.D], BF16, kind="Internal")
        self.wpb_d = dt_("wpb_scr", [c.D, c.D], BF16, kind="Internal")
        self.w2b_d = dt_("w2b_scr", [c.E, c.NDH, c.NP1, 128, 4 * c.DHW], BF16, kind="Internal")

        self.id_bf_d = nc.inline_tensor(np.eye(128, dtype=ml_dtypes.bfloat16), "id_bf_c")
        self.id_f_d = nc.inline_tensor(np.eye(128, dtype=np.float32), "id_f_c")
        self.ones_f_d = nc.inline_tensor(np.ones((1, 128), dtype=np.float32), "ones_f_c")
        m = np.zeros((128, 4, 512), dtype=np.float32)
        for d in range(4):
            for kp in range(128):
                qf0 = kp + 128 * d
                if qf0 < 512:
                    m[kp, d, qf0:] = 1.0
        self.mask_d = nc.inline_tensor(m.reshape(128, 2048).astype(ml_dtypes.bfloat16), "mask_c")
        ecap = np.zeros((128, c.NTC * c.E), dtype=np.float32)
        for e in range(c.E):
            ecap[:, e::c.E] = e * c.CAP
        self.ecap_d = nc.inline_tensor(ecap, "ecap_c")

        ARENA = 208960
        BASE = 16512
        ah = nc.alloc_sbuf_tensor("arena", [128, ARENA], mybir.dt.uint8)
        assert nc.lookup_mloc(ah).addr == BASE, nc.lookup_mloc(ah).addr
        self.ARENA = ARENA
        A_raw = nc.alloc_sbuf_tensor_at

        def A(name, shape, dtype, offset):
            return A_raw(name, shape, dtype, offset=offset + BASE)
        self._off = 0

        def at(name, shape, dtype):
            self._off = (self._off + 63) // 64 * 64
            t = A(name, shape, dtype, offset=self._off)
            self._off += int(np.prod(shape[1:])) * mybir.dt.size(dtype)
            return t

        self.id_bf = at("id_bf_s", [128, 128], BF16)
        self.id_f = at("id_f_s", [128, 128], F32)
        self.ones_f = at("ones_f_s", [1, 128], F32)
        self.masks = at("masks_s", [128, 2048], BF16)
        self.bqkT = at("bqkT", [128, 2 * c.NDC], F32)
        self.b1T = at("b1T", [128, c.HC * c.E], F32)
        self.b2_bf = at("b2_bf", [c.E, c.D], BF16)
        self.bp_row = at("bp_row", [1, c.D], F32)
        self.bv_row = at("bv_row", [1, c.D], F32)
        self.gb_s = at("gb_s", [c.E, 1], F32)
        self.gate_wT = at("gate_wT", [128, c.NDC * c.E], BF16)
        self.w_gates_tm = at("w_gates_tm", [128, c.NTC * c.E], F32)
        self.w_fm = at("w_fm", [c.E, c.S], BF16)
        self.negb = at("negb", [128, c.NDC], F32)
        self.scr = at("scr", [128, 64], F32)
        self.consts = at("consts", [128, 2], F32)
        self.scr1 = at("scr1", [1, 64], F32)
        self.recip64 = at("recip64", [64, 512], F32)
        self.sumrow = at("sumrow", [1, 512], F32)
        self.sumrow2 = at("sumrow2", [1, 512], F32)

        A0 = self._off = (self._off + 63) // 64 * 64
        self.xn = A("xn", [128, c.NDC * c.S], BF16, offset=A0)
        o = A0
        def atA(name, shape, dtype):
            nonlocal o
            o = (o + 63) // 64 * 64
            t = A(name, shape, dtype, offset=o)
            o += int(np.prod(shape[1:])) * mybir.dt.size(dtype)
            return t
        self.xga = [atA(f"xga{i}", [128, c.NDC * c.TT], BF16) for i in range(2)]
        self.w1buf = [atA(f"w1buf{i}", [128, c.NDC, 4, 128], BF16) for i in range(2)]
        self.h_rd = [atA(f"h_rd{i}", [128, c.TT], BF16) for i in range(4)]
        self.h_wr = [atA(f"h_wr{i}", [128, c.TT], BF16) for i in range(2)]
        self.wsl = [atA(f"wsl{i}", [128, (c.TT // 128) * 2 * c.E], BF16) for i in range(2)]
        self._off = max(o, A0 + c.NDC * c.S * 2)

        self.w2buf = [at(f"w2buf{i}", [128, 4, c.DHW], BF16) for i in range(2)]
        self.y_stage = [at(f"y_stage{i}", [64, 512], BF16) for i in range(2)]
        self.x_rd = [at(f"x_rd{i}", [128, c.D], F32) for i in range(2)]
        self.attn_rd = [at(f"attn_rd{i}", [128, c.D], F32) for i in range(2)]
        self.y_tc = [at(f"y_tc{i}", [128, c.NDC * 128], BF16) for i in range(2)]
        self.out_stage = [at(f"out_stage{i}", [128, 512], F32) for i in range(4)]
        self.xrb = [at(f"xrb{i}", [128, c.RW], BF16) for i in range(2)]
        self.idxs_i = at("idxs_i", [128, c.NTC * c.E], I32)
        self.s12_i = at("s12_i", [128, 2 * c.NTC], I32)
        self.sc_a = at("sc_a", [128, c.NTC * c.E], F32)
        self.sc_b = at("sc_b", [128, c.NTC * c.E], F32)
        self.sc_c = at("sc_c", [128, c.NTC * c.E], F32)
        self.sc_d = at("sc_d", [128, c.NTC * c.E], F32)
        self.sc_e = at("sc_e", [128, c.NTC * c.E], F32)
        self.sc_f = at("sc_f", [128, c.NTC * c.E], F32)
        self.sc_sm = at("sc_sm", [128, 256], F32)
        self.pp_a = at("pp_a", [8, 256], F32)
        self.pp_b = at("pp_b", [8, 256], F32)
        self.ecap_s = at("ecap_s", [128, c.NTC * c.E], F32)
        self.whi_bf = at("whi_bf", [128, c.NTC * c.E], BF16)
        self.wlo_bf = at("wlo_bf", [128, c.NTC * c.E], BF16)
        self.w2x = at("w2x", [128, c.NTC * 2 * c.E], BF16)
        self.wslf = [at(f"wslf{i}", [128, c.NTCT], F32) for i in range(2)]

        C0 = self._off = (self._off + 63) // 64 * 64
        self.x_tm = [A(f"x_tm{i}", [128, c.D], F32, offset=C0 + i * c.D * 4) for i in range(2)]
        o = C0 + 2 * c.D * 4
        self.xn_tm = [A(f"xn_tm{i}", [128, c.D], BF16, offset=o + i * c.D * 2) for i in range(2)]
        o += 2 * c.D * 2
        self.sq = [A(f"sq{i}", [128, c.D], F32, offset=o + i * c.D * 4) for i in range(2)]
        o += 2 * c.D * 4
        self.b1_stage = A("b1_stage", [c.E, c.HF], F32, offset=o); o += c.HF * 4
        self.gw_stage = A("gw_stage", [c.E, c.D], F32, offset=o); o += c.D * 4
        self.bqk_stage = A("bqk_stage", [8, 256], F32, offset=o); o += 1024
        p0_end = o
        self.gates_fm = A("gates_fm", [c.E, c.S], F32, offset=C0)
        o = C0 + c.S * 4
        self.g_tm = A("g_tm", [128, c.NTC * c.E], F32, offset=o); o += c.NTC * c.E * 4
        self.tk = []
        for i in range(4):
            self.tk.append(A(f"tk{i}", [128, c.NTC * c.E], F32, offset=o)); o += c.NTC * c.E * 4
        self.tks = A("tks", [128, 4 * c.NTC], F32, offset=o); o += 4 * c.NTC * 4
        p1_end = o
        self.Q_g = A("Q_g", [128, c.S], BF16, offset=C0)
        self.K_g = A("K_g", [128, c.S], BF16, offset=C0 + c.S * 2)
        o = C0 + 2 * c.S * 2
        vsz = ((c.S // 128) * 65 * 2 + 63) // 64 * 64
        self.v_tm = [A(f"v_tm{i}", [128, (c.S // 128) * 65], BF16, offset=o + i * vsz) for i in range(2)]
        o += 2 * vsz
        self.att_sb = [A(f"att_sb{i}", [128, 1024], BF16, offset=o + i * 2048) for i in range(4)]
        o += 8192
        self.wqT_g = A("wqT_g", [128, c.D], BF16, offset=o); o += c.D * 2
        self.wkT_g = A("wkT_g", [128, c.D], BF16, offset=o); o += c.D * 2
        self.wvT_g = A("wvT_g", [128, c.D], BF16, offset=o); o += c.D * 2
        self.w_stage = [A(f"w_stage{i}", [128, c.D], BF16, offset=o + i * c.D * 2) for i in range(2)]
        o += 2 * c.D * 2
        p2_end = o
        self.wpT = A("wpT", [128, c.NDC * c.D], BF16, offset=C0)
        o = C0 + c.NDC * c.D * 2
        self.attn_stage = [A(f"attn_stage{i}", [128, c.D], F32, offset=o + i * c.D * 4) for i in range(2)]
        o += 2 * c.D * 4
        p3_end = o
        self.out_acc = A("out_acc", [128, c.NTCT * c.D], F32, offset=C0)
        p4_end = C0 + c.NTCT * c.D * 4
        self.sbuf_top = max(p0_end, p1_end, p2_end, p3_end, p4_end)
        assert self.sbuf_top <= ARENA, f"SBUF over budget: {self.sbuf_top} > {ARENA}"

        self.Q = [nc.alloc_psum_tensor(f"psq{i}", [128, 1024], F32) for i in range(4)]

    def op(self, eng, emit, waits=(), incs=(), tag=""):
        return self.p.op(eng, emit, waits, incs, tag)

    def one(self, eng, emit, waits=(), tag=""):
        return self.op(eng, emit, waits, [(eng, 1)], tag)[0]

    def dma(self, eng, sem, emit, waits=(), tag=""):
        return self.op(eng, emit, waits, [(sem, 16)], tag)[0]

    def barrier(self, tag="bar", exclude=()):
        snap = dict(self.p.cnt)
        self._nbar = getattr(self, "_nbar", 0) + 1
        bs = f"bar{self._nbar}"
        self.op("pool", lambda eng: eng.drain(), (), [(bs, 1)], tag=tag + "_a")
        waits = [(bs, 1)]
        for e in ENGS:
            if e != "pool" and self.p.q[e] and not self.p.q[e][-1][2]:
                self.p.inc_last(e, e)
            if self.p.last_event[e] is not None:
                waits.append(self.p.last_event[e])
        waits += [(s, v) for s, v in snap.items()
                  if s not in ENGS and not s.startswith("bar") and v > 0
                  and s not in exclude]
        wm = {}
        for s, v in waits:
            wm[s] = max(wm.get(s, 0), v)
        waits = sorted(wm.items())
        for e in ENGS:
            self.op(e, lambda eng: eng.nop(), waits, (), tag=tag + "_b")

    def build(self):
        self.p0_consts()
        self.p0_ln()
        self.barrier("after_ln")
        self.p1_gates()
        self.barrier("after_gates", exclude=("scp0", "scp1", "xb0", "xb1"))
        self.make_hgen()
        self.mk_relay()
        self.p2_attn()
        self.relay_drip(10 ** 9)
        self.barrier("after_attn")
        self.p3_proj()
        self.barrier("after_proj")
        self.p4_ffn()
        self.op("sp", lambda eng: eng.nop(),
                [(s, v) for s, v in self.p.cnt.items() if s not in ENGS and v > 0],
                [("sp", 1)], tag="final_wait")
        self.p.check()
        self.emit()
        return self.nc

    def p0_consts(self):
        c = self.cfg
        for dst, srcd in ((self.wqb_d, self.wq_d), (self.wkb_d, self.wk_d),
                          (self.wvb_d, self.wv_d), (self.wpb_d, self.wp_d)):
            self.dma("pool", "rlw", (lambda dst, srcd: (lambda eng: eng.dma_start(
                dst[:], srcd[:])))(dst, srcd), [], tag="rlw")
        loads = [
            (self.id_bf[:], self.id_bf_d[:]),
            (self.id_f[:], self.id_f_d[:]),
            (self.ones_f[:], self.ones_f_d[:]),
            (self.masks[:], self.mask_d[:]),
            (self.bp_row[:], self.bp_d[:].unsqueeze(0)),
            (self.bv_row[:], self.bv_d[:].unsqueeze(0)),
            (self.gb_s[:], self.gb_d[:].unsqueeze(1)),
            (self.b1_stage[:], self.b1_d[:]),
            (self.gw_stage[:], self.gw_d[:]),
            (self.bqk_stage[0:c.NDC, 0:128], self.bq_d[:].rearrange("(a b) -> a b", b=128)),
            (self.bqk_stage[0:c.NDC, 128:256], self.bk_d[:].rearrange("(a b) -> a b", b=128)),
            (self.ecap_s[:], self.ecap_d[:]),
        ]
        for dst, src in loads:
            self.dma("sp", "cst", (lambda d, s: (lambda eng: eng.dma_start(d, s)))(dst, src), tag="cst")
        self.op("pool", lambda eng: eng.memset(self.consts[:, 0:1], 1e-5), (), [("ms0", 1)], tag="c_eps")
        self.op("pool", lambda eng: eng.memset(self.consts[:, 1:2], 1.0), (), [("ms1", 1)], tag="c_one")
        self.ev_consts = ("ms0", 1)
        self.dma("pool", "cstp", lambda eng: eng.dma_start(self.b2_bf[:], self.b2_d[:]), tag="b2")
        cst_all = ("cst", self.p.cnt["cst"])

        tp = self.Q[2]
        for j, colbase in ((0, 0), (1, c.NDC)):
            evt = self.op("pe", (lambda j: (lambda eng: eng.transpose(
                self.Q[2][0:128, 0:c.NDC], self.bqk_stage[0:c.NDC, 128 * j:128 * (j + 1)],
                self.id_f[0:c.NDC, 0:c.NDC])))(j),
                [cst_all, ("act", self.p.cnt["act"])], [("pe", 1)], tag=f"bqk_tr{j}")[0]
            self.one("act", (lambda cb: (lambda eng: eng.copy(
                self.bqkT[:, cb:cb + c.NDC], self.Q[2][0:128, 0:c.NDC])))(colbase), [evt], tag=f"bqk_ev{j}")
        for hc in range(c.HC):
            evt = self.op("pe", (lambda hc: (lambda eng: eng.transpose(
                self.Q[2][0:128, 8:8 + c.E], self.b1_stage[0:c.E, hc * 128:(hc + 1) * 128],
                self.id_f[0:c.E, 0:c.E])))(hc),
                [cst_all, ("act", self.p.cnt["act"])], [("pe", 1)], tag=f"b1_tr{hc}")[0]
            self.one("act", (lambda hc: (lambda eng: eng.copy(
                self.b1T[:, hc * c.E:(hc + 1) * c.E], self.Q[2][0:128, 8:8 + c.E])))(hc), [evt], tag=f"b1_ev{hc}")
        for dc in range(c.NDC):
            evt = self.op("pe", (lambda dc: (lambda eng: eng.transpose(
                self.Q[2][0:128, 16:16 + c.E], self.gw_stage[0:c.E, dc * 128:(dc + 1) * 128],
                self.id_f[0:c.E, 0:c.E])))(dc),
                [cst_all, ("act", self.p.cnt["act"])], [("pe", 1)], tag=f"gw_tr{dc}")[0]
            self.one("act", (lambda dc: (lambda eng: eng.copy(
                self.gate_wT[:, dc * c.E:(dc + 1) * c.E], self.Q[2][0:128, 16:16 + c.E])))(dc),
                [evt], tag=f"gw_ev{dc}")

    def p0_ln(self):
        c = self.cfg
        NP = 1.0 / c.D
        ev_dve = [None, None]
        ev_sq = [None, None]
        ev_evac = [None, None]
        ev_var = [None, None]
        ev_trans = [None, None]
        ev_xrw = [None, None]
        for tc in range(c.NTC):
            par = tc % 2
            waits = [ev_dve[par], ev_sq[par]]
            evx = self.dma("sp" if par == 0 else "pool", f"xl{par}",
                (lambda tc, par: (lambda eng: eng.dma_start(
                    self.x_tm[par][:], self.x_d[tc * 128:(tc + 1) * 128, :])))(tc, par),
                waits, tag=f"ln_x{tc}")
            o = par * 8
            m_sl, nm_sl, v_sl, sd_sl, rs_sl = (self.scr[:, o + i:o + i + 1] for i in range(5))
            e1 = self.one("dve", (lambda par, m_sl: (lambda eng: eng.reduce_sum(
                m_sl, self.x_tm[par][:], axis=AX.X)))(par, m_sl), [evx], tag=f"ln_mean{tc}")
            e2 = self.one("act", (lambda m_sl, nm_sl: (lambda eng: eng.mul(nm_sl, m_sl, -NP)))(m_sl, nm_sl),
                          [e1], tag=f"ln_negm{tc}")
            e3 = self.one("act", (lambda par, nm_sl: (lambda eng: eng.activation(
                self.sq[par][:], self.x_tm[par][:], AF.Square, bias=nm_sl)))(par, nm_sl),
                [e2, ev_var[par]], tag=f"ln_sq{tc}")
            ev_sq[par] = e3
            e4 = self.one("dve", (lambda par, v_sl: (lambda eng: eng.reduce_sum(
                v_sl, self.sq[par][:], axis=AX.X)))(par, v_sl), [e3], tag=f"ln_var{tc}")
            ev_var[par] = e4
            e5 = self.one("act", (lambda v_sl, sd_sl: (lambda eng: eng.activation(
                sd_sl, v_sl, AF.Sqrt, bias=self.consts[:, 0:1], scale=NP)))(v_sl, sd_sl),
                [e4, self.ev_consts if tc == 0 else None], tag=f"ln_std{tc}")
            e6 = self.one("dve", (lambda sd_sl, rs_sl: (lambda eng: eng.reciprocal(rs_sl, sd_sl)))(sd_sl, rs_sl),
                          [e5], tag=f"ln_rstd{tc}")
            e7 = self.one("dve", (lambda par, nm_sl, rs_sl: (lambda eng: eng.tensor_scalar(
                self.xn_tm[par][:], self.x_tm[par][:], nm_sl, rs_sl, ALU.add, ALU.mult)))(par, nm_sl, rs_sl),
                [e6, ev_trans[par], ev_xrw[par]], tag=f"ln_xn{tc}")
            ev_dve[par] = e7
            ev_xrw[par] = self.dma("sp", f"xw{par}", (lambda tc, par: (lambda eng: eng.dma_start(
                self.xrow_d[tc * 128:(tc + 1) * 128, 0:c.D], self.xn_tm[par][:])))(tc, par),
                [e7], tag=f"xrw{tc}")
            pv = self.Q[par][:].bitcast(BF16)
            for dc in range(c.NDC):
                self.op("pe", (lambda par, dc, pv: (lambda eng: eng.transpose(
                    pv[:, dc * 128:(dc + 1) * 128],
                    self.xn_tm[par][:, dc * 128:(dc + 1) * 128], self.id_bf[:])))(par, dc, pv),
                    [e7, ev_evac[par]] if dc == 0 else (),
                    [("pe", 1)] if dc == c.NDC - 1 else (), tag=f"ln_tr{tc}_{dc}")
            evt = ("pe", self.p.cnt["pe"])
            ev_trans[par] = evt
            dst = self.xn[:].rearrange("p (dc s) -> p dc s", dc=c.NDC)[:, :, tc * 128:(tc + 1) * 128]
            src = pv[:, 0:c.NDC * 128].rearrange("p (dc t) -> p dc t", dc=c.NDC)
            ev_evac[par] = self.one("act", (lambda dst, src: (lambda eng: eng.copy(dst, src)))(dst, src),
                                    [evt], tag=f"ln_ev{tc}")

    def p1_gates(self):
        c = self.cfg
        ns = c.S // 512
        ev_ev = [None, None]
        for s in range(ns):
            par = s % 2
            gp = self.Q[1][0:c.E, par * 512:(par + 1) * 512]
            for dc in range(c.NDC):
                self.op("pe", (lambda s, dc, gp: (lambda eng: eng.matmul(
                    gp, self.gate_wT[:, dc * c.E:(dc + 1) * c.E],
                    self.xn[:, dc * c.S + s * 512: dc * c.S + (s + 1) * 512],
                    start=(dc == 0), stop=(dc == c.NDC - 1))))(s, dc, gp),
                    [ev_ev[par]] if dc == 0 else (),
                    [("pe", 1)] if dc == c.NDC - 1 else (), tag=f"g_mm{s}_{dc}")
            evt = ("pe", self.p.cnt["pe"])
            ev_ev[par] = self.one("act", (lambda s, gp: (lambda eng: eng.activation(
                self.gates_fm[:, s * 512:(s + 1) * 512], gp, AF.Identity, bias=self.gb_s[:])))(s, gp),
                [evt], tag=f"g_ev{s}")
        ga0, ga1 = ev_ev[0], ev_ev[1]
        ev_tre = [None, None]
        for tc in range(c.NTC):
            par = tc % 2
            sl = 32 + par * 512
            evt = self.op("pe", (lambda tc, sl: (lambda eng: eng.transpose(
                self.Q[2][0:128, sl:sl + c.E], self.gates_fm[0:c.E, tc * 128:(tc + 1) * 128],
                self.id_f[0:c.E, 0:c.E])))(tc, sl),
                [ga0, ga1, ev_tre[par]], [("pe", 1)], tag=f"g_tr{tc}")[0]
            ev_tre[par] = self.one("act", (lambda tc, sl: (lambda eng: eng.copy(
                self.g_tm[:, tc * c.E:(tc + 1) * c.E], self.Q[2][0:128, sl:sl + c.E])))(tc, sl),
                [evt], tag=f"g_tre{tc}")
        gtm = [ev_tre[0], ev_tre[1]]
        NE, NT = c.E, c.NTC
        g3 = self.g_tm[:].rearrange("p (t e) -> p t e", e=NE)
        mm1 = self.tks[:, 0:NT]
        mm2 = self.tks[:, NT:2 * NT]
        em2 = self.tks[:, 2 * NT:3 * NT]
        rden = self.tks[:, 3 * NT:4 * NT]
        d_, eq_, msk_, wun_ = (t[:] for t in self.tk[0:4])
        d3 = d_.rearrange("p (t e) -> p t e", e=NE)
        e1 = self.one("dve", lambda eng: eng.reduce_max(mm1, g3, axis=AX.X), gtm, tag="tk_m1")
        e2 = self.one("dve", lambda eng: eng.tensor_tensor(
            d3, g3, mm1.unsqueeze(2).broadcast_to([128, NT, NE]), ALU.subtract), [e1], tag="tk_d")
        e3 = self.one("dve", lambda eng: eng.tensor_single_scalar(eq_, d_, 0.0, ALU.is_equal), [e2], tag="tk_eq")
        e4 = self.one("dve", lambda eng: eng.scalar_tensor_tensor(
            msk_, eq_, -1e30, d_, ALU.mult, ALU.add), [e3], tag="tk_msk")
        e5 = self.one("dve", lambda eng: eng.reduce_max(
            mm2, msk_.rearrange("p (t e) -> p t e", e=NE), axis=AX.X), [e4], tag="tk_m2")
        e6 = self.one("act", lambda eng: eng.activation(eq_, d_, AF.Exp), [e2, e4], tag="tk_expd")
        expd = eq_
        e7 = self.one("dve", lambda eng: eng.tensor_tensor(
            msk_.rearrange("p (t e) -> p t e", e=NE), d3,
            mm2.unsqueeze(2).broadcast_to([128, NT, NE]), ALU.is_ge), [e5], tag="tk_ge")
        ge_ = msk_
        e8 = self.one("dve", lambda eng: eng.tensor_tensor(wun_, expd, ge_, ALU.mult), [e6, e7], tag="tk_wun")
        e9 = self.one("act", lambda eng: eng.activation(em2, mm2, AF.Exp), [e5], tag="tk_em2")
        e10 = self.one("act", lambda eng: eng.add(em2, em2, 1.0), [e9], tag="tk_den")
        e11 = self.one("dve", lambda eng: eng.reciprocal(rden, em2), [e10], tag="tk_rden")
        e12 = self.one("dve", lambda eng: eng.tensor_tensor(
            self.w_gates_tm[:].rearrange("p (t e) -> p t e", e=NE),
            wun_.rearrange("p (t e) -> p t e", e=NE),
            rden.unsqueeze(2).broadcast_to([128, NT, NE]), ALU.mult), [e8, e11], tag="tk_w")
        ev_tre = [None, None]
        for tc in range(c.NTC):
            par = tc % 2
            sl = 48 + par * 512
            evt = self.op("pe", (lambda tc, sl: (lambda eng: eng.transpose(
                self.Q[2][0:c.E, sl:sl + 128], self.w_gates_tm[:, tc * c.E:(tc + 1) * c.E],
                self.id_f[:])))(tc, sl), [e12, ev_tre[par]], [("pe", 1)], tag=f"w_tr{tc}")[0]
            ev_tre[par] = self.one("act", (lambda tc, sl: (lambda eng: eng.copy(
                self.w_fm[0:c.E, tc * 128:(tc + 1) * 128], self.Q[2][0:c.E, sl:sl + 128])))(tc, sl),
                [evt], tag=f"w_tre{tc}")
        eh1 = self.one("dve", lambda eng: eng.tensor_copy(self.whi_bf[:], self.w_gates_tm[:]),
                       [e12], tag="wsp1")
        eh2 = self.one("dve", lambda eng: eng.tensor_copy(self.sc_a[:], self.whi_bf[:]),
                       [eh1], tag="wsp2")
        eh3 = self.one("dve", lambda eng: eng.tensor_sub(self.sc_b[:], self.w_gates_tm[:], self.sc_a[:]),
                       [eh2], tag="wsp3")
        eh4 = self.one("dve", lambda eng: eng.tensor_copy(self.wlo_bf[:], self.sc_b[:]),
                       [eh3], tag="wsp4")
        v2 = self.w2x[:].rearrange("p (t e two) -> p t e two", e=c.E, two=2)
        eh5 = self.one("dve", lambda eng: eng.tensor_copy(
            v2[:, :, :, 0], self.whi_bf[:].rearrange("p (t e) -> p t e", e=c.E)), [eh4], tag="wsp5")
        eh6 = self.one("dve", lambda eng: eng.tensor_copy(
            v2[:, :, :, 1], self.wlo_bf[:].rearrange("p (t e) -> p t e", e=c.E)), [eh5], tag="wsp6")
        for tc in range(c.NTC):
            self.dma("sp", "ww", (lambda tc: (lambda eng: eng.dma_start(
                self.xrow_d[tc * 128:(tc + 1) * 128, c.D:c.D + 2 * c.E],
                self.w2x[:, tc * 2 * c.E:(tc + 1) * 2 * c.E])))(tc),
                [eh6], tag=f"wwr{tc}")
        self.p15_route(eh6)


    def mk_relay(self):
        c = self.cfg
        self.n_relay = c.E * c.NP1 + c.E * c.NDH * c.NP1

        def gen():
            for e in range(c.E):
                for hp in range(c.NP1):
                    yield ("w1", e, hp, 0)
                for dh in range(c.NDH):
                    for hp in range(c.NP1):
                        yield ("w2", e, hp, dh)
        self._relay_it = gen()

    def relay_drip(self, n):
        c = self.cfg
        for _ in range(n):
            try:
                kind, e, hp, dh = next(self._relay_it)
            except StopIteration:
                return
            if kind == "w1":
                self.dma("pool", "rl", (lambda e, hp: (lambda eng: eng.dma_start(
                    self.w1b_d[e, hp].rearrange("p (dc hcl f) -> p dc hcl f", dc=c.NDC, f=128),
                    self.w1_d[e, :, hp * 512:(hp + 1) * 512].rearrange(
                        "(dc p) (hcl f) -> p dc hcl f", p=128, f=128))))(e, hp),
                    [], tag=f"rl1_{e}_{hp}")
            else:
                self.dma("pool", "rl", (lambda e, hp, dh: (lambda eng: eng.dma_start(
                    self.w2b_d[e, dh, hp].rearrange("p (hcl f) -> p hcl f", f=c.DHW),
                    self.w2_d[e, hp * 512:(hp + 1) * 512,
                              dh * c.DHW:(dh + 1) * c.DHW].rearrange(
                        "(hcl p) f -> p hcl f", p=128))))(e, hp, dh),
                    [], tag=f"rl2_{e}_{dh}_{hp}")

    def p15_route(self, e12):
        c = self.cfg
        NE, NT = c.E, c.NTC
        BIG = 60000.0
        A3 = lambda t: t[:].rearrange("p (t e) -> p t e", e=NE)
        ev = e12

        def dve(emit, extra=(), tag=""):
            nonlocal ev
            ev = self.one("dve", emit, [ev] + list(extra), tag=tag)
            return ev
        dve(lambda eng: eng.tensor_single_scalar(self.sc_d[:], self.w_gates_tm[:], 0.0, ALU.is_gt), tag="rt_mk")
        dve(lambda eng: eng.tensor_copy(self.sc_a[:], self.sc_d[:]), tag="rt_cp")
        cur, nxt = self.sc_a, self.sc_b
        k = 1
        while k < NT:
            dve((lambda cur, nxt, k: (lambda eng: eng.tensor_copy(
                A3(nxt)[:, 0:k, :], A3(cur)[:, 0:k, :])))(cur, nxt, k), tag=f"rt_s{k}a")
            dve((lambda cur, nxt, k: (lambda eng: eng.tensor_tensor(
                A3(nxt)[:, k:NT, :], A3(cur)[:, k:NT, :], A3(cur)[:, 0:NT - k, :],
                ALU.add)))(cur, nxt, k), tag=f"rt_s{k}b")
            cur, nxt = nxt, cur
            k *= 2
        incl = cur
        dve((lambda incl: (lambda eng: eng.tensor_sub(self.sc_c[:], incl[:], self.sc_d[:])))(incl), tag="rt_ex")
        ept = self.op("pe", (lambda incl: (lambda eng: eng.transpose(
            self.Q[2][0:NE, 256:384], incl[:, (NT - 1) * NE:NT * NE], self.id_f[:])))(incl),
            [ev, ("act", self.p.cnt["act"])], [("pe", 1)], tag="rt_tr1")[0]
        ea = self.one("act", lambda eng: eng.copy(self.pp_a[0:NE, 0:128], self.Q[2][0:NE, 256:384]),
                      [ept], tag="rt_ev1")
        ev = ea
        cur, nxt = self.pp_a, self.pp_b
        k = 1
        while k < 128:
            dve((lambda cur, nxt, k: (lambda eng: eng.tensor_copy(
                nxt[0:NE, 0:k], cur[0:NE, 0:k])))(cur, nxt, k), tag=f"rt_p{k}a")
            dve((lambda cur, nxt, k: (lambda eng: eng.tensor_tensor(
                nxt[0:NE, k:128], cur[0:NE, k:128], cur[0:NE, 0:128 - k],
                ALU.add)))(cur, nxt, k), tag=f"rt_p{k}b")
            cur, nxt = nxt, cur
            k *= 2
        inclp, other = cur, nxt
        evm = self.one("pool", (lambda other: (lambda eng: eng.memset(other[0:NE, 0:1], 0.0)))(other),
                       [ev], tag="rt_ms")
        dve((lambda inclp, other: (lambda eng: eng.tensor_copy(
            other[0:NE, 1:128], inclp[0:NE, 0:127])))(inclp, other), [evm], tag="rt_sh")
        ep2 = self.op("pe", (lambda other: (lambda eng: eng.transpose(
            self.Q[2][0:128, 384:384 + NE], other[0:NE, 0:128], self.id_f[0:NE, 0:NE])))(other),
            [ev, ("act", self.p.cnt["act"])], [("pe", 1)], tag="rt_tr2")[0]
        ev = self.one("act", lambda eng: eng.copy(self.sc_sm[:, 0:NE], self.Q[2][0:128, 384:384 + NE]),
                      [ep2], tag="rt_ev2")
        dve(lambda eng: eng.tensor_tensor(
            A3(self.sc_b), A3(self.sc_c),
            self.sc_sm[:, 0:NE].unsqueeze(1).broadcast_to([128, NT, NE]), ALU.add), tag="rt_pt")
        dve(lambda eng: eng.tensor_single_scalar(self.sc_c[:], self.sc_b[:], float(c.CAP), ALU.is_lt), tag="rt_ft")
        dve(lambda eng: eng.tensor_tensor(self.sc_c[:], self.sc_c[:], self.sc_d[:], ALU.mult), tag="rt_vl")
        dve(lambda eng: eng.tensor_tensor(self.sc_b[:], self.sc_b[:], self.ecap_s[:], ALU.add), tag="rt_sg")
        dve(lambda eng: eng.reduce_max(self.sc_sm[:, 32:32 + NT], A3(self.w_gates_tm), axis=AX.X), tag="rt_rm")
        dve(lambda eng: eng.tensor_tensor(
            A3(self.sc_e), A3(self.w_gates_tm),
            self.sc_sm[:, 32:32 + NT].unsqueeze(2).broadcast_to([128, NT, NE]), ALU.is_ge), tag="rt_s1")
        dve(lambda eng: eng.tensor_sub(self.sc_f[:], self.sc_d[:], self.sc_e[:]), tag="rt_s2")
        for k2, sel in ((0, self.sc_e), (1, self.sc_f)):
            dve((lambda sel: (lambda eng: eng.tensor_tensor(sel[:], sel[:], self.sc_c[:], ALU.mult)))(sel),
                tag=f"rt_a{k2}")
            dve((lambda sel: (lambda eng: eng.tensor_tensor(self.sc_a[:], sel[:], self.sc_b[:], ALU.mult)))(sel),
                tag=f"rt_q{k2}")
            dve(lambda eng: eng.reduce_sum(self.sc_sm[:, 64:64 + NT], A3(self.sc_a), axis=AX.X), tag=f"rt_sa{k2}")
            dve((lambda sel: (lambda eng: eng.reduce_sum(
                self.sc_sm[:, 96:96 + NT], A3(sel), axis=AX.X)))(sel), tag=f"rt_n{k2}")
            dve(lambda eng: eng.tensor_scalar(
                self.sc_sm[:, 128:128 + NT], self.sc_sm[:, 96:96 + NT], -1.0, -BIG,
                ALU.add, ALU.mult), tag=f"rt_u{k2}")
            dve(lambda eng: eng.tensor_tensor(
                self.sc_sm[:, 64:64 + NT], self.sc_sm[:, 64:64 + NT],
                self.sc_sm[:, 128:128 + NT], ALU.add), tag=f"rt_ss{k2}")
            dve((lambda k2: (lambda eng: eng.tensor_copy(
                self.s12_i[:, k2 * NT:(k2 + 1) * NT], self.sc_sm[:, 64:64 + NT])))(k2), tag=f"rt_ci{k2}")
        ev_idx = ev
        def mk_reg(eng):
            self._bc_reg = eng.to_reg(c.E * c.CAP - 1)
            return eng.nop()
        self.op("pool", mk_reg, (), [("ms8", 1)], tag="rt_reg")
        wr_all = [(s, self.p.cnt[s]) for s in ("xw0", "xw1", "ww") if s in self.p.cnt]
        ev_rnd = [None, None]
        for tc in range(c.NTC):
            par = tc % 2
            exb = self.dma("sp", f"xb{par}", (lambda tc, par: (lambda eng: eng.dma_start(
                self.xrb[par][:], self.xrow_d[tc * 128:(tc + 1) * 128, :])))(tc, par),
                wr_all + [ev_rnd[par]], tag=f"xrl{tc}")
            for k2 in range(2):
                self.dma("pool", f"scp{par}", (lambda tc, par, k2: (lambda eng: eng.indirect_dma_start(
                    self.xg_d[:],
                    IndirectOffsetOnAxis(self.s12_i[:, k2 * NT + tc:k2 * NT + tc + 1], 0),
                    self.xrb[par][:], None,
                    bounds_check=self._bc_reg, oob_is_err=False)))(tc, par, k2),
                    [exb, ev_idx] if k2 == 0 else [], tag=f"sct{tc}_{k2}")
            ev_rnd[par] = (f"scp{par}", self.p.cnt[f"scp{par}"])

    def make_hgen(self):
        c = self.cfg
        nh = c.TT // 512
        self.ev_hwr_ev = {}
        self.h_pos = -1
        st = {
            "u": 0,
            "ev_gelu": [None, None],
            "ev_hwr_free": [None, None],
            "ev_w1_free": [None, None],
            "piece_loaded": {},
            "chunk_loaded": {},
            "ev_xga_free": [None, None],
            "q3_guard": [],
        }
        self.hst = st
        self.ev_wslf_ev = {}

        def load_piece(P):
            if P >= c.NSG * c.NP1 or P in st["piece_loaded"]:
                return
            sgi2, hp2 = divmod(P, c.NP1)
            e2 = sgi2 // c.NCH
            bi = P % 2
            ev = self.dma("sp", f"w1{bi}", (lambda e2, hp2, bi: (lambda eng: eng.dma_start(
                self.w1buf[bi][:],
                self.w1b_d[e2, hp2].rearrange(
                    "p (dc hcl f) -> p dc hcl f", dc=c.NDC, f=128))))(e2, hp2, bi),
                [st["ev_w1_free"][bi], ("rl", 16 * self.n_relay)], tag=f"w1d{sgi2}_{hp2}")
            st["piece_loaded"][P] = ev

        def load_chunk(sgi2):
            if sgi2 >= c.NSG or sgi2 in st["chunk_loaded"]:
                return
            bi = sgi2 % 2
            rb = sgi2 * c.TT
            sem = f"xg{bi}"
            for dc in range(c.NDC):
                self.dma("sp", sem, (lambda sgi2, dc, bi, rb: (lambda eng: eng.dma_start(
                    self.xga[bi][:, dc * c.TT:(dc + 1) * c.TT],
                    self.xg_d[rb:rb + c.TT, dc * 128:(dc + 1) * 128], transpose=True)))(sgi2, dc, bi, rb),
                    [st["ev_xga_free"][bi]] if dc == 0 else [], tag=f"xgl{sgi2}_{dc}")
            self.dma("sp", sem, (lambda sgi2, bi, rb: (lambda eng: eng.dma_start(
                self.wsl[bi][:].rearrange("p (a w) -> p a w", w=2 * c.E),
                self.xg_d[rb:rb + c.TT, c.D:c.D + 2 * c.E].rearrange(
                    "(a p) w -> p a w", p=128))))(sgi2, bi, rb),
                [self.ev_wslf_ev.get(sgi2 - 2)], tag=f"wsl{sgi2}")
            st["chunk_loaded"][sgi2] = (sem, self.p.cnt[sem])
        self.load_chunk = load_chunk

        def units():
            for sgi in range(c.NSG):
                for hp in range(c.NP1):
                    for hcl in range(4):
                        for half in range(nh):
                            yield sgi, hp, hcl, half
        it = units()

        def pull():
            try:
                sgi, hp, hcl, half = next(it)
            except StopIteration:
                return False
            e = sgi // c.NCH
            hc = hp * 4 + hcl
            P = sgi * c.NP1 + hp
            load_chunk(sgi)
            load_chunk(sgi + 1)
            load_piece(P)
            load_piece(P + 1)
            bi = P % 2
            cbi = sgi % 2
            u = st["u"]; st["u"] += 1
            qp = u % 2
            hQ = self.Q[3][:, qp * 512:qp * 512 + 512]
            for dc in range(c.NDC):
                wts = []
                if dc == 0:
                    wts = [st["piece_loaded"][P], st["chunk_loaded"][sgi],
                           st["ev_gelu"][qp]] + st["q3_guard"]
                    st["q3_guard"] = []
                self.op("pe", (lambda hcl, half, dc, bi, cbi, hQ: (lambda eng: eng.matmul(
                    hQ, self.w1buf[bi][:, dc, hcl, :],
                    self.xga[cbi][:, dc * c.TT + half * 512: dc * c.TT + (half + 1) * 512],
                    start=(dc == 0), stop=(dc == c.NDC - 1))))(hcl, half, dc, bi, cbi, hQ),
                    wts, [("pe", 1)] if dc == c.NDC - 1 else (),
                    tag=f"h_mm{sgi}_{hc}_{half}_{dc}")
            evh = ("pe", self.p.cnt["pe"])
            if hcl == 3 and half == nh - 1:
                st["ev_w1_free"][bi] = evh
            if hp == c.NP1 - 1 and hcl == 3 and half == nh - 1:
                st["ev_xga_free"][cbi] = evh
            wbi = hc % 2
            ege = self.one("act", (lambda e, hc, half, wbi, hQ: (lambda eng: eng.activation(
                self.h_wr[wbi][:, half * 512:(half + 1) * 512], hQ, AF.Gelu,
                bias=self.b1T[:, hc * c.E + e:hc * c.E + e + 1])))(e, hc, half, wbi, hQ),
                [evh, st["ev_hwr_free"][wbi] if half == 0 else None], tag=f"gelu{sgi}_{hc}_{half}")
            st["ev_gelu"][qp] = ege
            if half == nh - 1:
                self.dma("act", f"hw{wbi}", (lambda sgi, hc, wbi: (lambda eng: eng.dma_start(
                    self.h_d[sgi, hc * 128:(hc + 1) * 128, :], self.h_wr[wbi][:, 0:c.TT])))(sgi, hc, wbi),
                    [ege], tag=f"hwr{sgi}_{hc}")
                st["ev_hwr_free"][wbi] = (f"hw{wbi}", self.p.cnt[f"hw{wbi}"])
                self.ev_hwr_ev[(sgi, hc)] = st["ev_hwr_free"][wbi]
                if hc == c.HC - 1:
                    self.h_pos = sgi
            return True
        self.pull_h = pull

    def pull_h_n(self, n):
        for _ in range(n):
            if not self.pull_h():
                return

    def pull_h_until(self, gi):
        while self.h_pos < gi:
            if not self.pull_h():
                raise RuntimeError("h generator exhausted early")

    def p2_attn(self):
        c = self.cfg
        self.op("pool", lambda eng: eng.memset(self.v_tm[0][:], 1.0), (), [("ms2", 1)], tag="vt_ms0")
        self.op("pool", lambda eng: eng.memset(self.v_tm[1][:], 1.0), (), [("ms3", 1)], tag="vt_ms1")
        ev_vtm_war = [("ms2", 1), ("ms3", 1)]
        ev_wtr_ev = [None, None]
        ev_stage_free = [None, None]
        ev_exp_q = [None, None]
        ev_attsb = [None, None, None, None]
        ev_y_free = None
        ev_qk_last_ev = None
        ev_ystage = [None, None]
        wd = {0: self.wqb_d, 1: self.wkb_d, 2: self.wvb_d}
        for g in range(c.NDC):
            wT = {0: self.wqT_g, 1: self.wkT_g, 2: self.wvT_g}
            ev_wT = {}
            for j in range(3):
                par = j % 2
                evs = self.dma("act", f"ws{par}", (lambda g, j, par: (lambda eng: eng.dma_start(
                    self.w_stage[par][:], wd[j][g * 128:(g + 1) * 128, :])))(g, j, par),
                    [ev_stage_free[par], ("rlw", 64)], tag=f"wst{g}_{j}")
                pv = self.Q[2][:].bitcast(BF16)
                lastev = None
                for dc in range(c.NDC):
                    half = dc % 2
                    sl = half * 1024
                    wts = []
                    if dc == 0:
                        wts = [evs, ev_y_free]
                    wts.append(ev_wtr_ev[half])
                    self.op("pe", (lambda par, sl, dc, pv: (lambda eng: eng.transpose(
                        pv[:, sl:sl + 128], self.w_stage[par][:, dc * 128:(dc + 1) * 128],
                        self.id_bf[:])))(par, sl, dc, pv), wts, [("pe", 1)], tag=f"wtr{g}_{j}_{dc}")
                    evt = ("pe", self.p.cnt["pe"])
                    lastev = self.one("act", (lambda dc, sl, pv, wTj=wT[j]: (lambda eng: eng.copy(
                        wTj[:, dc * 128:(dc + 1) * 128], pv[:, sl:sl + 128])))(dc, sl, pv),
                        [evt], tag=f"wtre{g}_{j}_{dc}")
                    ev_wtr_ev[half] = lastev
                ev_wT[j] = lastev
                ev_stage_free[par] = lastev
            self.relay_drip(-(-self.n_relay // c.NDC))
            ev_vev = [None, None]
            for kc in range(c.S // 128):
                par = kc % 2
                vp = self.Q[par][:, 0:128]
                for dc in range(c.NDC):
                    wts = []
                    if dc == 0:
                        wts = [ev_wT[2], ev_vev[par], ev_y_free if kc == 0 else None]
                        if kc < 2:
                            wts.append(ev_exp_q[par])
                    self.op("pe", (lambda kc, dc, vp: (lambda eng: eng.matmul(
                        vp, self.xn[:, dc * c.S + kc * 128: dc * c.S + (kc + 1) * 128],
                        self.wvT_g[:, dc * 128:(dc + 1) * 128],
                        start=(dc == 0), stop=False)))(kc, dc, vp), wts, (), tag=f"v_mm{g}_{kc}_{dc}")
                self.op("pe", (lambda g, vp: (lambda eng: eng.matmul(
                    vp, self.ones_f[:], self.bv_row[:, g * 128:(g + 1) * 128],
                    start=False, stop=True)))(g, vp), (), [("pe", 1)], tag=f"v_bv{g}_{kc}")
                evt = ("pe", self.p.cnt["pe"])
                wv = [evt]
                if kc < 2:
                    wv += ev_vtm_war
                self.one("dve", (lambda kc, vp: (lambda eng: eng.tensor_copy(
                    self.v_tm[0][:, kc * 65:kc * 65 + 64], vp[:, 0:64])))(kc, vp), wv, tag=f"v_evA{g}_{kc}")
                ev_vev[par] = self.one("dve", (lambda kc, vp: (lambda eng: eng.tensor_copy(
                    self.v_tm[1][:, kc * 65:kc * 65 + 64], vp[:, 64:128])))(kc, vp), [evt], tag=f"v_evB{g}_{kc}")
            ev_vready = ev_vev[0] if (c.S // 128) % 2 == 1 else ev_vev[1]
            QK = {0: (self.wqT_g, self.Q_g, 0), 1: (self.wkT_g, self.K_g, c.NDC)}
            nsl = c.S // 512
            ev_qk_ev = ev_qk_last_ev
            for j in (0, 1):
                wTj, dstT, bcol = QK[j]
                for hb in range((nsl + 3) // 4):
                    sls = list(range(hb * 4, min(nsl, hb * 4 + 4)))
                    for dc in range(c.NDC):
                        for i, s in enumerate(sls):
                            pq = self.Q[i // 2][:, (i % 2) * 512:(i % 2) * 512 + 512]
                            wts = []
                            if dc == 0 and i == 0:
                                wts = [ev_wT[j], ev_qk_ev, ev_exp_q[0], ev_exp_q[1],
                                       ev_vev[0], ev_vev[1]]
                            self.op("pe", (lambda j, dc, s, pq, wTj: (lambda eng: eng.matmul(
                                pq, wTj[:, dc * 128:(dc + 1) * 128],
                                self.xn[:, dc * c.S + s * 512: dc * c.S + (s + 1) * 512],
                                start=(dc == 0), stop=(dc == c.NDC - 1))))(j, dc, s, pq, wTj),
                                wts, [("pe", 1)] if (dc == c.NDC - 1 and i == len(sls) - 1) else (),
                                tag=f"qk{g}_{j}_{hb}_{dc}_{s}")
                    evt = ("pe", self.p.cnt["pe"])
                    for i in range(0, len(sls), 2):
                        n2 = min(2, len(sls) - i)
                        pq = self.Q[i // 2][:, 0:512 * n2]
                        dst = dstT[:, sls[i] * 512: (sls[i] + n2) * 512]
                        ev_qk_ev = self.one("act", (lambda pq, dst, g, bcol: (lambda eng: eng.activation(
                            dst, pq, AF.Identity, bias=self.bqkT[:, bcol + g:bcol + g + 1])))(pq, dst, g, bcol),
                            [evt], tag=f"qk_ev{g}_{j}_{hb}_{i}")
            ev_qk = ev_qk_ev
            ev_qk_last_ev = ev_qk_ev
            qm = self.scr[:, 16:17]
            km = self.scr[:, 17:18]
            r1 = self.scr1[:, 0:1]
            r2 = self.scr1[:, 1:2]
            eq1 = self.one("dve", lambda eng: eng.tensor_reduce(
                qm, self.Q_g[:], AX.X, ALU.max, apply_absolute_value=True), [ev_qk], tag=f"b_q{g}")
            ek1 = self.one("dve", lambda eng: eng.tensor_reduce(
                km, self.K_g[:], AX.X, ALU.max, apply_absolute_value=True), [eq1], tag=f"b_k{g}")
            et1 = self.op("pe", lambda eng: eng.transpose(
                self.Q[2][0:1, 0:128], qm, self.id_f[:]),
                [eq1, ev_y_free], [("pe", 1)], tag=f"b_tq{g}")[0]
            er1 = self.one("dve", lambda eng: eng.reduce_max(r1, self.Q[2][0:1, 0:128], axis=AX.X),
                           [et1], tag=f"b_rq{g}")
            et2 = self.op("pe", lambda eng: eng.transpose(
                self.Q[2][0:1, 128:256], km, self.id_f[:]), [ek1, er1], [("pe", 1)], tag=f"b_tk{g}")[0]
            er2 = self.one("dve", lambda eng: eng.reduce_max(r2, self.Q[2][0:1, 128:256], axis=AX.X),
                           [et2], tag=f"b_rk{g}")
            es1 = self.one("act", lambda eng: eng.activation(r1, r1, AF.Square), [er1], tag=f"b_sq{g}")
            es2 = self.one("act", lambda eng: eng.activation(r2, r2, AF.Square), [er2], tag=f"b_sk{g}")
            ea = self.one("dve", lambda eng: eng.tensor_add(r1, r1, r2), [es1, es2], tag=f"b_add{g}")
            en = self.one("act", lambda eng: eng.mul(r1, r1, -1.0), [ea], tag=f"b_neg{g}")
            eb = self.op("pe", lambda eng: eng.matmul(
                self.Q[2][0:128, 256:257], self.ones_f[:], r1[:, 0:1], start=True, stop=True),
                [en], [("pe", 1)], tag=f"b_bc{g}")[0]
            ebb = self.one("act", (lambda g: (lambda eng: eng.copy(
                self.negb[:, g:g + 1], self.Q[2][0:128, 256:257])))(g), [eb], tag=f"b_cp{g}")
            last_mul = ev_y_free
            for qt in range(c.NQT):
                kcm = 4 * (qt + 1)
                pend = []

                def emit_av(kc, ap, ee, kcm=None):
                    avw = [ee]
                    if kc == 0:
                        avw += [ev_vready, last_mul]
                    self.op("pe", (lambda kc, ap, kcm: (lambda eng: eng.matmul(
                        self.Q[2][0:65, 0:512], self.v_tm[0][:, kc * 65:(kc + 1) * 65],
                        self.att_sb[ap][:, 0:512], start=(kc == 0), stop=(kc == kcm - 1))))(kc, ap, kcm),
                        avw, (), tag=f"avA{g}_{kc}")
                    self.op("pe", (lambda kc, ap, kcm: (lambda eng: eng.matmul(
                        self.Q[2][0:65, 512:1024], self.v_tm[1][:, kc * 65:(kc + 1) * 65],
                        self.att_sb[ap][:, 512:1024], start=(kc == 0), stop=(kc == kcm - 1))))(kc, ap, kcm),
                        (), [("pe", 1)], tag=f"avB{g}_{kc}")
                    ev_attsb[ap] = ("pe", self.p.cnt["pe"])

                for kc in range(kcm):
                    kp = kc % 2
                    ap = kc % 4
                    sp_ = self.Q[kp]
                    w0 = [ev_exp_q[kp]]
                    if qt == 0 and kc == 0:
                        w0 += [ev_qk, ebb]
                    if qt > 0 and kc == 0:
                        w0.append(ev_epi_q0)
                    self.op("pe", (lambda qt, kc, sp_: (lambda eng: eng.matmul(
                        sp_[:, 0:512], self.K_g[0:64, kc * 128:(kc + 1) * 128],
                        self.Q_g[0:64, qt * 512:(qt + 1) * 512], start=True, stop=True,
                        tile_position=(0, 0))))(qt, kc, sp_), w0, (), tag=f"scA{g}_{qt}_{kc}")
                    self.op("pe", (lambda qt, kc, sp_: (lambda eng: eng.matmul(
                        sp_[:, 512:1024], self.K_g[64:128, kc * 128:(kc + 1) * 128],
                        self.Q_g[64:128, qt * 512:(qt + 1) * 512], start=True, stop=True,
                        tile_position=(64, 0))))(qt, kc, sp_), (), [("pe", 1)], tag=f"scB{g}_{qt}_{kc}")
                    ev_sc = ("pe", self.p.cnt["pe"])
                    wts = [ev_sc, ev_attsb[ap]]
                    if qt == 0 and kc == 0:
                        wts.append(ebb)
                    ee = self.one("act", (lambda g, ap, sp_: (lambda eng: eng.activation(
                        self.att_sb[ap][:], sp_[:], AF.Exp, bias=self.negb[:, g:g + 1],
                        scale=self.cfg.scale)))(g, ap, sp_), wts, tag=f"exp{g}_{qt}_{kc}")
                    ev_exp_q[kp] = ee
                    dnum = kc - 4 * qt
                    if dnum >= 0:
                        ee = self.one("dve", (lambda ap, dnum: (lambda eng: eng.tensor_tensor(
                            self.att_sb[ap][:].rearrange("p (h q) -> p h q", h=2),
                            self.att_sb[ap][:].rearrange("p (h q) -> p h q", h=2),
                            self.masks[:, dnum * 512:(dnum + 1) * 512].unsqueeze(1)
                                .broadcast_to([128, 2, 512]), ALU.mult)))(ap, dnum),
                            [ee], tag=f"msk{g}_{qt}_{kc}")
                    pend.append((kc, ap, ee))
                    if len(pend) > 2:
                        emit_av(*pend.pop(0), kcm=kcm)
                while pend:
                    emit_av(*pend.pop(0), kcm=kcm)
                ev_avB = ("pe", self.p.cnt["pe"])
                bslot = self.Q[0][0:64, 0:512]
                erA = self.one("act", lambda eng: eng.activation(
                    self.sumrow[:], self.Q[2][64:65, 0:512], AF.Ln), [ev_avB, last_mul], tag=f"nrA{g}_{qt}")
                ebcA = self.op("pe", lambda eng: eng.matmul(
                    bslot, self.ones_f[:, 0:64], self.sumrow[:], start=True, stop=True),
                    [erA, ev_exp_q[0]], [("pe", 1)], tag=f"nbA{g}_{qt}")[0]
                ecA = self.one("act", lambda eng: eng.activation(
                    self.recip64[:], bslot, AF.Exp, scale=-1.0), [ebcA], tag=f"ncA{g}_{qt}")
                mulA = self.one("dve", lambda eng: eng.tensor_mul(
                    self.y_stage[0][:], self.Q[2][0:64, 0:512], self.recip64[:]),
                    [ecA, ev_ystage[0]], tag=f"nmA{g}_{qt}")
                self.dma("sp", "yw0", (lambda g, qt: (lambda eng: eng.dma_start(
                    self.y_d_scr[0:64, g * c.S + qt * 512:g * c.S + (qt + 1) * 512],
                    self.y_stage[0][:])))(g, qt), [mulA], tag=f"ywA{g}_{qt}")
                ev_ystage[0] = ("yw0", self.p.cnt["yw0"])
                erB = self.one("act", lambda eng: eng.activation(
                    self.sumrow2[:], self.Q[2][64:65, 512:1024], AF.Ln), [ev_avB], tag=f"nrB{g}_{qt}")
                ebcB = self.op("pe", lambda eng: eng.matmul(
                    bslot, self.ones_f[:, 0:64], self.sumrow2[:], start=True, stop=True),
                    [erB, ecA], [("pe", 1)], tag=f"nbB{g}_{qt}")[0]
                ecB = self.one("act", lambda eng: eng.activation(
                    self.recip64[:], bslot, AF.Exp, scale=-1.0), [ebcB, mulA], tag=f"ncB{g}_{qt}")
                last_mul = self.one("dve", lambda eng: eng.tensor_mul(
                    self.y_stage[1][:], self.Q[2][0:64, 512:1024], self.recip64[:]),
                    [ecB, ev_ystage[1]], tag=f"nmB{g}_{qt}")
                self.dma("sp", "yw1", (lambda g, qt: (lambda eng: eng.dma_start(
                    self.y_d_scr[64:128, g * c.S + qt * 512:g * c.S + (qt + 1) * 512],
                    self.y_stage[1][:])))(g, qt), [last_mul], tag=f"ywB{g}_{qt}")
                ev_ystage[1] = ("yw1", self.p.cnt["yw1"])
                ev_epi_q0 = ecB
                ev_y_free = last_mul
            ev_vtm_war = [ev_attsb[0], ev_attsb[1]]

    def p3_proj(self):
        c = self.cfg
        ev_wtr_ev = [None, None]
        ev_stage_free = [None, None]
        for dcH in range(c.NDC):
            par = dcH % 2
            evs = self.dma("act", f"ws{par}", (lambda dcH, par: (lambda eng: eng.dma_start(
                self.w_stage[par][:], self.wpb_d[dcH * 128:(dcH + 1) * 128, :])))(dcH, par),
                [ev_stage_free[par], ("rlw", 64)], tag=f"wp_st{dcH}")
            self.pull_h_n(1)
            pv = self.Q[2][:].bitcast(BF16)
            lastev = None
            for fc in range(c.NDC):
                half = fc % 2
                sl = half * 1024
                wts = [ev_wtr_ev[half]]
                if fc == 0:
                    wts.append(evs)
                self.op("pe", (lambda par, fc, sl, pv: (lambda eng: eng.transpose(
                    pv[:, sl:sl + 128], self.w_stage[par][:, fc * 128:(fc + 1) * 128],
                    self.id_bf[:])))(par, fc, sl, pv), wts, [("pe", 1)], tag=f"wp_tr{dcH}_{fc}")
                evt = ("pe", self.p.cnt["pe"])
                lastev = self.one("act", (lambda dcH, fc, sl, pv: (lambda eng: eng.copy(
                    self.wpT[:, fc * c.D + dcH * 128:fc * c.D + (dcH + 1) * 128],
                    pv[:, sl:sl + 128])))(dcH, fc, sl, pv), [evt], tag=f"wp_tre{dcH}_{fc}")
                ev_wtr_ev[half] = lastev
            ev_stage_free[par] = lastev
        ev_wp = lastev
        ev_ev = [None, None]
        ev_ytc = [None, None]
        ev_xp_free = [None, None]
        self.ev_p3_xrd = [None, None]
        yv = self.y_d_scr[:].rearrange("p (fc s) -> p fc s", fc=c.NDC)
        for tc in range(c.NTC):
            par = tc % 2
            prev_aw = ev_ev[par]
            pp = self.Q[par]
            if tc % 2 == 0:
                self.pull_h_n(1)
            eyt = self.dma("sp", f"yt{par}", (lambda tc, par: (lambda eng: eng.dma_start(
                self.y_tc[par][:].rearrange("p (fc t) -> p fc t", fc=c.NDC),
                yv[:, :, tc * 128:(tc + 1) * 128])))(tc, par), [ev_ytc[par]], tag=f"ytd{tc}")
            for dh in range(c.NDH):
                po = pp[:, dh * 512:dh * 512 + c.DHW]
                for fc in range(c.NDC):
                    wts = []
                    if fc == 0 and dh == 0:
                        wts = [ev_wp, prev_aw, eyt]
                    self.op("pe", (lambda tc, dh, fc, po, par: (lambda eng: eng.matmul(
                        po, self.y_tc[par][:, fc * 128:(fc + 1) * 128],
                        self.wpT[:, fc * c.D + dh * c.DHW:fc * c.D + (dh + 1) * c.DHW],
                        start=(fc == 0), stop=False)))(tc, dh, fc, po, par), wts, (), tag=f"pj{tc}_{dh}_{fc}")
                self.op("pe", (lambda dh, po: (lambda eng: eng.matmul(
                    po, self.ones_f[:], self.bp_row[:, dh * c.DHW:(dh + 1) * c.DHW],
                    start=False, stop=False)))(dh, po), (), (), tag=f"pj_bp{tc}_{dh}")
                self.op("pe", (lambda tc, dh, po: (lambda eng: eng.matmul(
                    po, self.w_fm[:, tc * 128:(tc + 1) * 128],
                    self.b2_bf[:, dh * c.DHW:(dh + 1) * c.DHW],
                    start=False, stop=True)))(tc, dh, po), (),
                    [("pe", 1)] if dh == c.NDH - 1 else (), tag=f"pj_b2{tc}_{dh}")
            evt = ("pe", self.p.cnt["pe"])
            ev_ytc[par] = evt
            exl = self.dma("sp", f"xp{par}", (lambda tc, par: (lambda eng: eng.dma_start(
                self.x_rd[par][:], self.x_d[tc * 128:(tc + 1) * 128, :])))(tc, par),
                [ev_xp_free[par]], tag=f"pj_x{tc}")
            eev = self.one("dve", (lambda par, pp: (lambda eng: eng.tensor_add(
                self.attn_stage[par][:], pp[:, 0:c.D], self.x_rd[par][:])))(par, pp),
                [evt, prev_aw, exl], tag=f"pj_ev{tc}")
            ev_xp_free[par] = eev
            self.ev_p3_xrd[par] = eev
            self.dma("act", f"aw{par}", (lambda tc, par: (lambda eng: eng.dma_start(
                self.attn_d[tc * 128:(tc + 1) * 128, :], self.attn_stage[par][:])))(tc, par),
                [eev], tag=f"pj_wr{tc}")
            ev_ev[par] = (f"aw{par}", self.p.cnt[f"aw{par}"])

    def p4_ffn(self):
        c = self.cfg
        ev_w2_free = [None, None]
        ev_hrd_free = [None, None, None, None]
        ev_psum_tc = [None] * c.NTCT
        ev_stop_tc = [None] * c.NTCT
        ev_ostage = [None, None, None, None]
        self.ow_marks = []
        hrd_ix = 0
        w2_ix = 0
        for sgi in range(c.NSG):
            e = sgi // c.NCH
            q3g = [ev_psum_tc[6], ev_psum_tc[7]] if c.NTCT > 6 else []
            self.hst["q3_guard"] = [x for x in q3g if x is not None]
            cbi = sgi % 2
            self.load_chunk(sgi)
            wv3 = self.wsl[cbi][:].rearrange("p (a w) -> p a w", w=2 * c.E)
            ev_wslf = self.one("dve", (lambda e, cbi, wv3: (lambda eng: eng.tensor_add(
                self.wslf[cbi][:], wv3[:, :, 2 * e:2 * e + 1].squeeze(2),
                wv3[:, :, 2 * e + 1:2 * e + 2].squeeze(2))))(e, cbi, wv3),
                [self.hst["chunk_loaded"][sgi]], tag=f"wslf{sgi}")
            self.ev_wslf_ev[sgi] = ev_wslf
            self.pull_h_until(min(sgi + 1, c.NSG - 1))
            ge_last = [self.hst["ev_gelu"][0], self.hst["ev_gelu"][1]]
            for dh in range(c.NDH):
                for hp2 in range(c.NP1):
                    bi = w2_ix % 2
                    self.dma("act", f"w2{bi}", (lambda e, hp2, dh, bi: (lambda eng: eng.dma_start(
                        self.w2buf[bi][:],
                        self.w2b_d[e, dh, hp2].rearrange(
                            "p (hcl f) -> p hcl f", f=c.DHW))))(e, hp2, dh, bi),
                        [ev_w2_free[bi], ("rl", 16 * self.n_relay)], tag=f"w2d{sgi}_{dh}_{hp2}")
                    evw2 = (f"w2{bi}", self.p.cnt[f"w2{bi}"])
                    w2_ix += 1
                    for hcl in range(4):
                        hc = hp2 * 4 + hcl
                        rbi = hrd_ix % 4
                        self.dma("sp", f"hr{rbi}", (lambda sgi, hc, rbi: (lambda eng: eng.dma_start(
                            self.h_rd[rbi][:], self.h_d[sgi, hc * 128:(hc + 1) * 128, :])))(sgi, hc, rbi),
                            [self.ev_hwr_ev[(sgi, hc)], ev_hrd_free[rbi]], tag=f"hrd{sgi}_{dh}_{hc}")
                        evhr = (f"hr{rbi}", self.p.cnt[f"hr{rbi}"])
                        hrd_ix += 1
                        for tc in range(c.NTCT):
                            op = self.Q[tc // 2][:, (tc % 2) * 512:(tc % 2) * 512 + c.DHW]
                            wts = []
                            if tc == 0:
                                wts = [evw2, evhr]
                            if hc == 0:
                                wts.append(ev_psum_tc[tc])
                                if tc >= 6:
                                    wts.append(ge_last[tc - 6])
                            do_inc = (tc == c.NTCT - 1) or (hc == c.HC - 1)
                            self.op("pe", (lambda hcl, tc, rbi, bi, op, hc: (lambda eng: eng.matmul(
                                op, self.h_rd[rbi][:, tc * 128:(tc + 1) * 128],
                                self.w2buf[bi][:, hcl, :],
                                start=(hc == 0), stop=(hc == c.HC - 1))))(hcl, tc, rbi, bi, op, hc),
                                wts,
                                [("pe", 1)] if do_inc else (),
                                tag=f"o_mm{sgi}_{dh}_{hc}_{tc}")
                            if hc == c.HC - 1:
                                ev_stop_tc[tc] = ("pe", self.p.cnt["pe"])
                        evo = ("pe", self.p.cnt["pe"])
                        if hcl == 3:
                            ev_w2_free[bi] = evo
                        ev_hrd_free[rbi] = evo
                order = [6, 7, 0, 1, 2, 3, 4, 5] if c.NTCT == 8 else list(range(c.NTCT))
                for k, tc in enumerate(order):
                    op = self.Q[tc // 2][:, (tc % 2) * 512:(tc % 2) * 512 + c.DHW]
                    o = k % 4
                    wcol = self.wslf[cbi][:, tc:tc + 1]
                    ev = self.one("dve", (lambda op, wcol, o: (lambda eng: eng.tensor_scalar(
                        self.out_stage[o][:], op, wcol, None, ALU.mult)))(op, wcol, o),
                        [ev_stop_tc[tc], ev_ostage[o], ev_wslf], tag=f"stt{sgi}_{dh}_{tc}")
                    ev_psum_tc[tc] = ev
                    self.dma("act", f"ow{o}", (lambda sgi, dh, tc, o: (lambda eng: eng.dma_start(
                        self.outsl_d[sgi * c.TT + tc * 128: sgi * c.TT + (tc + 1) * 128,
                                     dh * c.DHW:(dh + 1) * c.DHW],
                        self.out_stage[o][:])))(sgi, dh, tc, o), [ev], tag=f"osw{sgi}_{dh}_{tc}")
                    ev_ostage[o] = (f"ow{o}", self.p.cnt[f"ow{o}"])
            if (sgi + 1) % c.NCH == 0:
                self.ow_marks.append([(f"ow{o}", self.p.cnt[f"ow{o}"])
                                      for o in range(4) if f"ow{o}" in self.p.cnt])
        ow_all = [(s, self.p.cnt[s]) for s in ("ow0", "ow1", "ow2", "ow3") if s in self.p.cnt]
        aw_all = [(s, self.p.cnt[s]) for s in ("aw0", "aw1") if s in self.p.cnt]
        lanes = [self.attn_rd[0], self.attn_rd[1], self.x_rd[0], self.x_rd[1]]
        ev_lane = [None, None, self.ev_p3_xrd[0], self.ev_p3_xrd[1]]
        for tc in range(c.NTC):
            ln = tc % 4
            buf = lanes[ln]
            ea = self.dma("sp", f"al{ln}", (lambda tc, buf: (lambda eng: eng.dma_start(
                buf[:], self.attn_d[tc * 128:(tc + 1) * 128, :])))(tc, buf),
                aw_all + [ev_lane[ln]], tag=f"fa{tc}")
            evg = None
            for k2 in range(2):
                self.dma("pool", f"gp{ln}", (lambda tc, buf, k2: (lambda eng: eng.indirect_dma_start(
                    buf[:], None,
                    self.outsl_d[:],
                    IndirectOffsetOnAxis(self.s12_i[:, k2 * c.NTC + tc: k2 * c.NTC + tc + 1], 0),
                    bounds_check=self._bc_reg, oob_is_err=False,
                    compute_op=ALU.add)))(tc, buf, k2),
                    (ow_all + [ea]) if k2 == 0 else [evg], tag=f"gad{tc}_{k2}")
                evg = (f"gp{ln}", self.p.cnt[f"gp{ln}"])
            self.dma("sp", f"os{ln}", (lambda tc, buf: (lambda eng: eng.dma_start(
                self.y_d[tc * 128:(tc + 1) * 128, :], buf[:])))(tc, buf),
                [evg], tag=f"fout{tc}")
            ev_lane[ln] = (f"os{ln}", self.p.cnt[f"os{ln}"])

    def emit(self):
        nc = self.nc
        semh = self.semh = {s: nc.alloc_semaphore(s) for s in self.p.cnt}
        waited = {e: {} for e in ENGS}

        def run(eng_obj, ename):
            seen = waited[ename]
            for waits, emitfn, incs, tag in self.p.q[ename]:
                for s, v in waits:
                    if seen.get(s, -1) >= v:
                        continue
                    eng_obj.wait_ge(semh[s], v)
                    seen[s] = v
                inst = emitfn(eng_obj)
                for s, n in incs:
                    inst.then_inc(semh[s], n)

        with nc.Block() as blk:
            blk.tensor(lambda eng: run(eng, "pe"))
            blk.scalar(lambda eng: run(eng, "act"))
            blk.vector(lambda eng: run(eng, "dve"))
            blk.gpsimd(lambda eng: run(eng, "pool"))
            blk.sync(lambda eng: run(eng, "sp"))


def build_program(cfg: Cfg):
    return Builder(cfg).build()


_IN_KEYS = {
    "x": "L0_attn_x", "wq": "L0_attn_Wq", "bq": "L0_attn_bq", "wk": "L0_attn_Wk",
    "bk": "L0_attn_bk", "wv": "L0_attn_Wv", "bv": "L0_attn_bv", "wp": "L0_attn_Wp",
    "bp": "L0_attn_bp", "gw": "L0_ffn_gate_w", "gb": "L0_ffn_gate_b",
    "w1": "L0_ffn_w1", "b1": "L0_ffn_b1", "w2": "L0_ffn_w2", "b2": "L0_ffn_b2",
}


def kernel(**inputs):
    cfg = Cfg()
    n = 8
    x = np.ascontiguousarray(np.asarray(inputs["L0_attn_x"], dtype=np.float32))
    common = {k: np.ascontiguousarray(np.asarray(inputs[v], dtype=np.float32))
              for k, v in _IN_KEYS.items() if k != "x"}
    nc = build_program(cfg)
    in_maps = []
    for ci in range(n):
        m = dict(common)
        m["x"] = np.ascontiguousarray(x[ci])
        in_maps.append(m)
    res = run_bass_kernel_spmd(nc, in_maps, core_ids=list(range(n)))
    out = np.stack([res.results[ci]["y"] for ci in range(n)], axis=0)
    return out.astype(np.float32)


# version 16
# speedup vs baseline: 1.1556x; 1.0447x over previous
import math
from dataclasses import dataclass

import numpy as np
import ml_dtypes

import concourse.bass as bass
import concourse.mybir as mybir
from concourse.bass_utils import run_bass_kernel_spmd
from concourse.bass_core_rust import IndirectOffsetOnAxis

F32 = mybir.dt.float32
BF16 = mybir.dt.bfloat16
I32 = mybir.dt.int32
AF = mybir.ActivationFunctionType
ALU = mybir.AluOpType
AX = mybir.AxisListType

ENGS = ("pe", "act", "dve", "pool", "sp")
DMA_SEMS = ("w1", "w2", "wst", "hwr", "hrd", "xld", "out", "cst", "cstp", "awr")


@dataclass
class Cfg:
    D: int = 1024
    S: int = 4096
    E: int = 8
    HF: int = 4096
    TT: int = 1024
    QT: int = 512
    CAP: int = 3072

    @property
    def NDC(self): return self.D // 128
    @property
    def NTC(self): return self.S // 128
    @property
    def NQT(self): return self.S // self.QT
    @property
    def HC(self): return self.HF // 128
    @property
    def NP1(self): return self.HF // 512
    @property
    def NDH(self): return max(1, self.D // 512)
    @property
    def DHW(self): return min(512, self.D)
    @property
    def NTCT(self): return self.TT // 128
    @property
    def NTT(self): return self.S // self.TT
    @property
    def NCH(self): return self.CAP // self.TT
    @property
    def NSG(self): return self.E * self.NCH
    @property
    def RW(self): return self.D + 2 * self.E
    @property
    def scale(self): return 1.0 / math.sqrt(self.D)


class Planner:
    def __init__(self):
        self.q = {e: [] for e in ENGS}
        self.cnt = {}
        self.last_event = {e: None for e in ENGS}

    def sem(self, name):
        self.cnt[name] = 0

    def __post_init_events(self):
        pass

    def op(self, eng, emit, waits=(), incs=(), tag=""):
        w = []
        for x in waits:
            if x is None:
                continue
            s, v = x
            assert s in self.cnt, s
            w.append((s, int(v)))
        out = []
        for s, n in incs:
            if s not in self.cnt:
                self.cnt[s] = 0
            self.cnt[s] += n
            out.append((s, self.cnt[s]))
        if out:
            self.last_event[eng] = out[0]
        self.q[eng].append([tuple(w), emit, tuple(incs), tag])
        return out

    def inc_last(self, eng, sem):
        assert self.q[eng], eng
        last = self.q[eng][-1]
        assert not last[2], last
        if sem not in self.cnt:
            self.cnt[sem] = 0
        self.cnt[sem] += 1
        last[2] = ((sem, 1),)
        self.last_event[eng] = (sem, self.cnt[sem])
        return 1

    def check(self):
        for e in ENGS:
            for waits, _, _, tag in self.q[e]:
                for s, v in waits:
                    assert v <= self.cnt[s], \
                        f"unreachable wait {s}>={v} (final {self.cnt[s]}) at {e}:{tag}"
        cur = {s: 0 for s in self.cnt}
        ptr = {e: 0 for e in ENGS}
        total = sum(len(self.q[e]) for e in ENGS)
        done = 0
        while done < total:
            progressed = False
            for e in ENGS:
                while ptr[e] < len(self.q[e]):
                    waits, _, incs, tag = self.q[e][ptr[e]]
                    if all(cur[s] >= v for s, v in waits):
                        for s, n in incs:
                            cur[s] += n
                        ptr[e] += 1
                        done += 1
                        progressed = True
                    else:
                        break
            if not progressed:
                details = []
                for e in ENGS:
                    if ptr[e] < len(self.q[e]):
                        waits, _, _, tag = self.q[e][ptr[e]]
                        blocked = [(s, v, cur[s]) for s, v in waits if cur[s] < v]
                        details.append(f"{e}@{ptr[e]} tag={tag} blocked_on={blocked}")
                raise RuntimeError("DEADLOCK in plan:\n" + "\n".join(details))


class Builder:
    def __init__(self, cfg: Cfg):
        self.cfg = cfg
        nc = bass.Bass("TRN2", target_bir_lowering=False)
        self.nc = nc
        self.p = Planner()
        for s in ENGS:
            self.p.sem(s)

        c = cfg
        dt_ = nc.dram_tensor
        self.x_d = dt_("x", [c.S, c.D], F32, kind="ExternalInput")
        self.wq_d = dt_("wq", [c.D, c.D], F32, kind="ExternalInput")
        self.wk_d = dt_("wk", [c.D, c.D], F32, kind="ExternalInput")
        self.wv_d = dt_("wv", [c.D, c.D], F32, kind="ExternalInput")
        self.wp_d = dt_("wp", [c.D, c.D], F32, kind="ExternalInput")
        self.bq_d = dt_("bq", [c.D], F32, kind="ExternalInput")
        self.bk_d = dt_("bk", [c.D], F32, kind="ExternalInput")
        self.bv_d = dt_("bv", [c.D], F32, kind="ExternalInput")
        self.bp_d = dt_("bp", [c.D], F32, kind="ExternalInput")
        self.gw_d = dt_("gw", [c.E, c.D], F32, kind="ExternalInput")
        self.gb_d = dt_("gb", [c.E], F32, kind="ExternalInput")
        self.w1_d = dt_("w1", [c.E, c.D, c.HF], F32, kind="ExternalInput")
        self.b1_d = dt_("b1", [c.E, c.HF], F32, kind="ExternalInput")
        self.w2_d = dt_("w2", [c.E, c.HF, c.D], F32, kind="ExternalInput")
        self.b2_d = dt_("b2", [c.E, c.D], F32, kind="ExternalInput")
        self.y_d = dt_("y", [c.S, c.D], F32, kind="ExternalOutput")
        self.attn_d = dt_("attn_scr", [c.S, c.D], F32, kind="Internal")
        self.y_d_scr = dt_("y_scr", [128, c.NDC * c.S], BF16, kind="Internal")
        self.h_d = dt_("h_scr", [c.NSG, c.HF, c.TT], BF16, kind="Internal")
        self.xrow_d = dt_("xrow_scr", [c.S, c.RW], BF16, kind="Internal")
        self.xg_d = dt_("xg_scr", [c.E * c.CAP, c.RW], BF16, kind="Internal")
        self.outsl_d = dt_("outsl_scr", [c.E * c.CAP, c.D], F32, kind="Internal")
        self.w1b_d = dt_("w1b_scr", [c.E, c.NP1, 128, c.NDC * 4 * 128], BF16, kind="Internal")
        self.wqb_d = dt_("wqb_scr", [c.D, c.D], BF16, kind="Internal")
        self.wkb_d = dt_("wkb_scr", [c.D, c.D], BF16, kind="Internal")
        self.wvb_d = dt_("wvb_scr", [c.D, c.D], BF16, kind="Internal")
        self.wpb_d = dt_("wpb_scr", [c.D, c.D], BF16, kind="Internal")
        self.w2b_d = dt_("w2b_scr", [c.E, c.NDH, c.NP1, 128, 4 * c.DHW], BF16, kind="Internal")

        self.id_bf_d = nc.inline_tensor(np.eye(128, dtype=ml_dtypes.bfloat16), "id_bf_c")
        self.id_f_d = nc.inline_tensor(np.eye(128, dtype=np.float32), "id_f_c")
        self.ones_f_d = nc.inline_tensor(np.ones((1, 128), dtype=np.float32), "ones_f_c")
        m = np.zeros((128, 4, 512), dtype=np.float32)
        for d in range(4):
            for kp in range(128):
                qf0 = kp + 128 * d
                if qf0 < 512:
                    m[kp, d, qf0:] = 1.0
        self.mask_d = nc.inline_tensor(m.reshape(128, 2048).astype(ml_dtypes.bfloat16), "mask_c")
        ecap = np.zeros((128, c.NTC * c.E), dtype=np.float32)
        for e in range(c.E):
            ecap[:, e::c.E] = e * c.CAP
        self.ecap_d = nc.inline_tensor(ecap, "ecap_c")

        ARENA = 208960
        BASE = 16512
        ah = nc.alloc_sbuf_tensor("arena", [128, ARENA], mybir.dt.uint8)
        assert nc.lookup_mloc(ah).addr == BASE, nc.lookup_mloc(ah).addr
        self.ARENA = ARENA
        A_raw = nc.alloc_sbuf_tensor_at

        def A(name, shape, dtype, offset):
            return A_raw(name, shape, dtype, offset=offset + BASE)
        self._off = 0

        def at(name, shape, dtype):
            self._off = (self._off + 63) // 64 * 64
            t = A(name, shape, dtype, offset=self._off)
            self._off += int(np.prod(shape[1:])) * mybir.dt.size(dtype)
            return t

        self.id_bf = at("id_bf_s", [128, 128], BF16)
        self.id_f = at("id_f_s", [128, 128], F32)
        self.ones_f = at("ones_f_s", [1, 128], F32)
        self.masks = at("masks_s", [128, 2048], BF16)
        self.bqkT = at("bqkT", [128, 2 * c.NDC], F32)
        self.b1T = at("b1T", [128, c.HC * c.E], F32)
        self.b2_bf = at("b2_bf", [c.E, c.D], BF16)
        self.bp_row = at("bp_row", [1, c.D], F32)
        self.bv_row = at("bv_row", [1, c.D], F32)
        self.gb_s = at("gb_s", [c.E, 1], F32)
        self.gate_wT = at("gate_wT", [128, c.NDC * c.E], BF16)
        self.w_gates_tm = at("w_gates_tm", [128, c.NTC * c.E], F32)
        self.w_fm = at("w_fm", [c.E, c.S], BF16)
        self.negb = at("negb", [128, c.NDC], F32)
        self.scr = at("scr", [128, 64], F32)
        self.consts = at("consts", [128, 2], F32)
        self.scr1 = at("scr1", [1, 64], F32)
        self.recip64 = at("recip64", [64, 512], F32)
        self.sumrow = at("sumrow", [1, 512], F32)
        self.sumrow2 = at("sumrow2", [1, 512], F32)

        A0 = self._off = (self._off + 63) // 64 * 64
        self.xn = A("xn", [128, c.NDC * c.S], BF16, offset=A0)
        o = A0
        def atA(name, shape, dtype):
            nonlocal o
            o = (o + 63) // 64 * 64
            t = A(name, shape, dtype, offset=o)
            o += int(np.prod(shape[1:])) * mybir.dt.size(dtype)
            return t
        self.xga = [atA(f"xga{i}", [128, c.NDC * c.TT], BF16) for i in range(2)]
        self.w1buf = [atA(f"w1buf{i}", [128, c.NDC, 4, 128], BF16) for i in range(2)]
        self.h_rd = [atA(f"h_rd{i}", [128, c.TT], BF16) for i in range(4)]
        self.h_wr = [atA(f"h_wr{i}", [128, c.TT], BF16) for i in range(4)]
        self.wsl = [atA(f"wsl{i}", [128, (c.TT // 128) * 2 * c.E], BF16) for i in range(2)]
        self._off = max(o, A0 + c.NDC * c.S * 2)

        self.w2buf = [at(f"w2buf{i}", [128, 4, c.DHW], BF16) for i in range(2)]
        self.y_stage = [at(f"y_stage{i}", [64, 512], BF16) for i in range(2)]
        self.x_rd = [at(f"x_rd{i}", [128, c.D], F32) for i in range(2)]
        self.attn_rd = [at(f"attn_rd{i}", [128, c.D], F32) for i in range(2)]
        self.y_tc = [at(f"y_tc{i}", [128, c.NDC * 128], BF16) for i in range(2)]
        self.out_stage = [at(f"out_stage{i}", [128, 512], F32) for i in range(4)]
        self.xrb = [at(f"xrb{i}", [128, c.RW], BF16) for i in range(2)]
        self.idxs_i = at("idxs_i", [128, c.NTC * c.E], I32)
        self.s12_i = at("s12_i", [128, 2 * c.NTC], I32)
        self.sc_a = at("sc_a", [128, c.NTC * c.E], F32)
        self.sc_b = at("sc_b", [128, c.NTC * c.E], F32)
        self.sc_c = at("sc_c", [128, c.NTC * c.E], F32)
        self.sc_d = at("sc_d", [128, c.NTC * c.E], F32)
        self.sc_e = at("sc_e", [128, c.NTC * c.E], F32)
        self.sc_f = at("sc_f", [128, c.NTC * c.E], F32)
        self.sc_sm = at("sc_sm", [128, 256], F32)
        self.pp_a = at("pp_a", [8, 256], F32)
        self.pp_b = at("pp_b", [8, 256], F32)
        self.ecap_s = at("ecap_s", [128, c.NTC * c.E], F32)
        self.whi_bf = at("whi_bf", [128, c.NTC * c.E], BF16)
        self.wlo_bf = at("wlo_bf", [128, c.NTC * c.E], BF16)
        self.w2x = at("w2x", [128, c.NTC * 2 * c.E], BF16)
        self.wslf = [at(f"wslf{i}", [128, c.NTCT], F32) for i in range(2)]

        C0 = self._off = (self._off + 63) // 64 * 64
        self.x_tm = [A(f"x_tm{i}", [128, c.D], F32, offset=C0 + i * c.D * 4) for i in range(2)]
        o = C0 + 2 * c.D * 4
        self.xn_tm = [A(f"xn_tm{i}", [128, c.D], BF16, offset=o + i * c.D * 2) for i in range(2)]
        o += 2 * c.D * 2
        self.sq = [A(f"sq{i}", [128, c.D], F32, offset=o + i * c.D * 4) for i in range(2)]
        o += 2 * c.D * 4
        self.b1_stage = A("b1_stage", [c.E, c.HF], F32, offset=o); o += c.HF * 4
        self.gw_stage = A("gw_stage", [c.E, c.D], F32, offset=o); o += c.D * 4
        self.bqk_stage = A("bqk_stage", [8, 256], F32, offset=o); o += 1024
        p0_end = o
        self.gates_fm = A("gates_fm", [c.E, c.S], F32, offset=C0)
        o = C0 + c.S * 4
        self.g_tm = A("g_tm", [128, c.NTC * c.E], F32, offset=o); o += c.NTC * c.E * 4
        self.tk = []
        for i in range(4):
            self.tk.append(A(f"tk{i}", [128, c.NTC * c.E], F32, offset=o)); o += c.NTC * c.E * 4
        self.tks = A("tks", [128, 4 * c.NTC], F32, offset=o); o += 4 * c.NTC * 4
        p1_end = o
        self.Q_g = A("Q_g", [128, c.S], BF16, offset=C0)
        self.K_g = A("K_g", [128, c.S], BF16, offset=C0 + c.S * 2)
        o = C0 + 2 * c.S * 2
        vsz = ((c.S // 128) * 65 * 2 + 63) // 64 * 64
        self.v_tm = [A(f"v_tm{i}", [128, (c.S // 128) * 65], BF16, offset=o + i * vsz) for i in range(2)]
        o += 2 * vsz
        self.att_sb = [A(f"att_sb{i}", [128, 1024], BF16, offset=o + i * 2048) for i in range(4)]
        o += 8192
        self.wqT_g = A("wqT_g", [128, c.D], BF16, offset=o); o += c.D * 2
        self.wkT_g = A("wkT_g", [128, c.D], BF16, offset=o); o += c.D * 2
        self.wvT_g = A("wvT_g", [128, c.D], BF16, offset=o); o += c.D * 2
        self.w_stage = [A(f"w_stage{i}", [128, c.D], BF16, offset=o + i * c.D * 2) for i in range(2)]
        o += 2 * c.D * 2
        p2_end = o
        self.wpT = A("wpT", [128, c.NDC * c.D], BF16, offset=C0)
        o = C0 + c.NDC * c.D * 2
        self.attn_stage = [A(f"attn_stage{i}", [128, c.D], F32, offset=o + i * c.D * 4) for i in range(2)]
        o += 2 * c.D * 4
        p3_end = o
        self.out_acc = A("out_acc", [128, c.NTCT * c.D], F32, offset=C0)
        p4_end = C0 + c.NTCT * c.D * 4
        self.sbuf_top = max(p0_end, p1_end, p2_end, p3_end, p4_end)
        assert self.sbuf_top <= ARENA, f"SBUF over budget: {self.sbuf_top} > {ARENA}"

        self.Q = [nc.alloc_psum_tensor(f"psq{i}", [128, 1024], F32) for i in range(4)]

    def op(self, eng, emit, waits=(), incs=(), tag=""):
        return self.p.op(eng, emit, waits, incs, tag)

    def one(self, eng, emit, waits=(), tag=""):
        return self.op(eng, emit, waits, [(eng, 1)], tag)[0]

    def dma(self, eng, sem, emit, waits=(), tag=""):
        return self.op(eng, emit, waits, [(sem, 16)], tag)[0]

    def barrier(self, tag="bar", exclude=()):
        snap = dict(self.p.cnt)
        self._nbar = getattr(self, "_nbar", 0) + 1
        bs = f"bar{self._nbar}"
        self.op("pool", lambda eng: eng.drain(), (), [(bs, 1)], tag=tag + "_a")
        waits = [(bs, 1)]
        for e in ENGS:
            if e != "pool" and self.p.q[e] and not self.p.q[e][-1][2]:
                self.p.inc_last(e, e)
            if self.p.last_event[e] is not None:
                waits.append(self.p.last_event[e])
        waits += [(s, v) for s, v in snap.items()
                  if s not in ENGS and not s.startswith("bar") and v > 0
                  and s not in exclude]
        wm = {}
        for s, v in waits:
            wm[s] = max(wm.get(s, 0), v)
        waits = sorted(wm.items())
        for e in ENGS:
            self.op(e, lambda eng: eng.nop(), waits, (), tag=tag + "_b")

    def build(self):
        self.p0_consts()
        self.p0_ln()
        self.barrier("after_ln")
        self.p1_gates()
        self.barrier("after_gates", exclude=("scp0", "scp1", "xb0", "xb1"))
        self.make_hgen()
        self.mk_relay()
        self.p2_attn()
        self.relay_drip(10 ** 9)
        self.barrier("after_attn")
        self.p3_proj()
        self.barrier("after_proj")
        self.p4_ffn()
        self.op("sp", lambda eng: eng.nop(),
                [(s, v) for s, v in self.p.cnt.items() if s not in ENGS and v > 0],
                [("sp", 1)], tag="final_wait")
        self.p.check()
        self.emit()
        return self.nc

    def p0_consts(self):
        c = self.cfg
        for dst, srcd in ((self.wqb_d, self.wq_d), (self.wkb_d, self.wk_d),
                          (self.wvb_d, self.wv_d), (self.wpb_d, self.wp_d)):
            self.dma("pool", "rlw", (lambda dst, srcd: (lambda eng: eng.dma_start(
                dst[:], srcd[:])))(dst, srcd), [], tag="rlw")
        loads = [
            (self.id_bf[:], self.id_bf_d[:]),
            (self.id_f[:], self.id_f_d[:]),
            (self.ones_f[:], self.ones_f_d[:]),
            (self.masks[:], self.mask_d[:]),
            (self.bp_row[:], self.bp_d[:].unsqueeze(0)),
            (self.bv_row[:], self.bv_d[:].unsqueeze(0)),
            (self.gb_s[:], self.gb_d[:].unsqueeze(1)),
            (self.b1_stage[:], self.b1_d[:]),
            (self.gw_stage[:], self.gw_d[:]),
            (self.bqk_stage[0:c.NDC, 0:128], self.bq_d[:].rearrange("(a b) -> a b", b=128)),
            (self.bqk_stage[0:c.NDC, 128:256], self.bk_d[:].rearrange("(a b) -> a b", b=128)),
            (self.ecap_s[:], self.ecap_d[:]),
        ]
        for dst, src in loads:
            self.dma("sp", "cst", (lambda d, s: (lambda eng: eng.dma_start(d, s)))(dst, src), tag="cst")
        self.op("pool", lambda eng: eng.memset(self.consts[:, 0:1], 1e-5), (), [("ms0", 1)], tag="c_eps")
        self.op("pool", lambda eng: eng.memset(self.consts[:, 1:2], 1.0), (), [("ms1", 1)], tag="c_one")
        self.ev_consts = ("ms0", 1)
        self.dma("pool", "cstp", lambda eng: eng.dma_start(self.b2_bf[:], self.b2_d[:]), tag="b2")
        cst_all = ("cst", self.p.cnt["cst"])

        tp = self.Q[2]
        for j, colbase in ((0, 0), (1, c.NDC)):
            evt = self.op("pe", (lambda j: (lambda eng: eng.transpose(
                self.Q[2][0:128, 0:c.NDC], self.bqk_stage[0:c.NDC, 128 * j:128 * (j + 1)],
                self.id_f[0:c.NDC, 0:c.NDC])))(j),
                [cst_all, ("act", self.p.cnt["act"])], [("pe", 1)], tag=f"bqk_tr{j}")[0]
            self.one("act", (lambda cb: (lambda eng: eng.copy(
                self.bqkT[:, cb:cb + c.NDC], self.Q[2][0:128, 0:c.NDC])))(colbase), [evt], tag=f"bqk_ev{j}")
        for hc in range(c.HC):
            evt = self.op("pe", (lambda hc: (lambda eng: eng.transpose(
                self.Q[2][0:128, 8:8 + c.E], self.b1_stage[0:c.E, hc * 128:(hc + 1) * 128],
                self.id_f[0:c.E, 0:c.E])))(hc),
                [cst_all, ("act", self.p.cnt["act"])], [("pe", 1)], tag=f"b1_tr{hc}")[0]
            self.one("act", (lambda hc: (lambda eng: eng.copy(
                self.b1T[:, hc * c.E:(hc + 1) * c.E], self.Q[2][0:128, 8:8 + c.E])))(hc), [evt], tag=f"b1_ev{hc}")
        for dc in range(c.NDC):
            evt = self.op("pe", (lambda dc: (lambda eng: eng.transpose(
                self.Q[2][0:128, 16:16 + c.E], self.gw_stage[0:c.E, dc * 128:(dc + 1) * 128],
                self.id_f[0:c.E, 0:c.E])))(dc),
                [cst_all, ("act", self.p.cnt["act"])], [("pe", 1)], tag=f"gw_tr{dc}")[0]
            self.one("act", (lambda dc: (lambda eng: eng.copy(
                self.gate_wT[:, dc * c.E:(dc + 1) * c.E], self.Q[2][0:128, 16:16 + c.E])))(dc),
                [evt], tag=f"gw_ev{dc}")

    def p0_ln(self):
        c = self.cfg
        NP = 1.0 / c.D
        ev_dve = [None, None]
        ev_sq = [None, None]
        ev_evac = [None, None]
        ev_var = [None, None]
        ev_trans = [None, None]
        ev_xrw = [None, None]
        for tc in range(c.NTC):
            par = tc % 2
            waits = [ev_dve[par], ev_sq[par]]
            evx = self.dma("sp" if par == 0 else "pool", f"xl{par}",
                (lambda tc, par: (lambda eng: eng.dma_start(
                    self.x_tm[par][:], self.x_d[tc * 128:(tc + 1) * 128, :])))(tc, par),
                waits, tag=f"ln_x{tc}")
            o = par * 8
            m_sl, nm_sl, v_sl, sd_sl, rs_sl = (self.scr[:, o + i:o + i + 1] for i in range(5))
            e1 = self.one("dve", (lambda par, m_sl: (lambda eng: eng.reduce_sum(
                m_sl, self.x_tm[par][:], axis=AX.X)))(par, m_sl), [evx], tag=f"ln_mean{tc}")
            e2 = self.one("act", (lambda m_sl, nm_sl: (lambda eng: eng.mul(nm_sl, m_sl, -NP)))(m_sl, nm_sl),
                          [e1], tag=f"ln_negm{tc}")
            e3 = self.one("act", (lambda par, nm_sl: (lambda eng: eng.activation(
                self.sq[par][:], self.x_tm[par][:], AF.Square, bias=nm_sl)))(par, nm_sl),
                [e2, ev_var[par]], tag=f"ln_sq{tc}")
            ev_sq[par] = e3
            e4 = self.one("dve", (lambda par, v_sl: (lambda eng: eng.reduce_sum(
                v_sl, self.sq[par][:], axis=AX.X)))(par, v_sl), [e3], tag=f"ln_var{tc}")
            ev_var[par] = e4
            e5 = self.one("act", (lambda v_sl, sd_sl: (lambda eng: eng.activation(
                sd_sl, v_sl, AF.Sqrt, bias=self.consts[:, 0:1], scale=NP)))(v_sl, sd_sl),
                [e4, self.ev_consts if tc == 0 else None], tag=f"ln_std{tc}")
            e6 = self.one("dve", (lambda sd_sl, rs_sl: (lambda eng: eng.reciprocal(rs_sl, sd_sl)))(sd_sl, rs_sl),
                          [e5], tag=f"ln_rstd{tc}")
            e7 = self.one("dve", (lambda par, nm_sl, rs_sl: (lambda eng: eng.tensor_scalar(
                self.xn_tm[par][:], self.x_tm[par][:], nm_sl, rs_sl, ALU.add, ALU.mult)))(par, nm_sl, rs_sl),
                [e6, ev_trans[par], ev_xrw[par]], tag=f"ln_xn{tc}")
            ev_dve[par] = e7
            ev_xrw[par] = self.dma("sp", f"xw{par}", (lambda tc, par: (lambda eng: eng.dma_start(
                self.xrow_d[tc * 128:(tc + 1) * 128, 0:c.D], self.xn_tm[par][:])))(tc, par),
                [e7], tag=f"xrw{tc}")
            pv = self.Q[par][:].bitcast(BF16)
            for dc in range(c.NDC):
                self.op("pe", (lambda par, dc, pv: (lambda eng: eng.transpose(
                    pv[:, dc * 128:(dc + 1) * 128],
                    self.xn_tm[par][:, dc * 128:(dc + 1) * 128], self.id_bf[:])))(par, dc, pv),
                    [e7, ev_evac[par]] if dc == 0 else (),
                    [("pe", 1)] if dc == c.NDC - 1 else (), tag=f"ln_tr{tc}_{dc}")
            evt = ("pe", self.p.cnt["pe"])
            ev_trans[par] = evt
            dst = self.xn[:].rearrange("p (dc s) -> p dc s", dc=c.NDC)[:, :, tc * 128:(tc + 1) * 128]
            src = pv[:, 0:c.NDC * 128].rearrange("p (dc t) -> p dc t", dc=c.NDC)
            ev_evac[par] = self.one("act", (lambda dst, src: (lambda eng: eng.copy(dst, src)))(dst, src),
                                    [evt], tag=f"ln_ev{tc}")

    def p1_gates(self):
        c = self.cfg
        ns = c.S // 512
        ev_ev = [None, None]
        for s in range(ns):
            par = s % 2
            gp = self.Q[1][0:c.E, par * 512:(par + 1) * 512]
            for dc in range(c.NDC):
                self.op("pe", (lambda s, dc, gp: (lambda eng: eng.matmul(
                    gp, self.gate_wT[:, dc * c.E:(dc + 1) * c.E],
                    self.xn[:, dc * c.S + s * 512: dc * c.S + (s + 1) * 512],
                    start=(dc == 0), stop=(dc == c.NDC - 1))))(s, dc, gp),
                    [ev_ev[par]] if dc == 0 else (),
                    [("pe", 1)] if dc == c.NDC - 1 else (), tag=f"g_mm{s}_{dc}")
            evt = ("pe", self.p.cnt["pe"])
            ev_ev[par] = self.one("act", (lambda s, gp: (lambda eng: eng.activation(
                self.gates_fm[:, s * 512:(s + 1) * 512], gp, AF.Identity, bias=self.gb_s[:])))(s, gp),
                [evt], tag=f"g_ev{s}")
        ga0, ga1 = ev_ev[0], ev_ev[1]
        ev_tre = [None, None]
        for tc in range(c.NTC):
            par = tc % 2
            sl = 32 + par * 512
            evt = self.op("pe", (lambda tc, sl: (lambda eng: eng.transpose(
                self.Q[2][0:128, sl:sl + c.E], self.gates_fm[0:c.E, tc * 128:(tc + 1) * 128],
                self.id_f[0:c.E, 0:c.E])))(tc, sl),
                [ga0, ga1, ev_tre[par]], [("pe", 1)], tag=f"g_tr{tc}")[0]
            ev_tre[par] = self.one("act", (lambda tc, sl: (lambda eng: eng.copy(
                self.g_tm[:, tc * c.E:(tc + 1) * c.E], self.Q[2][0:128, sl:sl + c.E])))(tc, sl),
                [evt], tag=f"g_tre{tc}")
        gtm = [ev_tre[0], ev_tre[1]]
        NE, NT = c.E, c.NTC
        g3 = self.g_tm[:].rearrange("p (t e) -> p t e", e=NE)
        mm1 = self.tks[:, 0:NT]
        mm2 = self.tks[:, NT:2 * NT]
        em2 = self.tks[:, 2 * NT:3 * NT]
        rden = self.tks[:, 3 * NT:4 * NT]
        d_, eq_, msk_, wun_ = (t[:] for t in self.tk[0:4])
        d3 = d_.rearrange("p (t e) -> p t e", e=NE)
        e1 = self.one("dve", lambda eng: eng.reduce_max(mm1, g3, axis=AX.X), gtm, tag="tk_m1")
        e2 = self.one("dve", lambda eng: eng.tensor_tensor(
            d3, g3, mm1.unsqueeze(2).broadcast_to([128, NT, NE]), ALU.subtract), [e1], tag="tk_d")
        e3 = self.one("dve", lambda eng: eng.tensor_single_scalar(eq_, d_, 0.0, ALU.is_equal), [e2], tag="tk_eq")
        e4 = self.one("dve", lambda eng: eng.scalar_tensor_tensor(
            msk_, eq_, -1e30, d_, ALU.mult, ALU.add), [e3], tag="tk_msk")
        e5 = self.one("dve", lambda eng: eng.reduce_max(
            mm2, msk_.rearrange("p (t e) -> p t e", e=NE), axis=AX.X), [e4], tag="tk_m2")
        e6 = self.one("act", lambda eng: eng.activation(eq_, d_, AF.Exp), [e2, e4], tag="tk_expd")
        expd = eq_
        e7 = self.one("dve", lambda eng: eng.tensor_tensor(
            msk_.rearrange("p (t e) -> p t e", e=NE), d3,
            mm2.unsqueeze(2).broadcast_to([128, NT, NE]), ALU.is_ge), [e5], tag="tk_ge")
        ge_ = msk_
        e8 = self.one("dve", lambda eng: eng.tensor_tensor(wun_, expd, ge_, ALU.mult), [e6, e7], tag="tk_wun")
        e9 = self.one("act", lambda eng: eng.activation(em2, mm2, AF.Exp), [e5], tag="tk_em2")
        e10 = self.one("act", lambda eng: eng.add(em2, em2, 1.0), [e9], tag="tk_den")
        e11 = self.one("dve", lambda eng: eng.reciprocal(rden, em2), [e10], tag="tk_rden")
        e12 = self.one("dve", lambda eng: eng.tensor_tensor(
            self.w_gates_tm[:].rearrange("p (t e) -> p t e", e=NE),
            wun_.rearrange("p (t e) -> p t e", e=NE),
            rden.unsqueeze(2).broadcast_to([128, NT, NE]), ALU.mult), [e8, e11], tag="tk_w")
        ev_tre = [None, None]
        for tc in range(c.NTC):
            par = tc % 2
            sl = 48 + par * 512
            evt = self.op("pe", (lambda tc, sl: (lambda eng: eng.transpose(
                self.Q[2][0:c.E, sl:sl + 128], self.w_gates_tm[:, tc * c.E:(tc + 1) * c.E],
                self.id_f[:])))(tc, sl), [e12, ev_tre[par]], [("pe", 1)], tag=f"w_tr{tc}")[0]
            ev_tre[par] = self.one("act", (lambda tc, sl: (lambda eng: eng.copy(
                self.w_fm[0:c.E, tc * 128:(tc + 1) * 128], self.Q[2][0:c.E, sl:sl + 128])))(tc, sl),
                [evt], tag=f"w_tre{tc}")
        eh1 = self.one("dve", lambda eng: eng.tensor_copy(self.whi_bf[:], self.w_gates_tm[:]),
                       [e12], tag="wsp1")
        eh2 = self.one("dve", lambda eng: eng.tensor_copy(self.sc_a[:], self.whi_bf[:]),
                       [eh1], tag="wsp2")
        eh3 = self.one("dve", lambda eng: eng.tensor_sub(self.sc_b[:], self.w_gates_tm[:], self.sc_a[:]),
                       [eh2], tag="wsp3")
        eh4 = self.one("dve", lambda eng: eng.tensor_copy(self.wlo_bf[:], self.sc_b[:]),
                       [eh3], tag="wsp4")
        v2 = self.w2x[:].rearrange("p (t e two) -> p t e two", e=c.E, two=2)
        eh5 = self.one("dve", lambda eng: eng.tensor_copy(
            v2[:, :, :, 0], self.whi_bf[:].rearrange("p (t e) -> p t e", e=c.E)), [eh4], tag="wsp5")
        eh6 = self.one("dve", lambda eng: eng.tensor_copy(
            v2[:, :, :, 1], self.wlo_bf[:].rearrange("p (t e) -> p t e", e=c.E)), [eh5], tag="wsp6")
        for tc in range(c.NTC):
            self.dma("sp", "ww", (lambda tc: (lambda eng: eng.dma_start(
                self.xrow_d[tc * 128:(tc + 1) * 128, c.D:c.D + 2 * c.E],
                self.w2x[:, tc * 2 * c.E:(tc + 1) * 2 * c.E])))(tc),
                [eh6], tag=f"wwr{tc}")
        self.p15_route(eh6)


    def mk_relay(self):
        c = self.cfg
        self.n_relay = c.E * c.NP1 + c.E * c.NDH * c.NP1

        def gen():
            for e in range(c.E):
                for hp in range(c.NP1):
                    yield ("w1", e, hp, 0)
                for dh in range(c.NDH):
                    for hp in range(c.NP1):
                        yield ("w2", e, hp, dh)
        self._relay_it = gen()

    def relay_drip(self, n):
        c = self.cfg
        for _ in range(n):
            try:
                kind, e, hp, dh = next(self._relay_it)
            except StopIteration:
                return
            if kind == "w1":
                self.dma("pool", "rl", (lambda e, hp: (lambda eng: eng.dma_start(
                    self.w1b_d[e, hp].rearrange("p (dc hcl f) -> p dc hcl f", dc=c.NDC, f=128),
                    self.w1_d[e, :, hp * 512:(hp + 1) * 512].rearrange(
                        "(dc p) (hcl f) -> p dc hcl f", p=128, f=128))))(e, hp),
                    [], tag=f"rl1_{e}_{hp}")
            else:
                self.dma("pool", "rl", (lambda e, hp, dh: (lambda eng: eng.dma_start(
                    self.w2b_d[e, dh, hp].rearrange("p (hcl f) -> p hcl f", f=c.DHW),
                    self.w2_d[e, hp * 512:(hp + 1) * 512,
                              dh * c.DHW:(dh + 1) * c.DHW].rearrange(
                        "(hcl p) f -> p hcl f", p=128))))(e, hp, dh),
                    [], tag=f"rl2_{e}_{dh}_{hp}")

    def p15_route(self, e12):
        c = self.cfg
        NE, NT = c.E, c.NTC
        BIG = 60000.0
        A3 = lambda t: t[:].rearrange("p (t e) -> p t e", e=NE)
        ev = e12

        def dve(emit, extra=(), tag=""):
            nonlocal ev
            ev = self.one("dve", emit, [ev] + list(extra), tag=tag)
            return ev
        dve(lambda eng: eng.tensor_single_scalar(self.sc_d[:], self.w_gates_tm[:], 0.0, ALU.is_gt), tag="rt_mk")
        dve(lambda eng: eng.tensor_copy(self.sc_a[:], self.sc_d[:]), tag="rt_cp")
        cur, nxt = self.sc_a, self.sc_b
        k = 1
        while k < NT:
            dve((lambda cur, nxt, k: (lambda eng: eng.tensor_copy(
                A3(nxt)[:, 0:k, :], A3(cur)[:, 0:k, :])))(cur, nxt, k), tag=f"rt_s{k}a")
            dve((lambda cur, nxt, k: (lambda eng: eng.tensor_tensor(
                A3(nxt)[:, k:NT, :], A3(cur)[:, k:NT, :], A3(cur)[:, 0:NT - k, :],
                ALU.add)))(cur, nxt, k), tag=f"rt_s{k}b")
            cur, nxt = nxt, cur
            k *= 2
        incl = cur
        dve((lambda incl: (lambda eng: eng.tensor_sub(self.sc_c[:], incl[:], self.sc_d[:])))(incl), tag="rt_ex")
        ept = self.op("pe", (lambda incl: (lambda eng: eng.transpose(
            self.Q[2][0:NE, 256:384], incl[:, (NT - 1) * NE:NT * NE], self.id_f[:])))(incl),
            [ev, ("act", self.p.cnt["act"])], [("pe", 1)], tag="rt_tr1")[0]
        ea = self.one("act", lambda eng: eng.copy(self.pp_a[0:NE, 0:128], self.Q[2][0:NE, 256:384]),
                      [ept], tag="rt_ev1")
        ev = ea
        cur, nxt = self.pp_a, self.pp_b
        k = 1
        while k < 128:
            dve((lambda cur, nxt, k: (lambda eng: eng.tensor_copy(
                nxt[0:NE, 0:k], cur[0:NE, 0:k])))(cur, nxt, k), tag=f"rt_p{k}a")
            dve((lambda cur, nxt, k: (lambda eng: eng.tensor_tensor(
                nxt[0:NE, k:128], cur[0:NE, k:128], cur[0:NE, 0:128 - k],
                ALU.add)))(cur, nxt, k), tag=f"rt_p{k}b")
            cur, nxt = nxt, cur
            k *= 2
        inclp, other = cur, nxt
        evm = self.one("pool", (lambda other: (lambda eng: eng.memset(other[0:NE, 0:1], 0.0)))(other),
                       [ev], tag="rt_ms")
        dve((lambda inclp, other: (lambda eng: eng.tensor_copy(
            other[0:NE, 1:128], inclp[0:NE, 0:127])))(inclp, other), [evm], tag="rt_sh")
        ep2 = self.op("pe", (lambda other: (lambda eng: eng.transpose(
            self.Q[2][0:128, 384:384 + NE], other[0:NE, 0:128], self.id_f[0:NE, 0:NE])))(other),
            [ev, ("act", self.p.cnt["act"])], [("pe", 1)], tag="rt_tr2")[0]
        ev = self.one("act", lambda eng: eng.copy(self.sc_sm[:, 0:NE], self.Q[2][0:128, 384:384 + NE]),
                      [ep2], tag="rt_ev2")
        dve(lambda eng: eng.tensor_tensor(
            A3(self.sc_b), A3(self.sc_c),
            self.sc_sm[:, 0:NE].unsqueeze(1).broadcast_to([128, NT, NE]), ALU.add), tag="rt_pt")
        dve(lambda eng: eng.tensor_single_scalar(self.sc_c[:], self.sc_b[:], float(c.CAP), ALU.is_lt), tag="rt_ft")
        dve(lambda eng: eng.tensor_tensor(self.sc_c[:], self.sc_c[:], self.sc_d[:], ALU.mult), tag="rt_vl")
        dve(lambda eng: eng.tensor_tensor(self.sc_b[:], self.sc_b[:], self.ecap_s[:], ALU.add), tag="rt_sg")
        dve(lambda eng: eng.reduce_max(self.sc_sm[:, 32:32 + NT], A3(self.w_gates_tm), axis=AX.X), tag="rt_rm")
        dve(lambda eng: eng.tensor_tensor(
            A3(self.sc_e), A3(self.w_gates_tm),
            self.sc_sm[:, 32:32 + NT].unsqueeze(2).broadcast_to([128, NT, NE]), ALU.is_ge), tag="rt_s1")
        dve(lambda eng: eng.tensor_sub(self.sc_f[:], self.sc_d[:], self.sc_e[:]), tag="rt_s2")
        for k2, sel in ((0, self.sc_e), (1, self.sc_f)):
            dve((lambda sel: (lambda eng: eng.tensor_tensor(sel[:], sel[:], self.sc_c[:], ALU.mult)))(sel),
                tag=f"rt_a{k2}")
            dve((lambda sel: (lambda eng: eng.tensor_tensor(self.sc_a[:], sel[:], self.sc_b[:], ALU.mult)))(sel),
                tag=f"rt_q{k2}")
            dve(lambda eng: eng.reduce_sum(self.sc_sm[:, 64:64 + NT], A3(self.sc_a), axis=AX.X), tag=f"rt_sa{k2}")
            dve((lambda sel: (lambda eng: eng.reduce_sum(
                self.sc_sm[:, 96:96 + NT], A3(sel), axis=AX.X)))(sel), tag=f"rt_n{k2}")
            dve(lambda eng: eng.tensor_scalar(
                self.sc_sm[:, 128:128 + NT], self.sc_sm[:, 96:96 + NT], -1.0, -BIG,
                ALU.add, ALU.mult), tag=f"rt_u{k2}")
            dve(lambda eng: eng.tensor_tensor(
                self.sc_sm[:, 64:64 + NT], self.sc_sm[:, 64:64 + NT],
                self.sc_sm[:, 128:128 + NT], ALU.add), tag=f"rt_ss{k2}")
            dve((lambda k2: (lambda eng: eng.tensor_copy(
                self.s12_i[:, k2 * NT:(k2 + 1) * NT], self.sc_sm[:, 64:64 + NT])))(k2), tag=f"rt_ci{k2}")
        ev_idx = ev
        def mk_reg(eng):
            self._bc_reg = eng.to_reg(c.E * c.CAP - 1)
            return eng.nop()
        self.op("pool", mk_reg, (), [("ms8", 1)], tag="rt_reg")
        wr_all = [(s, self.p.cnt[s]) for s in ("xw0", "xw1", "ww") if s in self.p.cnt]
        ev_rnd = [None, None]
        for tc in range(c.NTC):
            par = tc % 2
            exb = self.dma("sp", f"xb{par}", (lambda tc, par: (lambda eng: eng.dma_start(
                self.xrb[par][:], self.xrow_d[tc * 128:(tc + 1) * 128, :])))(tc, par),
                wr_all + [ev_rnd[par]], tag=f"xrl{tc}")
            for k2 in range(2):
                self.dma("pool", f"scp{par}", (lambda tc, par, k2: (lambda eng: eng.indirect_dma_start(
                    self.xg_d[:],
                    IndirectOffsetOnAxis(self.s12_i[:, k2 * NT + tc:k2 * NT + tc + 1], 0),
                    self.xrb[par][:], None,
                    bounds_check=self._bc_reg, oob_is_err=False)))(tc, par, k2),
                    [exb, ev_idx] if k2 == 0 else [], tag=f"sct{tc}_{k2}")
            ev_rnd[par] = (f"scp{par}", self.p.cnt[f"scp{par}"])

    def make_hgen(self):
        c = self.cfg
        nh = c.TT // 512
        self.ev_hwr_ev = {}
        self.h_pos = -1
        st = {
            "u": 0,
            "ev_gelu": [None, None],
            "ev_hwr_free": [None, None, None, None],
            "ev_w1_free": [None, None],
            "piece_loaded": {},
            "chunk_loaded": {},
            "ev_xga_free": [None, None],
            "q3_guard": [],
        }
        self.hst = st
        self.ev_wslf_ev = {}

        def load_piece(P):
            if P >= c.NSG * c.NP1 or P in st["piece_loaded"]:
                return
            sgi2, hp2 = divmod(P, c.NP1)
            e2 = sgi2 // c.NCH
            bi = P % 2
            ev = self.dma("pool", f"w1{bi}", (lambda e2, hp2, bi: (lambda eng: eng.dma_start(
                self.w1buf[bi][:],
                self.w1b_d[e2, hp2].rearrange(
                    "p (dc hcl f) -> p dc hcl f", dc=c.NDC, f=128))))(e2, hp2, bi),
                [st["ev_w1_free"][bi], ("rl", 16 * self.n_relay)], tag=f"w1d{sgi2}_{hp2}")
            st["piece_loaded"][P] = ev

        def load_chunk(sgi2):
            if sgi2 >= c.NSG or sgi2 in st["chunk_loaded"]:
                return
            bi = sgi2 % 2
            rb = sgi2 * c.TT
            sem = f"xg{bi}"
            for dc in range(c.NDC):
                self.dma("sp", sem, (lambda sgi2, dc, bi, rb: (lambda eng: eng.dma_start(
                    self.xga[bi][:, dc * c.TT:(dc + 1) * c.TT],
                    self.xg_d[rb:rb + c.TT, dc * 128:(dc + 1) * 128], transpose=True)))(sgi2, dc, bi, rb),
                    [st["ev_xga_free"][bi]] if dc == 0 else [], tag=f"xgl{sgi2}_{dc}")
            self.dma("sp", sem, (lambda sgi2, bi, rb: (lambda eng: eng.dma_start(
                self.wsl[bi][:].rearrange("p (a w) -> p a w", w=2 * c.E),
                self.xg_d[rb:rb + c.TT, c.D:c.D + 2 * c.E].rearrange(
                    "(a p) w -> p a w", p=128))))(sgi2, bi, rb),
                [self.ev_wslf_ev.get(sgi2 - 2)], tag=f"wsl{sgi2}")
            st["chunk_loaded"][sgi2] = (sem, self.p.cnt[sem])
        self.load_chunk = load_chunk

        def units():
            for sgi in range(c.NSG):
                for hp in range(c.NP1):
                    for hcl in range(4):
                        for half in range(nh):
                            yield sgi, hp, hcl, half
        it = units()

        def pull():
            try:
                sgi, hp, hcl, half = next(it)
            except StopIteration:
                return False
            e = sgi // c.NCH
            hc = hp * 4 + hcl
            P = sgi * c.NP1 + hp
            load_chunk(sgi)
            load_chunk(sgi + 1)
            load_piece(P)
            load_piece(P + 1)
            bi = P % 2
            cbi = sgi % 2
            u = st["u"]; st["u"] += 1
            qp = u % 2
            hQ = self.Q[3][:, qp * 512:qp * 512 + 512]
            for dc in range(c.NDC):
                wts = []
                if dc == 0:
                    wts = [st["piece_loaded"][P], st["chunk_loaded"][sgi],
                           st["ev_gelu"][qp]] + st["q3_guard"]
                    st["q3_guard"] = []
                self.op("pe", (lambda hcl, half, dc, bi, cbi, hQ: (lambda eng: eng.matmul(
                    hQ, self.w1buf[bi][:, dc, hcl, :],
                    self.xga[cbi][:, dc * c.TT + half * 512: dc * c.TT + (half + 1) * 512],
                    start=(dc == 0), stop=(dc == c.NDC - 1))))(hcl, half, dc, bi, cbi, hQ),
                    wts, [("pe", 1)] if dc == c.NDC - 1 else (),
                    tag=f"h_mm{sgi}_{hc}_{half}_{dc}")
            evh = ("pe", self.p.cnt["pe"])
            if hcl == 3 and half == nh - 1:
                st["ev_w1_free"][bi] = evh
            if hp == c.NP1 - 1 and hcl == 3 and half == nh - 1:
                st["ev_xga_free"][cbi] = evh
            wbi = hc % 4
            ege = self.one("act", (lambda e, hc, half, wbi, hQ: (lambda eng: eng.activation(
                self.h_wr[wbi][:, half * 512:(half + 1) * 512], hQ, AF.Gelu,
                bias=self.b1T[:, hc * c.E + e:hc * c.E + e + 1])))(e, hc, half, wbi, hQ),
                [evh, st["ev_hwr_free"][wbi] if half == 0 else None], tag=f"gelu{sgi}_{hc}_{half}")
            st["ev_gelu"][qp] = ege
            if half == nh - 1:
                self.dma("act", f"hw{wbi}", (lambda sgi, hc, wbi: (lambda eng: eng.dma_start(
                    self.h_d[sgi, hc * 128:(hc + 1) * 128, :], self.h_wr[wbi][:, 0:c.TT])))(sgi, hc, wbi),
                    [ege], tag=f"hwr{sgi}_{hc}")
                st["ev_hwr_free"][wbi] = (f"hw{wbi}", self.p.cnt[f"hw{wbi}"])
                self.ev_hwr_ev[(sgi, hc)] = st["ev_hwr_free"][wbi]
                if hc == c.HC - 1:
                    self.h_pos = sgi
            return True
        self.pull_h = pull

    def pull_h_n(self, n):
        for _ in range(n):
            if not self.pull_h():
                return

    def pull_h_until(self, gi):
        while self.h_pos < gi:
            if not self.pull_h():
                raise RuntimeError("h generator exhausted early")

    def p2_attn(self):
        c = self.cfg
        self.op("pool", lambda eng: eng.memset(self.v_tm[0][:], 1.0), (), [("ms2", 1)], tag="vt_ms0")
        self.op("pool", lambda eng: eng.memset(self.v_tm[1][:], 1.0), (), [("ms3", 1)], tag="vt_ms1")
        ev_vtm_war = [("ms2", 1), ("ms3", 1)]
        ev_wtr_ev = [None, None]
        ev_stage_free = [None, None]
        ev_exp_q = [None, None]
        ev_attsb = [None, None, None, None]
        ev_y_free = None
        ev_qk_last_ev = None
        ev_ystage = [None, None]
        wd = {0: self.wqb_d, 1: self.wkb_d, 2: self.wvb_d}
        for g in range(c.NDC):
            wT = {0: self.wqT_g, 1: self.wkT_g, 2: self.wvT_g}
            ev_wT = {}
            for j in range(3):
                par = j % 2
                evs = self.dma("act", f"ws{par}", (lambda g, j, par: (lambda eng: eng.dma_start(
                    self.w_stage[par][:], wd[j][g * 128:(g + 1) * 128, :])))(g, j, par),
                    [ev_stage_free[par], ("rlw", 64)], tag=f"wst{g}_{j}")
                pv = self.Q[2][:].bitcast(BF16)
                lastev = None
                for dc in range(c.NDC):
                    half = dc % 2
                    sl = half * 1024
                    wts = []
                    if dc == 0:
                        wts = [evs, ev_y_free]
                    wts.append(ev_wtr_ev[half])
                    self.op("pe", (lambda par, sl, dc, pv: (lambda eng: eng.transpose(
                        pv[:, sl:sl + 128], self.w_stage[par][:, dc * 128:(dc + 1) * 128],
                        self.id_bf[:])))(par, sl, dc, pv), wts, [("pe", 1)], tag=f"wtr{g}_{j}_{dc}")
                    evt = ("pe", self.p.cnt["pe"])
                    lastev = self.one("act", (lambda dc, sl, pv, wTj=wT[j]: (lambda eng: eng.copy(
                        wTj[:, dc * 128:(dc + 1) * 128], pv[:, sl:sl + 128])))(dc, sl, pv),
                        [evt], tag=f"wtre{g}_{j}_{dc}")
                    ev_wtr_ev[half] = lastev
                ev_wT[j] = lastev
                ev_stage_free[par] = lastev
            self.relay_drip(-(-self.n_relay // c.NDC))
            ev_vev = [None, None]
            for kc in range(c.S // 128):
                par = kc % 2
                vp = self.Q[par][:, 0:128]
                for dc in range(c.NDC):
                    wts = []
                    if dc == 0:
                        wts = [ev_wT[2], ev_vev[par], ev_y_free if kc == 0 else None]
                        if kc < 2:
                            wts.append(ev_exp_q[par])
                    self.op("pe", (lambda kc, dc, vp: (lambda eng: eng.matmul(
                        vp, self.xn[:, dc * c.S + kc * 128: dc * c.S + (kc + 1) * 128],
                        self.wvT_g[:, dc * 128:(dc + 1) * 128],
                        start=(dc == 0), stop=False)))(kc, dc, vp), wts, (), tag=f"v_mm{g}_{kc}_{dc}")
                self.op("pe", (lambda g, vp: (lambda eng: eng.matmul(
                    vp, self.ones_f[:], self.bv_row[:, g * 128:(g + 1) * 128],
                    start=False, stop=True)))(g, vp), (), [("pe", 1)], tag=f"v_bv{g}_{kc}")
                evt = ("pe", self.p.cnt["pe"])
                wv = [evt]
                if kc < 2:
                    wv += ev_vtm_war
                self.one("dve", (lambda kc, vp: (lambda eng: eng.tensor_copy(
                    self.v_tm[0][:, kc * 65:kc * 65 + 64], vp[:, 0:64])))(kc, vp), wv, tag=f"v_evA{g}_{kc}")
                ev_vev[par] = self.one("dve", (lambda kc, vp: (lambda eng: eng.tensor_copy(
                    self.v_tm[1][:, kc * 65:kc * 65 + 64], vp[:, 64:128])))(kc, vp), [evt], tag=f"v_evB{g}_{kc}")
            ev_vready = ev_vev[0] if (c.S // 128) % 2 == 1 else ev_vev[1]
            QK = {0: (self.wqT_g, self.Q_g, 0), 1: (self.wkT_g, self.K_g, c.NDC)}
            nsl = c.S // 512
            ev_qk_ev = ev_qk_last_ev
            for j in (0, 1):
                wTj, dstT, bcol = QK[j]
                for hb in range((nsl + 3) // 4):
                    sls = list(range(hb * 4, min(nsl, hb * 4 + 4)))
                    for dc in range(c.NDC):
                        for i, s in enumerate(sls):
                            pq = self.Q[i // 2][:, (i % 2) * 512:(i % 2) * 512 + 512]
                            wts = []
                            if dc == 0 and i == 0:
                                wts = [ev_wT[j], ev_qk_ev, ev_exp_q[0], ev_exp_q[1],
                                       ev_vev[0], ev_vev[1]]
                            self.op("pe", (lambda j, dc, s, pq, wTj: (lambda eng: eng.matmul(
                                pq, wTj[:, dc * 128:(dc + 1) * 128],
                                self.xn[:, dc * c.S + s * 512: dc * c.S + (s + 1) * 512],
                                start=(dc == 0), stop=(dc == c.NDC - 1))))(j, dc, s, pq, wTj),
                                wts, [("pe", 1)] if (dc == c.NDC - 1 and i == len(sls) - 1) else (),
                                tag=f"qk{g}_{j}_{hb}_{dc}_{s}")
                    evt = ("pe", self.p.cnt["pe"])
                    for i in range(0, len(sls), 2):
                        n2 = min(2, len(sls) - i)
                        pq = self.Q[i // 2][:, 0:512 * n2]
                        dst = dstT[:, sls[i] * 512: (sls[i] + n2) * 512]
                        ev_qk_ev = self.one("act", (lambda pq, dst, g, bcol: (lambda eng: eng.activation(
                            dst, pq, AF.Identity, bias=self.bqkT[:, bcol + g:bcol + g + 1])))(pq, dst, g, bcol),
                            [evt], tag=f"qk_ev{g}_{j}_{hb}_{i}")
            ev_qk = ev_qk_ev
            ev_qk_last_ev = ev_qk_ev
            qm = self.scr[:, 16:17]
            km = self.scr[:, 17:18]
            r1 = self.scr1[:, 0:1]
            r2 = self.scr1[:, 1:2]
            eq1 = self.one("dve", lambda eng: eng.tensor_reduce(
                qm, self.Q_g[:], AX.X, ALU.max, apply_absolute_value=True), [ev_qk], tag=f"b_q{g}")
            ek1 = self.one("dve", lambda eng: eng.tensor_reduce(
                km, self.K_g[:], AX.X, ALU.max, apply_absolute_value=True), [eq1], tag=f"b_k{g}")
            et1 = self.op("pe", lambda eng: eng.transpose(
                self.Q[2][0:1, 0:128], qm, self.id_f[:]),
                [eq1, ev_y_free], [("pe", 1)], tag=f"b_tq{g}")[0]
            er1 = self.one("dve", lambda eng: eng.reduce_max(r1, self.Q[2][0:1, 0:128], axis=AX.X),
                           [et1], tag=f"b_rq{g}")
            et2 = self.op("pe", lambda eng: eng.transpose(
                self.Q[2][0:1, 128:256], km, self.id_f[:]), [ek1, er1], [("pe", 1)], tag=f"b_tk{g}")[0]
            er2 = self.one("dve", lambda eng: eng.reduce_max(r2, self.Q[2][0:1, 128:256], axis=AX.X),
                           [et2], tag=f"b_rk{g}")
            es1 = self.one("act", lambda eng: eng.activation(r1, r1, AF.Square), [er1], tag=f"b_sq{g}")
            es2 = self.one("act", lambda eng: eng.activation(r2, r2, AF.Square), [er2], tag=f"b_sk{g}")
            ea = self.one("dve", lambda eng: eng.tensor_add(r1, r1, r2), [es1, es2], tag=f"b_add{g}")
            en = self.one("act", lambda eng: eng.mul(r1, r1, -1.0), [ea], tag=f"b_neg{g}")
            eb = self.op("pe", lambda eng: eng.matmul(
                self.Q[2][0:128, 256:257], self.ones_f[:], r1[:, 0:1], start=True, stop=True),
                [en], [("pe", 1)], tag=f"b_bc{g}")[0]
            ebb = self.one("act", (lambda g: (lambda eng: eng.copy(
                self.negb[:, g:g + 1], self.Q[2][0:128, 256:257])))(g), [eb], tag=f"b_cp{g}")
            last_mul = ev_y_free
            for qt in range(c.NQT):
                kcm = 4 * (qt + 1)
                pend = []

                def emit_av(kc, ap, ee, kcm=None):
                    avw = [ee]
                    if kc == 0:
                        avw += [ev_vready, last_mul]
                    self.op("pe", (lambda kc, ap, kcm: (lambda eng: eng.matmul(
                        self.Q[2][0:65, 0:512], self.v_tm[0][:, kc * 65:(kc + 1) * 65],
                        self.att_sb[ap][:, 0:512], start=(kc == 0), stop=(kc == kcm - 1))))(kc, ap, kcm),
                        avw, (), tag=f"avA{g}_{kc}")
                    self.op("pe", (lambda kc, ap, kcm: (lambda eng: eng.matmul(
                        self.Q[2][0:65, 512:1024], self.v_tm[1][:, kc * 65:(kc + 1) * 65],
                        self.att_sb[ap][:, 512:1024], start=(kc == 0), stop=(kc == kcm - 1))))(kc, ap, kcm),
                        (), [("pe", 1)], tag=f"avB{g}_{kc}")
                    ev_attsb[ap] = ("pe", self.p.cnt["pe"])

                for kc in range(kcm):
                    kp = kc % 2
                    ap = kc % 4
                    sp_ = self.Q[kp]
                    w0 = [ev_exp_q[kp]]
                    if qt == 0 and kc == 0:
                        w0 += [ev_qk, ebb]
                    if qt > 0 and kc == 0:
                        w0.append(ev_epi_q0)
                    self.op("pe", (lambda qt, kc, sp_: (lambda eng: eng.matmul(
                        sp_[:, 0:512], self.K_g[0:64, kc * 128:(kc + 1) * 128],
                        self.Q_g[0:64, qt * 512:(qt + 1) * 512], start=True, stop=True,
                        tile_position=(0, 0))))(qt, kc, sp_), w0, (), tag=f"scA{g}_{qt}_{kc}")
                    self.op("pe", (lambda qt, kc, sp_: (lambda eng: eng.matmul(
                        sp_[:, 512:1024], self.K_g[64:128, kc * 128:(kc + 1) * 128],
                        self.Q_g[64:128, qt * 512:(qt + 1) * 512], start=True, stop=True,
                        tile_position=(64, 0))))(qt, kc, sp_), (), [("pe", 1)], tag=f"scB{g}_{qt}_{kc}")
                    ev_sc = ("pe", self.p.cnt["pe"])
                    wts = [ev_sc, ev_attsb[ap]]
                    if qt == 0 and kc == 0:
                        wts.append(ebb)
                    ee = self.one("act", (lambda g, ap, sp_: (lambda eng: eng.activation(
                        self.att_sb[ap][:], sp_[:], AF.Exp, bias=self.negb[:, g:g + 1],
                        scale=self.cfg.scale)))(g, ap, sp_), wts, tag=f"exp{g}_{qt}_{kc}")
                    ev_exp_q[kp] = ee
                    dnum = kc - 4 * qt
                    if dnum >= 0:
                        ee = self.one("dve", (lambda ap, dnum: (lambda eng: eng.tensor_tensor(
                            self.att_sb[ap][:].rearrange("p (h q) -> p h q", h=2),
                            self.att_sb[ap][:].rearrange("p (h q) -> p h q", h=2),
                            self.masks[:, dnum * 512:(dnum + 1) * 512].unsqueeze(1)
                                .broadcast_to([128, 2, 512]), ALU.mult)))(ap, dnum),
                            [ee], tag=f"msk{g}_{qt}_{kc}")
                    pend.append((kc, ap, ee))
                    if len(pend) > 2:
                        emit_av(*pend.pop(0), kcm=kcm)
                while pend:
                    emit_av(*pend.pop(0), kcm=kcm)
                ev_avB = ("pe", self.p.cnt["pe"])
                bslot = self.Q[0][0:64, 0:512]
                erA = self.one("act", lambda eng: eng.activation(
                    self.sumrow[:], self.Q[2][64:65, 0:512], AF.Ln), [ev_avB, last_mul], tag=f"nrA{g}_{qt}")
                ebcA = self.op("pe", lambda eng: eng.matmul(
                    bslot, self.ones_f[:, 0:64], self.sumrow[:], start=True, stop=True),
                    [erA, ev_exp_q[0]], [("pe", 1)], tag=f"nbA{g}_{qt}")[0]
                ecA = self.one("act", lambda eng: eng.activation(
                    self.recip64[:], bslot, AF.Exp, scale=-1.0), [ebcA], tag=f"ncA{g}_{qt}")
                mulA = self.one("dve", lambda eng: eng.tensor_mul(
                    self.y_stage[0][:], self.Q[2][0:64, 0:512], self.recip64[:]),
                    [ecA, ev_ystage[0]], tag=f"nmA{g}_{qt}")
                self.dma("sp", "yw0", (lambda g, qt: (lambda eng: eng.dma_start(
                    self.y_d_scr[0:64, g * c.S + qt * 512:g * c.S + (qt + 1) * 512],
                    self.y_stage[0][:])))(g, qt), [mulA], tag=f"ywA{g}_{qt}")
                ev_ystage[0] = ("yw0", self.p.cnt["yw0"])
                erB = self.one("act", lambda eng: eng.activation(
                    self.sumrow2[:], self.Q[2][64:65, 512:1024], AF.Ln), [ev_avB], tag=f"nrB{g}_{qt}")
                ebcB = self.op("pe", lambda eng: eng.matmul(
                    bslot, self.ones_f[:, 0:64], self.sumrow2[:], start=True, stop=True),
                    [erB, ecA], [("pe", 1)], tag=f"nbB{g}_{qt}")[0]
                ecB = self.one("act", lambda eng: eng.activation(
                    self.recip64[:], bslot, AF.Exp, scale=-1.0), [ebcB, mulA], tag=f"ncB{g}_{qt}")
                last_mul = self.one("dve", lambda eng: eng.tensor_mul(
                    self.y_stage[1][:], self.Q[2][0:64, 512:1024], self.recip64[:]),
                    [ecB, ev_ystage[1]], tag=f"nmB{g}_{qt}")
                self.dma("sp", "yw1", (lambda g, qt: (lambda eng: eng.dma_start(
                    self.y_d_scr[64:128, g * c.S + qt * 512:g * c.S + (qt + 1) * 512],
                    self.y_stage[1][:])))(g, qt), [last_mul], tag=f"ywB{g}_{qt}")
                ev_ystage[1] = ("yw1", self.p.cnt["yw1"])
                ev_epi_q0 = ecB
                ev_y_free = last_mul
            ev_vtm_war = [ev_attsb[0], ev_attsb[1]]

    def p3_proj(self):
        c = self.cfg
        ev_wtr_ev = [None, None]
        ev_stage_free = [None, None]
        for dcH in range(c.NDC):
            par = dcH % 2
            evs = self.dma("act", f"ws{par}", (lambda dcH, par: (lambda eng: eng.dma_start(
                self.w_stage[par][:], self.wpb_d[dcH * 128:(dcH + 1) * 128, :])))(dcH, par),
                [ev_stage_free[par], ("rlw", 64)], tag=f"wp_st{dcH}")
            self.pull_h_n(1)
            pv = self.Q[2][:].bitcast(BF16)
            lastev = None
            for fc in range(c.NDC):
                half = fc % 2
                sl = half * 1024
                wts = [ev_wtr_ev[half]]
                if fc == 0:
                    wts.append(evs)
                self.op("pe", (lambda par, fc, sl, pv: (lambda eng: eng.transpose(
                    pv[:, sl:sl + 128], self.w_stage[par][:, fc * 128:(fc + 1) * 128],
                    self.id_bf[:])))(par, fc, sl, pv), wts, [("pe", 1)], tag=f"wp_tr{dcH}_{fc}")
                evt = ("pe", self.p.cnt["pe"])
                lastev = self.one("act", (lambda dcH, fc, sl, pv: (lambda eng: eng.copy(
                    self.wpT[:, fc * c.D + dcH * 128:fc * c.D + (dcH + 1) * 128],
                    pv[:, sl:sl + 128])))(dcH, fc, sl, pv), [evt], tag=f"wp_tre{dcH}_{fc}")
                ev_wtr_ev[half] = lastev
            ev_stage_free[par] = lastev
        ev_wp = lastev
        ev_ev = [None, None]
        ev_ytc = [None, None]
        ev_xp_free = [None, None]
        self.ev_p3_xrd = [None, None]
        yv = self.y_d_scr[:].rearrange("p (fc s) -> p fc s", fc=c.NDC)
        for tc in range(c.NTC):
            par = tc % 2
            prev_aw = ev_ev[par]
            pp = self.Q[par]
            if tc % 2 == 0:
                self.pull_h_n(1)
            eyt = self.dma("sp", f"yt{par}", (lambda tc, par: (lambda eng: eng.dma_start(
                self.y_tc[par][:].rearrange("p (fc t) -> p fc t", fc=c.NDC),
                yv[:, :, tc * 128:(tc + 1) * 128])))(tc, par), [ev_ytc[par]], tag=f"ytd{tc}")
            for dh in range(c.NDH):
                po = pp[:, dh * 512:dh * 512 + c.DHW]
                for fc in range(c.NDC):
                    wts = []
                    if fc == 0 and dh == 0:
                        wts = [ev_wp, prev_aw, eyt]
                    self.op("pe", (lambda tc, dh, fc, po, par: (lambda eng: eng.matmul(
                        po, self.y_tc[par][:, fc * 128:(fc + 1) * 128],
                        self.wpT[:, fc * c.D + dh * c.DHW:fc * c.D + (dh + 1) * c.DHW],
                        start=(fc == 0), stop=False)))(tc, dh, fc, po, par), wts, (), tag=f"pj{tc}_{dh}_{fc}")
                self.op("pe", (lambda dh, po: (lambda eng: eng.matmul(
                    po, self.ones_f[:], self.bp_row[:, dh * c.DHW:(dh + 1) * c.DHW],
                    start=False, stop=False)))(dh, po), (), (), tag=f"pj_bp{tc}_{dh}")
                self.op("pe", (lambda tc, dh, po: (lambda eng: eng.matmul(
                    po, self.w_fm[:, tc * 128:(tc + 1) * 128],
                    self.b2_bf[:, dh * c.DHW:(dh + 1) * c.DHW],
                    start=False, stop=True)))(tc, dh, po), (),
                    [("pe", 1)] if dh == c.NDH - 1 else (), tag=f"pj_b2{tc}_{dh}")
            evt = ("pe", self.p.cnt["pe"])
            ev_ytc[par] = evt
            exl = self.dma("sp", f"xp{par}", (lambda tc, par: (lambda eng: eng.dma_start(
                self.x_rd[par][:], self.x_d[tc * 128:(tc + 1) * 128, :])))(tc, par),
                [ev_xp_free[par]], tag=f"pj_x{tc}")
            eev = self.one("dve", (lambda par, pp: (lambda eng: eng.tensor_add(
                self.attn_stage[par][:], pp[:, 0:c.D], self.x_rd[par][:])))(par, pp),
                [evt, prev_aw, exl], tag=f"pj_ev{tc}")
            ev_xp_free[par] = eev
            self.ev_p3_xrd[par] = eev
            self.dma("act", f"aw{par}", (lambda tc, par: (lambda eng: eng.dma_start(
                self.attn_d[tc * 128:(tc + 1) * 128, :], self.attn_stage[par][:])))(tc, par),
                [eev], tag=f"pj_wr{tc}")
            ev_ev[par] = (f"aw{par}", self.p.cnt[f"aw{par}"])

    def p4_ffn(self):
        c = self.cfg
        ev_w2_free = [None, None]
        ev_hrd_free = [None, None, None, None]
        ev_psum_tc = [None] * c.NTCT
        ev_stop_tc = [None] * c.NTCT
        ev_ostage = [None, None, None, None]
        hrd_ix = 0
        w2_ix = 0
        for sgi in range(c.NSG):
            e = sgi // c.NCH
            q3g = [ev_psum_tc[6], ev_psum_tc[7]] if c.NTCT > 6 else []
            self.hst["q3_guard"] = [x for x in q3g if x is not None]
            cbi = sgi % 2
            self.load_chunk(sgi)
            wv3 = self.wsl[cbi][:].rearrange("p (a w) -> p a w", w=2 * c.E)
            ev_wslf = self.one("dve", (lambda e, cbi, wv3: (lambda eng: eng.tensor_add(
                self.wslf[cbi][:], wv3[:, :, 2 * e:2 * e + 1].squeeze(2),
                wv3[:, :, 2 * e + 1:2 * e + 2].squeeze(2))))(e, cbi, wv3),
                [self.hst["chunk_loaded"][sgi]], tag=f"wslf{sgi}")
            self.ev_wslf_ev[sgi] = ev_wslf
            self.pull_h_until(min(sgi + 1, c.NSG - 1))
            ge_last = [self.hst["ev_gelu"][0], self.hst["ev_gelu"][1]]
            for dh in range(c.NDH):
                for hp2 in range(c.NP1):
                    bi = w2_ix % 2
                    self.dma("pool", f"w2{bi}", (lambda e, hp2, dh, bi: (lambda eng: eng.dma_start(
                        self.w2buf[bi][:],
                        self.w2b_d[e, dh, hp2].rearrange(
                            "p (hcl f) -> p hcl f", f=c.DHW))))(e, hp2, dh, bi),
                        [ev_w2_free[bi], ("rl", 16 * self.n_relay)], tag=f"w2d{sgi}_{dh}_{hp2}")
                    evw2 = (f"w2{bi}", self.p.cnt[f"w2{bi}"])
                    w2_ix += 1
                    for hcl in range(4):
                        hc = hp2 * 4 + hcl
                        rbi = hrd_ix % 4
                        self.dma("sp", f"hr{rbi}", (lambda sgi, hc, rbi: (lambda eng: eng.dma_start(
                            self.h_rd[rbi][:], self.h_d[sgi, hc * 128:(hc + 1) * 128, :])))(sgi, hc, rbi),
                            [self.ev_hwr_ev[(sgi, hc)], ev_hrd_free[rbi]], tag=f"hrd{sgi}_{dh}_{hc}")
                        evhr = (f"hr{rbi}", self.p.cnt[f"hr{rbi}"])
                        hrd_ix += 1
                        for tc in range(c.NTCT):
                            op = self.Q[tc // 2][:, (tc % 2) * 512:(tc % 2) * 512 + c.DHW]
                            wts = []
                            if tc == 0:
                                wts = [evw2, evhr]
                            if hc == 0:
                                wts.append(ev_psum_tc[tc])
                                if tc >= 6:
                                    wts.append(ge_last[tc - 6])
                            do_inc = (tc == c.NTCT - 1) or (hc == c.HC - 1)
                            self.op("pe", (lambda hcl, tc, rbi, bi, op, hc: (lambda eng: eng.matmul(
                                op, self.h_rd[rbi][:, tc * 128:(tc + 1) * 128],
                                self.w2buf[bi][:, hcl, :],
                                start=(hc == 0), stop=(hc == c.HC - 1))))(hcl, tc, rbi, bi, op, hc),
                                wts,
                                [("pe", 1)] if do_inc else (),
                                tag=f"o_mm{sgi}_{dh}_{hc}_{tc}")
                            if hc == c.HC - 1:
                                ev_stop_tc[tc] = ("pe", self.p.cnt["pe"])
                        evo = ("pe", self.p.cnt["pe"])
                        if hcl == 3:
                            ev_w2_free[bi] = evo
                        ev_hrd_free[rbi] = evo
                order = [6, 7, 0, 1, 2, 3, 4, 5] if c.NTCT == 8 else list(range(c.NTCT))
                for k, tc in enumerate(order):
                    op = self.Q[tc // 2][:, (tc % 2) * 512:(tc % 2) * 512 + c.DHW]
                    o = k % 4
                    wcol = self.wslf[cbi][:, tc:tc + 1]
                    ev = self.one("dve", (lambda op, wcol, o: (lambda eng: eng.tensor_scalar(
                        self.out_stage[o][:], op, wcol, None, ALU.mult)))(op, wcol, o),
                        [ev_stop_tc[tc], ev_ostage[o], ev_wslf], tag=f"stt{sgi}_{dh}_{tc}")
                    ev_psum_tc[tc] = ev
                    self.dma("act", f"ow{o}", (lambda sgi, dh, tc, o: (lambda eng: eng.dma_start(
                        self.outsl_d[sgi * c.TT + tc * 128: sgi * c.TT + (tc + 1) * 128,
                                     dh * c.DHW:(dh + 1) * c.DHW],
                        self.out_stage[o][:])))(sgi, dh, tc, o), [ev], tag=f"osw{sgi}_{dh}_{tc}")
                    ev_ostage[o] = (f"ow{o}", self.p.cnt[f"ow{o}"])
        ow_all = [(s, self.p.cnt[s]) for s in ("ow0", "ow1", "ow2", "ow3") if s in self.p.cnt]
        aw_all = [(s, self.p.cnt[s]) for s in ("aw0", "aw1") if s in self.p.cnt]
        lanes = [self.attn_rd[0], self.attn_rd[1], self.x_rd[0], self.x_rd[1]]
        ev_lane = [None, None, self.ev_p3_xrd[0], self.ev_p3_xrd[1]]
        for tc in range(c.NTC):
            ln = tc % 4
            buf = lanes[ln]
            ea = self.dma("sp", f"al{ln}", (lambda tc, buf: (lambda eng: eng.dma_start(
                buf[:], self.attn_d[tc * 128:(tc + 1) * 128, :])))(tc, buf),
                aw_all + [ev_lane[ln]], tag=f"fa{tc}")
            evg = None
            for k2 in range(2):
                self.dma("pool", f"gp{ln}", (lambda tc, buf, k2: (lambda eng: eng.indirect_dma_start(
                    buf[:], None,
                    self.outsl_d[:],
                    IndirectOffsetOnAxis(self.s12_i[:, k2 * c.NTC + tc: k2 * c.NTC + tc + 1], 0),
                    bounds_check=self._bc_reg, oob_is_err=False,
                    compute_op=ALU.add)))(tc, buf, k2),
                    (ow_all + [ea]) if k2 == 0 else [evg], tag=f"gad{tc}_{k2}")
                evg = (f"gp{ln}", self.p.cnt[f"gp{ln}"])
            self.dma("sp", f"os{ln}", (lambda tc, buf: (lambda eng: eng.dma_start(
                self.y_d[tc * 128:(tc + 1) * 128, :], buf[:])))(tc, buf),
                [evg], tag=f"fout{tc}")
            ev_lane[ln] = (f"os{ln}", self.p.cnt[f"os{ln}"])

    def emit(self):
        nc = self.nc
        semh = self.semh = {s: nc.alloc_semaphore(s) for s in self.p.cnt}
        waited = {e: {} for e in ENGS}

        def run(eng_obj, ename):
            seen = waited[ename]
            for waits, emitfn, incs, tag in self.p.q[ename]:
                for s, v in waits:
                    if seen.get(s, -1) >= v:
                        continue
                    eng_obj.wait_ge(semh[s], v)
                    seen[s] = v
                inst = emitfn(eng_obj)
                for s, n in incs:
                    inst.then_inc(semh[s], n)

        with nc.Block() as blk:
            blk.tensor(lambda eng: run(eng, "pe"))
            blk.scalar(lambda eng: run(eng, "act"))
            blk.vector(lambda eng: run(eng, "dve"))
            blk.gpsimd(lambda eng: run(eng, "pool"))
            blk.sync(lambda eng: run(eng, "sp"))


def build_program(cfg: Cfg):
    return Builder(cfg).build()


_IN_KEYS = {
    "x": "L0_attn_x", "wq": "L0_attn_Wq", "bq": "L0_attn_bq", "wk": "L0_attn_Wk",
    "bk": "L0_attn_bk", "wv": "L0_attn_Wv", "bv": "L0_attn_bv", "wp": "L0_attn_Wp",
    "bp": "L0_attn_bp", "gw": "L0_ffn_gate_w", "gb": "L0_ffn_gate_b",
    "w1": "L0_ffn_w1", "b1": "L0_ffn_b1", "w2": "L0_ffn_w2", "b2": "L0_ffn_b2",
}


def kernel(**inputs):
    cfg = Cfg()
    n = 8
    x = np.ascontiguousarray(np.asarray(inputs["L0_attn_x"], dtype=np.float32))
    common = {k: np.ascontiguousarray(np.asarray(inputs[v], dtype=np.float32))
              for k, v in _IN_KEYS.items() if k != "x"}
    nc = build_program(cfg)
    in_maps = []
    for ci in range(n):
        m = dict(common)
        m["x"] = np.ascontiguousarray(x[ci])
        in_maps.append(m)
    res = run_bass_kernel_spmd(nc, in_maps, core_ids=list(range(n)))
    out = np.stack([res.results[ci]["y"] for ci in range(n)], axis=0)
    return out.astype(np.float32)


# version 18
# speedup vs baseline: 1.1575x; 1.0017x over previous
import math
from dataclasses import dataclass

import numpy as np
import ml_dtypes

import concourse.bass as bass
import concourse.mybir as mybir
from concourse.bass_utils import run_bass_kernel_spmd
from concourse.bass_core_rust import IndirectOffsetOnAxis

F32 = mybir.dt.float32
BF16 = mybir.dt.bfloat16
I32 = mybir.dt.int32
AF = mybir.ActivationFunctionType
ALU = mybir.AluOpType
AX = mybir.AxisListType

ENGS = ("pe", "act", "dve", "pool", "sp")
DMA_SEMS = ("w1", "w2", "wst", "hwr", "hrd", "xld", "out", "cst", "cstp", "awr")


@dataclass
class Cfg:
    D: int = 1024
    S: int = 4096
    E: int = 8
    HF: int = 4096
    TT: int = 1024
    QT: int = 512
    CAP: int = 3072

    @property
    def NDC(self): return self.D // 128
    @property
    def NTC(self): return self.S // 128
    @property
    def NQT(self): return self.S // self.QT
    @property
    def HC(self): return self.HF // 128
    @property
    def NP1(self): return self.HF // 512
    @property
    def NDH(self): return max(1, self.D // 512)
    @property
    def DHW(self): return min(512, self.D)
    @property
    def NTCT(self): return self.TT // 128
    @property
    def NTT(self): return self.S // self.TT
    @property
    def NCH(self): return self.CAP // self.TT
    @property
    def NSG(self): return self.E * self.NCH
    @property
    def RW(self): return self.D + 2 * self.E
    @property
    def scale(self): return 1.0 / math.sqrt(self.D)


class Planner:
    def __init__(self):
        self.q = {e: [] for e in ENGS}
        self.cnt = {}
        self.last_event = {e: None for e in ENGS}

    def sem(self, name):
        self.cnt[name] = 0

    def __post_init_events(self):
        pass

    def op(self, eng, emit, waits=(), incs=(), tag=""):
        w = []
        for x in waits:
            if x is None:
                continue
            s, v = x
            assert s in self.cnt, s
            w.append((s, int(v)))
        out = []
        for s, n in incs:
            if s not in self.cnt:
                self.cnt[s] = 0
            self.cnt[s] += n
            out.append((s, self.cnt[s]))
        if out:
            self.last_event[eng] = out[0]
        self.q[eng].append([tuple(w), emit, tuple(incs), tag])
        return out

    def inc_last(self, eng, sem):
        assert self.q[eng], eng
        last = self.q[eng][-1]
        assert not last[2], last
        if sem not in self.cnt:
            self.cnt[sem] = 0
        self.cnt[sem] += 1
        last[2] = ((sem, 1),)
        self.last_event[eng] = (sem, self.cnt[sem])
        return 1

    def check(self):
        for e in ENGS:
            for waits, _, _, tag in self.q[e]:
                for s, v in waits:
                    assert v <= self.cnt[s], \
                        f"unreachable wait {s}>={v} (final {self.cnt[s]}) at {e}:{tag}"
        cur = {s: 0 for s in self.cnt}
        ptr = {e: 0 for e in ENGS}
        total = sum(len(self.q[e]) for e in ENGS)
        done = 0
        while done < total:
            progressed = False
            for e in ENGS:
                while ptr[e] < len(self.q[e]):
                    waits, _, incs, tag = self.q[e][ptr[e]]
                    if all(cur[s] >= v for s, v in waits):
                        for s, n in incs:
                            cur[s] += n
                        ptr[e] += 1
                        done += 1
                        progressed = True
                    else:
                        break
            if not progressed:
                details = []
                for e in ENGS:
                    if ptr[e] < len(self.q[e]):
                        waits, _, _, tag = self.q[e][ptr[e]]
                        blocked = [(s, v, cur[s]) for s, v in waits if cur[s] < v]
                        details.append(f"{e}@{ptr[e]} tag={tag} blocked_on={blocked}")
                raise RuntimeError("DEADLOCK in plan:\n" + "\n".join(details))


class Builder:
    def __init__(self, cfg: Cfg):
        self.cfg = cfg
        nc = bass.Bass("TRN2", target_bir_lowering=False)
        self.nc = nc
        self.p = Planner()
        for s in ENGS:
            self.p.sem(s)

        c = cfg
        dt_ = nc.dram_tensor
        self.x_d = dt_("x", [c.S, c.D], F32, kind="ExternalInput")
        self.wq_d = dt_("wq", [c.D, c.D], F32, kind="ExternalInput")
        self.wk_d = dt_("wk", [c.D, c.D], F32, kind="ExternalInput")
        self.wv_d = dt_("wv", [c.D, c.D], F32, kind="ExternalInput")
        self.wp_d = dt_("wp", [c.D, c.D], F32, kind="ExternalInput")
        self.bq_d = dt_("bq", [c.D], F32, kind="ExternalInput")
        self.bk_d = dt_("bk", [c.D], F32, kind="ExternalInput")
        self.bv_d = dt_("bv", [c.D], F32, kind="ExternalInput")
        self.bp_d = dt_("bp", [c.D], F32, kind="ExternalInput")
        self.gw_d = dt_("gw", [c.E, c.D], F32, kind="ExternalInput")
        self.gb_d = dt_("gb", [c.E], F32, kind="ExternalInput")
        self.w1_d = dt_("w1", [c.E, c.D, c.HF], F32, kind="ExternalInput")
        self.b1_d = dt_("b1", [c.E, c.HF], F32, kind="ExternalInput")
        self.w2_d = dt_("w2", [c.E, c.HF, c.D], F32, kind="ExternalInput")
        self.b2_d = dt_("b2", [c.E, c.D], F32, kind="ExternalInput")
        self.y_d = dt_("y", [c.S, c.D], F32, kind="ExternalOutput")
        self.attn_d = dt_("attn_scr", [c.S, c.D], F32, kind="Internal")
        self.y_d_scr = dt_("y_scr", [128, c.NDC * c.S], BF16, kind="Internal")
        self.h_d = dt_("h_scr", [c.NSG, c.HF, c.TT], BF16, kind="Internal")
        self.xrow_d = dt_("xrow_scr", [c.S, c.RW], BF16, kind="Internal")
        self.xg_d = dt_("xg_scr", [c.E * c.CAP, c.RW], BF16, kind="Internal")
        self.outsl_d = dt_("outsl_scr", [c.E * c.CAP, c.D], F32, kind="Internal")
        self.w1b_d = dt_("w1b_scr", [c.E, c.NP1, 128, c.NDC * 4 * 128], BF16, kind="Internal")
        self.wqb_d = dt_("wqb_scr", [c.D, c.D], BF16, kind="Internal")
        self.wkb_d = dt_("wkb_scr", [c.D, c.D], BF16, kind="Internal")
        self.wvb_d = dt_("wvb_scr", [c.D, c.D], BF16, kind="Internal")
        self.wpb_d = dt_("wpb_scr", [c.D, c.D], BF16, kind="Internal")
        self.w2b_d = dt_("w2b_scr", [c.E, c.NP1, 128, 4 * c.D], BF16, kind="Internal")

        self.id_bf_d = nc.inline_tensor(np.eye(128, dtype=ml_dtypes.bfloat16), "id_bf_c")
        self.id_f_d = nc.inline_tensor(np.eye(128, dtype=np.float32), "id_f_c")
        self.ones_f_d = nc.inline_tensor(np.ones((1, 128), dtype=np.float32), "ones_f_c")
        m = np.zeros((128, 4, 512), dtype=np.float32)
        for d in range(4):
            for kp in range(128):
                qf0 = kp + 128 * d
                if qf0 < 512:
                    m[kp, d, qf0:] = 1.0
        self.mask_d = nc.inline_tensor(m.reshape(128, 2048).astype(ml_dtypes.bfloat16), "mask_c")
        ecap = np.zeros((128, c.NTC * c.E), dtype=np.float32)
        for e in range(c.E):
            ecap[:, e::c.E] = e * c.CAP
        self.ecap_d = nc.inline_tensor(ecap, "ecap_c")

        ARENA = 208960
        BASE = 16512
        ah = nc.alloc_sbuf_tensor("arena", [128, ARENA], mybir.dt.uint8)
        assert nc.lookup_mloc(ah).addr == BASE, nc.lookup_mloc(ah).addr
        self.ARENA = ARENA
        A_raw = nc.alloc_sbuf_tensor_at

        def A(name, shape, dtype, offset):
            return A_raw(name, shape, dtype, offset=offset + BASE)
        self._off = 0

        def at(name, shape, dtype):
            self._off = (self._off + 63) // 64 * 64
            t = A(name, shape, dtype, offset=self._off)
            self._off += int(np.prod(shape[1:])) * mybir.dt.size(dtype)
            return t

        self.id_bf = at("id_bf_s", [128, 128], BF16)
        self.id_f = at("id_f_s", [128, 128], F32)
        self.ones_f = at("ones_f_s", [1, 128], F32)
        self.masks = at("masks_s", [128, 2048], BF16)
        self.bqkT = at("bqkT", [128, 2 * c.NDC], F32)
        self.b1T = at("b1T", [128, c.HC * c.E], F32)
        self.b2_bf = at("b2_bf", [c.E, c.D], BF16)
        self.bp_row = at("bp_row", [1, c.D], F32)
        self.bv_row = at("bv_row", [1, c.D], F32)
        self.gb_s = at("gb_s", [c.E, 1], F32)
        self.gate_wT = at("gate_wT", [128, c.NDC * c.E], BF16)
        self.w_gates_tm = at("w_gates_tm", [128, c.NTC * c.E], F32)
        self.w_fm = at("w_fm", [c.E, c.S], BF16)
        self.negb = at("negb", [128, c.NDC], F32)
        self.scr = at("scr", [128, 64], F32)
        self.consts = at("consts", [128, 2], F32)
        self.scr1 = at("scr1", [1, 64], F32)
        self.recip64 = at("recip64", [64, 512], F32)
        self.sumrow = at("sumrow", [1, 512], F32)
        self.sumrow2 = at("sumrow2", [1, 512], F32)

        A0 = self._off = (self._off + 63) // 64 * 64
        self.xn = A("xn", [128, c.NDC * c.S], BF16, offset=A0)
        o = A0
        def atA(name, shape, dtype):
            nonlocal o
            o = (o + 63) // 64 * 64
            t = A(name, shape, dtype, offset=o)
            o += int(np.prod(shape[1:])) * mybir.dt.size(dtype)
            return t
        self.xga = [atA(f"xga{i}", [128, c.NDC * c.TT], BF16) for i in range(2)]
        self.w1buf = [atA(f"w1buf{i}", [128, c.NDC, 4, 128], BF16) for i in range(2)]
        self.h_rd = [atA(f"h_rd{i}", [128, 512], BF16) for i in range(4)]
        self.h_wr = [atA(f"h_wr{i}", [128, c.TT], BF16) for i in range(4)]
        self.wsl = [atA(f"wsl{i}", [128, (c.TT // 128) * 2 * c.E], BF16) for i in range(2)]
        self._off = max(o, A0 + c.NDC * c.S * 2)

        self.w2buf = [at(f"w2buf{i}", [128, 4, c.D], BF16) for i in range(2)]
        self.y_stage = [at(f"y_stage{i}", [64, 512], BF16) for i in range(2)]
        self.x_rd = [at(f"x_rd{i}", [128, c.D], F32) for i in range(2)]
        self.attn_rd = [at(f"attn_rd{i}", [128, c.D], F32) for i in range(2)]
        self.y_tc = [at(f"y_tc{i}", [128, c.NDC * 128], BF16) for i in range(2)]
        self.out_stage = [at(f"out_stage{i}", [128, 512], F32) for i in range(4)]
        self.xrb = [at(f"xrb{i}", [128, c.RW], BF16) for i in range(2)]
        self.idxs_i = at("idxs_i", [128, c.NTC * c.E], I32)
        self.s12_i = at("s12_i", [128, 2 * c.NTC], I32)
        self.sc_a = at("sc_a", [128, c.NTC * c.E], F32)
        self.sc_b = at("sc_b", [128, c.NTC * c.E], F32)
        self.sc_c = at("sc_c", [128, c.NTC * c.E], F32)
        self.sc_d = at("sc_d", [128, c.NTC * c.E], F32)
        self.sc_e = at("sc_e", [128, c.NTC * c.E], F32)
        self.sc_f = at("sc_f", [128, c.NTC * c.E], F32)
        self.sc_sm = at("sc_sm", [128, 256], F32)
        self.pp_a = at("pp_a", [8, 256], F32)
        self.pp_b = at("pp_b", [8, 256], F32)
        self.ecap_s = at("ecap_s", [128, c.NTC * c.E], F32)
        self.whi_bf = at("whi_bf", [128, c.NTC * c.E], BF16)
        self.wlo_bf = at("wlo_bf", [128, c.NTC * c.E], BF16)
        self.w2x = at("w2x", [128, c.NTC * 2 * c.E], BF16)
        self.wslf = [at(f"wslf{i}", [128, c.NTCT], F32) for i in range(2)]

        C0 = self._off = (self._off + 63) // 64 * 64
        self.x_tm = [A(f"x_tm{i}", [128, c.D], F32, offset=C0 + i * c.D * 4) for i in range(2)]
        o = C0 + 2 * c.D * 4
        self.xn_tm = [A(f"xn_tm{i}", [128, c.D], BF16, offset=o + i * c.D * 2) for i in range(2)]
        o += 2 * c.D * 2
        self.sq = [A(f"sq{i}", [128, c.D], F32, offset=o + i * c.D * 4) for i in range(2)]
        o += 2 * c.D * 4
        self.b1_stage = A("b1_stage", [c.E, c.HF], F32, offset=o); o += c.HF * 4
        self.gw_stage = A("gw_stage", [c.E, c.D], F32, offset=o); o += c.D * 4
        self.bqk_stage = A("bqk_stage", [8, 256], F32, offset=o); o += 1024
        p0_end = o
        self.gates_fm = A("gates_fm", [c.E, c.S], F32, offset=C0)
        o = C0 + c.S * 4
        self.g_tm = A("g_tm", [128, c.NTC * c.E], F32, offset=o); o += c.NTC * c.E * 4
        self.tk = []
        for i in range(4):
            self.tk.append(A(f"tk{i}", [128, c.NTC * c.E], F32, offset=o)); o += c.NTC * c.E * 4
        self.tks = A("tks", [128, 4 * c.NTC], F32, offset=o); o += 4 * c.NTC * 4
        p1_end = o
        self.Q_g = A("Q_g", [128, c.S], BF16, offset=C0)
        self.K_g = A("K_g", [128, c.S], BF16, offset=C0 + c.S * 2)
        o = C0 + 2 * c.S * 2
        vsz = ((c.S // 128) * 65 * 2 + 63) // 64 * 64
        self.v_tm = [A(f"v_tm{i}", [128, (c.S // 128) * 65], BF16, offset=o + i * vsz) for i in range(2)]
        o += 2 * vsz
        self.att_sb = [A(f"att_sb{i}", [128, 1024], BF16, offset=o + i * 2048) for i in range(4)]
        o += 8192
        self.wqT_g = A("wqT_g", [128, c.D], BF16, offset=o); o += c.D * 2
        self.wkT_g = A("wkT_g", [128, c.D], BF16, offset=o); o += c.D * 2
        self.wvT_g = A("wvT_g", [128, c.D], BF16, offset=o); o += c.D * 2
        self.w_stage = [A(f"w_stage{i}", [128, c.D], BF16, offset=o + i * c.D * 2) for i in range(2)]
        o += 2 * c.D * 2
        p2_end = o
        self.wpT = A("wpT", [128, c.NDC * c.D], BF16, offset=C0)
        o = C0 + c.NDC * c.D * 2
        self.attn_stage = [A(f"attn_stage{i}", [128, c.D], F32, offset=o + i * c.D * 4) for i in range(2)]
        o += 2 * c.D * 4
        p3_end = o
        self.out_acc = A("out_acc", [128, c.NTCT * c.D], F32, offset=C0)
        p4_end = C0 + c.NTCT * c.D * 4
        self.sbuf_top = max(p0_end, p1_end, p2_end, p3_end, p4_end)
        assert self.sbuf_top <= ARENA, f"SBUF over budget: {self.sbuf_top} > {ARENA}"

        self.Q = [nc.alloc_psum_tensor(f"psq{i}", [128, 1024], F32) for i in range(4)]

    def op(self, eng, emit, waits=(), incs=(), tag=""):
        return self.p.op(eng, emit, waits, incs, tag)

    def one(self, eng, emit, waits=(), tag=""):
        return self.op(eng, emit, waits, [(eng, 1)], tag)[0]

    def dma(self, eng, sem, emit, waits=(), tag=""):
        return self.op(eng, emit, waits, [(sem, 16)], tag)[0]

    def barrier(self, tag="bar", exclude=()):
        snap = dict(self.p.cnt)
        self._nbar = getattr(self, "_nbar", 0) + 1
        bs = f"bar{self._nbar}"
        self.op("pool", lambda eng: eng.drain(), (), [(bs, 1)], tag=tag + "_a")
        waits = [(bs, 1)]
        for e in ENGS:
            if e != "pool" and self.p.q[e] and not self.p.q[e][-1][2]:
                self.p.inc_last(e, e)
            if self.p.last_event[e] is not None:
                waits.append(self.p.last_event[e])
        waits += [(s, v) for s, v in snap.items()
                  if s not in ENGS and not s.startswith("bar") and v > 0
                  and s not in exclude]
        wm = {}
        for s, v in waits:
            wm[s] = max(wm.get(s, 0), v)
        waits = sorted(wm.items())
        for e in ENGS:
            self.op(e, lambda eng: eng.nop(), waits, (), tag=tag + "_b")

    def build(self):
        self.p0_consts()
        self.p0_ln()
        self.barrier("after_ln")
        self.p1_gates()
        self.barrier("after_gates", exclude=("scp0", "scp1", "xb0", "xb1"))
        self.make_hgen()
        self.mk_relay()
        self.p2_attn()
        self.relay_drip(10 ** 9)
        self.barrier("after_attn")
        self.p3_proj()
        self.barrier("after_proj")
        self.p4_ffn()
        self.op("sp", lambda eng: eng.nop(),
                [(s, v) for s, v in self.p.cnt.items() if s not in ENGS and v > 0],
                [("sp", 1)], tag="final_wait")
        self.p.check()
        self.emit()
        return self.nc

    def p0_consts(self):
        c = self.cfg
        for dst, srcd in ((self.wqb_d, self.wq_d), (self.wkb_d, self.wk_d),
                          (self.wvb_d, self.wv_d), (self.wpb_d, self.wp_d)):
            self.dma("pool", "rlw", (lambda dst, srcd: (lambda eng: eng.dma_start(
                dst[:], srcd[:])))(dst, srcd), [], tag="rlw")
        loads = [
            (self.id_bf[:], self.id_bf_d[:]),
            (self.id_f[:], self.id_f_d[:]),
            (self.ones_f[:], self.ones_f_d[:]),
            (self.masks[:], self.mask_d[:]),
            (self.bp_row[:], self.bp_d[:].unsqueeze(0)),
            (self.bv_row[:], self.bv_d[:].unsqueeze(0)),
            (self.gb_s[:], self.gb_d[:].unsqueeze(1)),
            (self.b1_stage[:], self.b1_d[:]),
            (self.gw_stage[:], self.gw_d[:]),
            (self.bqk_stage[0:c.NDC, 0:128], self.bq_d[:].rearrange("(a b) -> a b", b=128)),
            (self.bqk_stage[0:c.NDC, 128:256], self.bk_d[:].rearrange("(a b) -> a b", b=128)),
            (self.ecap_s[:], self.ecap_d[:]),
        ]
        for dst, src in loads:
            self.dma("sp", "cst", (lambda d, s: (lambda eng: eng.dma_start(d, s)))(dst, src), tag="cst")
        self.op("pool", lambda eng: eng.memset(self.consts[:, 0:1], 1e-5), (), [("ms0", 1)], tag="c_eps")
        self.op("pool", lambda eng: eng.memset(self.consts[:, 1:2], 1.0), (), [("ms1", 1)], tag="c_one")
        self.ev_consts = ("ms0", 1)
        self.dma("pool", "cstp", lambda eng: eng.dma_start(self.b2_bf[:], self.b2_d[:]), tag="b2")
        cst_all = ("cst", self.p.cnt["cst"])

        tp = self.Q[2]
        for j, colbase in ((0, 0), (1, c.NDC)):
            evt = self.op("pe", (lambda j: (lambda eng: eng.transpose(
                self.Q[2][0:128, 0:c.NDC], self.bqk_stage[0:c.NDC, 128 * j:128 * (j + 1)],
                self.id_f[0:c.NDC, 0:c.NDC])))(j),
                [cst_all, ("act", self.p.cnt["act"])], [("pe", 1)], tag=f"bqk_tr{j}")[0]
            self.one("act", (lambda cb: (lambda eng: eng.copy(
                self.bqkT[:, cb:cb + c.NDC], self.Q[2][0:128, 0:c.NDC])))(colbase), [evt], tag=f"bqk_ev{j}")
        for hc in range(c.HC):
            evt = self.op("pe", (lambda hc: (lambda eng: eng.transpose(
                self.Q[2][0:128, 8:8 + c.E], self.b1_stage[0:c.E, hc * 128:(hc + 1) * 128],
                self.id_f[0:c.E, 0:c.E])))(hc),
                [cst_all, ("act", self.p.cnt["act"])], [("pe", 1)], tag=f"b1_tr{hc}")[0]
            self.one("act", (lambda hc: (lambda eng: eng.copy(
                self.b1T[:, hc * c.E:(hc + 1) * c.E], self.Q[2][0:128, 8:8 + c.E])))(hc), [evt], tag=f"b1_ev{hc}")
        for dc in range(c.NDC):
            evt = self.op("pe", (lambda dc: (lambda eng: eng.transpose(
                self.Q[2][0:128, 16:16 + c.E], self.gw_stage[0:c.E, dc * 128:(dc + 1) * 128],
                self.id_f[0:c.E, 0:c.E])))(dc),
                [cst_all, ("act", self.p.cnt["act"])], [("pe", 1)], tag=f"gw_tr{dc}")[0]
            self.one("act", (lambda dc: (lambda eng: eng.copy(
                self.gate_wT[:, dc * c.E:(dc + 1) * c.E], self.Q[2][0:128, 16:16 + c.E])))(dc),
                [evt], tag=f"gw_ev{dc}")

    def p0_ln(self):
        c = self.cfg
        NP = 1.0 / c.D
        ev_dve = [None, None]
        ev_sq = [None, None]
        ev_evac = [None, None]
        ev_var = [None, None]
        ev_trans = [None, None]
        ev_xrw = [None, None]
        for tc in range(c.NTC):
            par = tc % 2
            waits = [ev_dve[par], ev_sq[par]]
            evx = self.dma("sp" if par == 0 else "pool", f"xl{par}",
                (lambda tc, par: (lambda eng: eng.dma_start(
                    self.x_tm[par][:], self.x_d[tc * 128:(tc + 1) * 128, :])))(tc, par),
                waits, tag=f"ln_x{tc}")
            o = par * 8
            m_sl, nm_sl, v_sl, sd_sl, rs_sl = (self.scr[:, o + i:o + i + 1] for i in range(5))
            e1 = self.one("dve", (lambda par, m_sl: (lambda eng: eng.reduce_sum(
                m_sl, self.x_tm[par][:], axis=AX.X)))(par, m_sl), [evx], tag=f"ln_mean{tc}")
            e2 = self.one("act", (lambda m_sl, nm_sl: (lambda eng: eng.mul(nm_sl, m_sl, -NP)))(m_sl, nm_sl),
                          [e1], tag=f"ln_negm{tc}")
            e3 = self.one("act", (lambda par, nm_sl: (lambda eng: eng.activation(
                self.sq[par][:], self.x_tm[par][:], AF.Square, bias=nm_sl)))(par, nm_sl),
                [e2, ev_var[par]], tag=f"ln_sq{tc}")
            ev_sq[par] = e3
            e4 = self.one("dve", (lambda par, v_sl: (lambda eng: eng.reduce_sum(
                v_sl, self.sq[par][:], axis=AX.X)))(par, v_sl), [e3], tag=f"ln_var{tc}")
            ev_var[par] = e4
            e5 = self.one("act", (lambda v_sl, sd_sl: (lambda eng: eng.activation(
                sd_sl, v_sl, AF.Sqrt, bias=self.consts[:, 0:1], scale=NP)))(v_sl, sd_sl),
                [e4, self.ev_consts if tc == 0 else None], tag=f"ln_std{tc}")
            e6 = self.one("dve", (lambda sd_sl, rs_sl: (lambda eng: eng.reciprocal(rs_sl, sd_sl)))(sd_sl, rs_sl),
                          [e5], tag=f"ln_rstd{tc}")
            e7 = self.one("dve", (lambda par, nm_sl, rs_sl: (lambda eng: eng.tensor_scalar(
                self.xn_tm[par][:], self.x_tm[par][:], nm_sl, rs_sl, ALU.add, ALU.mult)))(par, nm_sl, rs_sl),
                [e6, ev_trans[par], ev_xrw[par]], tag=f"ln_xn{tc}")
            ev_dve[par] = e7
            ev_xrw[par] = self.dma("sp", f"xw{par}", (lambda tc, par: (lambda eng: eng.dma_start(
                self.xrow_d[tc * 128:(tc + 1) * 128, 0:c.D], self.xn_tm[par][:])))(tc, par),
                [e7], tag=f"xrw{tc}")
            pv = self.Q[par][:].bitcast(BF16)
            for dc in range(c.NDC):
                self.op("pe", (lambda par, dc, pv: (lambda eng: eng.transpose(
                    pv[:, dc * 128:(dc + 1) * 128],
                    self.xn_tm[par][:, dc * 128:(dc + 1) * 128], self.id_bf[:])))(par, dc, pv),
                    [e7, ev_evac[par]] if dc == 0 else (),
                    [("pe", 1)] if dc == c.NDC - 1 else (), tag=f"ln_tr{tc}_{dc}")
            evt = ("pe", self.p.cnt["pe"])
            ev_trans[par] = evt
            dst = self.xn[:].rearrange("p (dc s) -> p dc s", dc=c.NDC)[:, :, tc * 128:(tc + 1) * 128]
            src = pv[:, 0:c.NDC * 128].rearrange("p (dc t) -> p dc t", dc=c.NDC)
            ev_evac[par] = self.one("act", (lambda dst, src: (lambda eng: eng.copy(dst, src)))(dst, src),
                                    [evt], tag=f"ln_ev{tc}")

    def p1_gates(self):
        c = self.cfg
        ns = c.S // 512
        ev_ev = [None, None]
        for s in range(ns):
            par = s % 2
            gp = self.Q[1][0:c.E, par * 512:(par + 1) * 512]
            for dc in range(c.NDC):
                self.op("pe", (lambda s, dc, gp: (lambda eng: eng.matmul(
                    gp, self.gate_wT[:, dc * c.E:(dc + 1) * c.E],
                    self.xn[:, dc * c.S + s * 512: dc * c.S + (s + 1) * 512],
                    start=(dc == 0), stop=(dc == c.NDC - 1))))(s, dc, gp),
                    [ev_ev[par]] if dc == 0 else (),
                    [("pe", 1)] if dc == c.NDC - 1 else (), tag=f"g_mm{s}_{dc}")
            evt = ("pe", self.p.cnt["pe"])
            ev_ev[par] = self.one("act", (lambda s, gp: (lambda eng: eng.activation(
                self.gates_fm[:, s * 512:(s + 1) * 512], gp, AF.Identity, bias=self.gb_s[:])))(s, gp),
                [evt], tag=f"g_ev{s}")
        ga0, ga1 = ev_ev[0], ev_ev[1]
        ev_tre = [None, None]
        for tc in range(c.NTC):
            par = tc % 2
            sl = 32 + par * 512
            evt = self.op("pe", (lambda tc, sl: (lambda eng: eng.transpose(
                self.Q[2][0:128, sl:sl + c.E], self.gates_fm[0:c.E, tc * 128:(tc + 1) * 128],
                self.id_f[0:c.E, 0:c.E])))(tc, sl),
                [ga0, ga1, ev_tre[par]], [("pe", 1)], tag=f"g_tr{tc}")[0]
            ev_tre[par] = self.one("act", (lambda tc, sl: (lambda eng: eng.copy(
                self.g_tm[:, tc * c.E:(tc + 1) * c.E], self.Q[2][0:128, sl:sl + c.E])))(tc, sl),
                [evt], tag=f"g_tre{tc}")
        gtm = [ev_tre[0], ev_tre[1]]
        NE, NT = c.E, c.NTC
        g3 = self.g_tm[:].rearrange("p (t e) -> p t e", e=NE)
        mm1 = self.tks[:, 0:NT]
        mm2 = self.tks[:, NT:2 * NT]
        em2 = self.tks[:, 2 * NT:3 * NT]
        rden = self.tks[:, 3 * NT:4 * NT]
        d_, eq_, msk_, wun_ = (t[:] for t in self.tk[0:4])
        d3 = d_.rearrange("p (t e) -> p t e", e=NE)
        e1 = self.one("dve", lambda eng: eng.reduce_max(mm1, g3, axis=AX.X), gtm, tag="tk_m1")
        e2 = self.one("dve", lambda eng: eng.tensor_tensor(
            d3, g3, mm1.unsqueeze(2).broadcast_to([128, NT, NE]), ALU.subtract), [e1], tag="tk_d")
        e3 = self.one("dve", lambda eng: eng.tensor_single_scalar(eq_, d_, 0.0, ALU.is_equal), [e2], tag="tk_eq")
        e4 = self.one("dve", lambda eng: eng.scalar_tensor_tensor(
            msk_, eq_, -1e30, d_, ALU.mult, ALU.add), [e3], tag="tk_msk")
        e5 = self.one("dve", lambda eng: eng.reduce_max(
            mm2, msk_.rearrange("p (t e) -> p t e", e=NE), axis=AX.X), [e4], tag="tk_m2")
        e6 = self.one("act", lambda eng: eng.activation(eq_, d_, AF.Exp), [e2, e4], tag="tk_expd")
        expd = eq_
        e7 = self.one("dve", lambda eng: eng.tensor_tensor(
            msk_.rearrange("p (t e) -> p t e", e=NE), d3,
            mm2.unsqueeze(2).broadcast_to([128, NT, NE]), ALU.is_ge), [e5], tag="tk_ge")
        ge_ = msk_
        e8 = self.one("dve", lambda eng: eng.tensor_tensor(wun_, expd, ge_, ALU.mult), [e6, e7], tag="tk_wun")
        e9 = self.one("act", lambda eng: eng.activation(em2, mm2, AF.Exp), [e5], tag="tk_em2")
        e10 = self.one("act", lambda eng: eng.add(em2, em2, 1.0), [e9], tag="tk_den")
        e11 = self.one("dve", lambda eng: eng.reciprocal(rden, em2), [e10], tag="tk_rden")
        e12 = self.one("dve", lambda eng: eng.tensor_tensor(
            self.w_gates_tm[:].rearrange("p (t e) -> p t e", e=NE),
            wun_.rearrange("p (t e) -> p t e", e=NE),
            rden.unsqueeze(2).broadcast_to([128, NT, NE]), ALU.mult), [e8, e11], tag="tk_w")
        ev_tre = [None, None]
        for tc in range(c.NTC):
            par = tc % 2
            sl = 48 + par * 512
            evt = self.op("pe", (lambda tc, sl: (lambda eng: eng.transpose(
                self.Q[2][0:c.E, sl:sl + 128], self.w_gates_tm[:, tc * c.E:(tc + 1) * c.E],
                self.id_f[:])))(tc, sl), [e12, ev_tre[par]], [("pe", 1)], tag=f"w_tr{tc}")[0]
            ev_tre[par] = self.one("act", (lambda tc, sl: (lambda eng: eng.copy(
                self.w_fm[0:c.E, tc * 128:(tc + 1) * 128], self.Q[2][0:c.E, sl:sl + 128])))(tc, sl),
                [evt], tag=f"w_tre{tc}")
        eh1 = self.one("dve", lambda eng: eng.tensor_copy(self.whi_bf[:], self.w_gates_tm[:]),
                       [e12], tag="wsp1")
        eh2 = self.one("dve", lambda eng: eng.tensor_copy(self.sc_a[:], self.whi_bf[:]),
                       [eh1], tag="wsp2")
        eh3 = self.one("dve", lambda eng: eng.tensor_sub(self.sc_b[:], self.w_gates_tm[:], self.sc_a[:]),
                       [eh2], tag="wsp3")
        eh4 = self.one("dve", lambda eng: eng.tensor_copy(self.wlo_bf[:], self.sc_b[:]),
                       [eh3], tag="wsp4")
        v2 = self.w2x[:].rearrange("p (t e two) -> p t e two", e=c.E, two=2)
        eh5 = self.one("dve", lambda eng: eng.tensor_copy(
            v2[:, :, :, 0], self.whi_bf[:].rearrange("p (t e) -> p t e", e=c.E)), [eh4], tag="wsp5")
        eh6 = self.one("dve", lambda eng: eng.tensor_copy(
            v2[:, :, :, 1], self.wlo_bf[:].rearrange("p (t e) -> p t e", e=c.E)), [eh5], tag="wsp6")
        for tc in range(c.NTC):
            self.dma("sp", "ww", (lambda tc: (lambda eng: eng.dma_start(
                self.xrow_d[tc * 128:(tc + 1) * 128, c.D:c.D + 2 * c.E],
                self.w2x[:, tc * 2 * c.E:(tc + 1) * 2 * c.E])))(tc),
                [eh6], tag=f"wwr{tc}")
        self.p15_route(eh6)


    def mk_relay(self):
        c = self.cfg
        self.n_relay = c.E * c.NP1 * 2

        def gen():
            for e in range(c.E):
                for hp in range(c.NP1):
                    yield ("w1", e, hp, 0)
                for hp in range(c.NP1):
                    yield ("w2", e, hp, 0)
        self._relay_it = gen()

    def relay_drip(self, n):
        c = self.cfg
        for _ in range(n):
            try:
                kind, e, hp, dh = next(self._relay_it)
            except StopIteration:
                return
            if kind == "w1":
                self.dma("pool", "rl", (lambda e, hp: (lambda eng: eng.dma_start(
                    self.w1b_d[e, hp].rearrange("p (dc hcl f) -> p dc hcl f", dc=c.NDC, f=128),
                    self.w1_d[e, :, hp * 512:(hp + 1) * 512].rearrange(
                        "(dc p) (hcl f) -> p dc hcl f", p=128, f=128))))(e, hp),
                    [], tag=f"rl1_{e}_{hp}")
            else:
                self.dma("pool", "rl", (lambda e, hp: (lambda eng: eng.dma_start(
                    self.w2b_d[e, hp].rearrange("p (hcl d) -> p hcl d", d=c.D),
                    self.w2_d[e, hp * 512:(hp + 1) * 512, :].rearrange(
                        "(hcl p) d -> p hcl d", p=128))))(e, hp),
                    [], tag=f"rl2_{e}_{hp}")

    def p15_route(self, e12):
        c = self.cfg
        NE, NT = c.E, c.NTC
        BIG = 60000.0
        A3 = lambda t: t[:].rearrange("p (t e) -> p t e", e=NE)
        ev = e12

        def dve(emit, extra=(), tag=""):
            nonlocal ev
            ev = self.one("dve", emit, [ev] + list(extra), tag=tag)
            return ev
        dve(lambda eng: eng.tensor_single_scalar(self.sc_d[:], self.w_gates_tm[:], 0.0, ALU.is_gt), tag="rt_mk")
        dve(lambda eng: eng.tensor_copy(self.sc_a[:], self.sc_d[:]), tag="rt_cp")
        cur, nxt = self.sc_a, self.sc_b
        k = 1
        while k < NT:
            dve((lambda cur, nxt, k: (lambda eng: eng.tensor_copy(
                A3(nxt)[:, 0:k, :], A3(cur)[:, 0:k, :])))(cur, nxt, k), tag=f"rt_s{k}a")
            dve((lambda cur, nxt, k: (lambda eng: eng.tensor_tensor(
                A3(nxt)[:, k:NT, :], A3(cur)[:, k:NT, :], A3(cur)[:, 0:NT - k, :],
                ALU.add)))(cur, nxt, k), tag=f"rt_s{k}b")
            cur, nxt = nxt, cur
            k *= 2
        incl = cur
        dve((lambda incl: (lambda eng: eng.tensor_sub(self.sc_c[:], incl[:], self.sc_d[:])))(incl), tag="rt_ex")
        ept = self.op("pe", (lambda incl: (lambda eng: eng.transpose(
            self.Q[2][0:NE, 256:384], incl[:, (NT - 1) * NE:NT * NE], self.id_f[:])))(incl),
            [ev, ("act", self.p.cnt["act"])], [("pe", 1)], tag="rt_tr1")[0]
        ea = self.one("act", lambda eng: eng.copy(self.pp_a[0:NE, 0:128], self.Q[2][0:NE, 256:384]),
                      [ept], tag="rt_ev1")
        ev = ea
        cur, nxt = self.pp_a, self.pp_b
        k = 1
        while k < 128:
            dve((lambda cur, nxt, k: (lambda eng: eng.tensor_copy(
                nxt[0:NE, 0:k], cur[0:NE, 0:k])))(cur, nxt, k), tag=f"rt_p{k}a")
            dve((lambda cur, nxt, k: (lambda eng: eng.tensor_tensor(
                nxt[0:NE, k:128], cur[0:NE, k:128], cur[0:NE, 0:128 - k],
                ALU.add)))(cur, nxt, k), tag=f"rt_p{k}b")
            cur, nxt = nxt, cur
            k *= 2
        inclp, other = cur, nxt
        evm = self.one("pool", (lambda other: (lambda eng: eng.memset(other[0:NE, 0:1], 0.0)))(other),
                       [ev], tag="rt_ms")
        dve((lambda inclp, other: (lambda eng: eng.tensor_copy(
            other[0:NE, 1:128], inclp[0:NE, 0:127])))(inclp, other), [evm], tag="rt_sh")
        ep2 = self.op("pe", (lambda other: (lambda eng: eng.transpose(
            self.Q[2][0:128, 384:384 + NE], other[0:NE, 0:128], self.id_f[0:NE, 0:NE])))(other),
            [ev, ("act", self.p.cnt["act"])], [("pe", 1)], tag="rt_tr2")[0]
        ev = self.one("act", lambda eng: eng.copy(self.sc_sm[:, 0:NE], self.Q[2][0:128, 384:384 + NE]),
                      [ep2], tag="rt_ev2")
        dve(lambda eng: eng.tensor_tensor(
            A3(self.sc_b), A3(self.sc_c),
            self.sc_sm[:, 0:NE].unsqueeze(1).broadcast_to([128, NT, NE]), ALU.add), tag="rt_pt")
        dve(lambda eng: eng.tensor_single_scalar(self.sc_c[:], self.sc_b[:], float(c.CAP), ALU.is_lt), tag="rt_ft")
        dve(lambda eng: eng.tensor_tensor(self.sc_c[:], self.sc_c[:], self.sc_d[:], ALU.mult), tag="rt_vl")
        dve(lambda eng: eng.tensor_tensor(self.sc_b[:], self.sc_b[:], self.ecap_s[:], ALU.add), tag="rt_sg")
        dve(lambda eng: eng.reduce_max(self.sc_sm[:, 32:32 + NT], A3(self.w_gates_tm), axis=AX.X), tag="rt_rm")
        dve(lambda eng: eng.tensor_tensor(
            A3(self.sc_e), A3(self.w_gates_tm),
            self.sc_sm[:, 32:32 + NT].unsqueeze(2).broadcast_to([128, NT, NE]), ALU.is_ge), tag="rt_s1")
        dve(lambda eng: eng.tensor_sub(self.sc_f[:], self.sc_d[:], self.sc_e[:]), tag="rt_s2")
        for k2, sel in ((0, self.sc_e), (1, self.sc_f)):
            dve((lambda sel: (lambda eng: eng.tensor_tensor(sel[:], sel[:], self.sc_c[:], ALU.mult)))(sel),
                tag=f"rt_a{k2}")
            dve((lambda sel: (lambda eng: eng.tensor_tensor(self.sc_a[:], sel[:], self.sc_b[:], ALU.mult)))(sel),
                tag=f"rt_q{k2}")
            dve(lambda eng: eng.reduce_sum(self.sc_sm[:, 64:64 + NT], A3(self.sc_a), axis=AX.X), tag=f"rt_sa{k2}")
            dve((lambda sel: (lambda eng: eng.reduce_sum(
                self.sc_sm[:, 96:96 + NT], A3(sel), axis=AX.X)))(sel), tag=f"rt_n{k2}")
            dve(lambda eng: eng.tensor_scalar(
                self.sc_sm[:, 128:128 + NT], self.sc_sm[:, 96:96 + NT], -1.0, -BIG,
                ALU.add, ALU.mult), tag=f"rt_u{k2}")
            dve(lambda eng: eng.tensor_tensor(
                self.sc_sm[:, 64:64 + NT], self.sc_sm[:, 64:64 + NT],
                self.sc_sm[:, 128:128 + NT], ALU.add), tag=f"rt_ss{k2}")
            dve((lambda k2: (lambda eng: eng.tensor_copy(
                self.s12_i[:, k2 * NT:(k2 + 1) * NT], self.sc_sm[:, 64:64 + NT])))(k2), tag=f"rt_ci{k2}")
        ev_idx = ev
        def mk_reg(eng):
            self._bc_reg = eng.to_reg(c.E * c.CAP - 1)
            return eng.nop()
        self.op("pool", mk_reg, (), [("ms8", 1)], tag="rt_reg")
        wr_all = [(s, self.p.cnt[s]) for s in ("xw0", "xw1", "ww") if s in self.p.cnt]
        ev_rnd = [None, None]
        for tc in range(c.NTC):
            par = tc % 2
            exb = self.dma("sp", f"xb{par}", (lambda tc, par: (lambda eng: eng.dma_start(
                self.xrb[par][:], self.xrow_d[tc * 128:(tc + 1) * 128, :])))(tc, par),
                wr_all + [ev_rnd[par]], tag=f"xrl{tc}")
            for k2 in range(2):
                self.dma("pool", f"scp{par}", (lambda tc, par, k2: (lambda eng: eng.indirect_dma_start(
                    self.xg_d[:],
                    IndirectOffsetOnAxis(self.s12_i[:, k2 * NT + tc:k2 * NT + tc + 1], 0),
                    self.xrb[par][:], None,
                    bounds_check=self._bc_reg, oob_is_err=False)))(tc, par, k2),
                    [exb, ev_idx] if k2 == 0 else [], tag=f"sct{tc}_{k2}")
            ev_rnd[par] = (f"scp{par}", self.p.cnt[f"scp{par}"])

    def make_hgen(self):
        c = self.cfg
        nh = c.TT // 512
        self.ev_hwr_ev = {}
        self.h_pos = -1
        st = {
            "u": 0,
            "ev_gelu": [None, None],
            "ev_hwr_free": [None, None, None, None],
            "ev_w1_free": [None, None],
            "piece_loaded": {},
            "chunk_loaded": {},
            "ev_xga_free": [None, None],
            "qg": [[], []],
        }
        self.hst = st
        self.ev_wslf_ev = {}

        def load_piece(P):
            if P >= c.NSG * c.NP1 or P in st["piece_loaded"]:
                return
            sgi2, hp2 = divmod(P, c.NP1)
            e2 = sgi2 // c.NCH
            bi = P % 2
            ev = self.dma("pool", f"w1{bi}", (lambda e2, hp2, bi: (lambda eng: eng.dma_start(
                self.w1buf[bi][:],
                self.w1b_d[e2, hp2].rearrange(
                    "p (dc hcl f) -> p dc hcl f", dc=c.NDC, f=128))))(e2, hp2, bi),
                [st["ev_w1_free"][bi], ("rl", 16 * self.n_relay)], tag=f"w1d{sgi2}_{hp2}")
            st["piece_loaded"][P] = ev

        def load_chunk(sgi2):
            if sgi2 >= c.NSG or sgi2 in st["chunk_loaded"]:
                return
            bi = sgi2 % 2
            rb = sgi2 * c.TT
            sem = f"xg{bi}"
            for dc in range(c.NDC):
                self.dma("sp", sem, (lambda sgi2, dc, bi, rb: (lambda eng: eng.dma_start(
                    self.xga[bi][:, dc * c.TT:(dc + 1) * c.TT],
                    self.xg_d[rb:rb + c.TT, dc * 128:(dc + 1) * 128], transpose=True)))(sgi2, dc, bi, rb),
                    [st["ev_xga_free"][bi]] if dc == 0 else [], tag=f"xgl{sgi2}_{dc}")
            self.dma("sp", sem, (lambda sgi2, bi, rb: (lambda eng: eng.dma_start(
                self.wsl[bi][:].rearrange("p (a w) -> p a w", w=2 * c.E),
                self.xg_d[rb:rb + c.TT, c.D:c.D + 2 * c.E].rearrange(
                    "(a p) w -> p a w", p=128))))(sgi2, bi, rb),
                [self.ev_wslf_ev.get(sgi2 - 2)], tag=f"wsl{sgi2}")
            st["chunk_loaded"][sgi2] = (sem, self.p.cnt[sem])
        self.load_chunk = load_chunk

        def units():
            for sgi in range(c.NSG):
                for hp in range(c.NP1):
                    for hcl in range(4):
                        yield sgi, hp, hcl
        it = units()

        def pull():
            try:
                sgi, hp, hcl = next(it)
            except StopIteration:
                return False
            e = sgi // c.NCH
            hc = hp * 4 + hcl
            P = sgi * c.NP1 + hp
            load_chunk(sgi)
            load_chunk(sgi + 1)
            load_piece(P)
            load_piece(P + 1)
            bi = P % 2
            cbi = sgi % 2
            u = st["u"]; st["u"] += 1
            qp = u % 2
            QR = self.Q[2 + qp]
            wts0 = [st["piece_loaded"][P], st["chunk_loaded"][sgi],
                    st["ev_gelu"][qp]] + st["qg"][qp]
            st["qg"][qp] = []
            for dc in range(c.NDC):
                for half in range(nh):
                    self.op("pe", (lambda hcl, half, dc, bi, cbi, QR: (lambda eng: eng.matmul(
                        QR[:, half * 512:half * 512 + 512], self.w1buf[bi][:, dc, hcl, :],
                        self.xga[cbi][:, dc * c.TT + half * 512: dc * c.TT + (half + 1) * 512],
                        start=(dc == 0), stop=(dc == c.NDC - 1))))(hcl, half, dc, bi, cbi, QR),
                        wts0 if (dc == 0 and half == 0) else (),
                        [("pe", 1)] if (dc == c.NDC - 1 and half == nh - 1) else (),
                        tag=f"h_mm{sgi}_{hc}_{half}_{dc}")
            evh = ("pe", self.p.cnt["pe"])
            if hcl == 3:
                st["ev_w1_free"][bi] = evh
            if hp == c.NP1 - 1 and hcl == 3:
                st["ev_xga_free"][cbi] = evh
            wbi = hc % 4
            ege = self.one("act", (lambda e, hc, wbi, QR: (lambda eng: eng.activation(
                self.h_wr[wbi][:, 0:c.TT], QR[:, 0:c.TT], AF.Gelu,
                bias=self.b1T[:, hc * c.E + e:hc * c.E + e + 1])))(e, hc, wbi, QR),
                [evh, st["ev_hwr_free"][wbi]], tag=f"gelu{sgi}_{hc}")
            st["ev_gelu"][qp] = ege
            self.dma("act", f"hw{wbi}", (lambda sgi, hc, wbi: (lambda eng: eng.dma_start(
                self.h_d[sgi, hc * 128:(hc + 1) * 128, :], self.h_wr[wbi][:, 0:c.TT])))(sgi, hc, wbi),
                [ege], tag=f"hwr{sgi}_{hc}")
            st["ev_hwr_free"][wbi] = (f"hw{wbi}", self.p.cnt[f"hw{wbi}"])
            self.ev_hwr_ev[(sgi, hc)] = st["ev_hwr_free"][wbi]
            if hc == c.HC - 1:
                self.h_pos = sgi
            return True
        self.pull_h = pull

    def pull_h_n(self, n):
        for _ in range(n):
            if not self.pull_h():
                return

    def pull_h_until(self, gi):
        while self.h_pos < gi:
            if not self.pull_h():
                raise RuntimeError("h generator exhausted early")

    def p2_attn(self):
        c = self.cfg
        self.op("pool", lambda eng: eng.memset(self.v_tm[0][:], 1.0), (), [("ms2", 1)], tag="vt_ms0")
        self.op("pool", lambda eng: eng.memset(self.v_tm[1][:], 1.0), (), [("ms3", 1)], tag="vt_ms1")
        ev_vtm_war = [("ms2", 1), ("ms3", 1)]
        ev_wtr_ev = [None, None]
        ev_stage_free = [None, None]
        ev_exp_q = [None, None]
        ev_attsb = [None, None, None, None]
        ev_y_free = None
        ev_qk_last_ev = None
        ev_ystage = [None, None]
        wd = {0: self.wqb_d, 1: self.wkb_d, 2: self.wvb_d}
        for g in range(c.NDC):
            wT = {0: self.wqT_g, 1: self.wkT_g, 2: self.wvT_g}
            ev_wT = {}
            for j in range(3):
                par = j % 2
                evs = self.dma("act", f"ws{par}", (lambda g, j, par: (lambda eng: eng.dma_start(
                    self.w_stage[par][:], wd[j][g * 128:(g + 1) * 128, :])))(g, j, par),
                    [ev_stage_free[par], ("rlw", 64)], tag=f"wst{g}_{j}")
                pv = self.Q[2][:].bitcast(BF16)
                lastev = None
                for dc in range(c.NDC):
                    half = dc % 2
                    sl = half * 1024
                    wts = []
                    if dc == 0:
                        wts = [evs, ev_y_free]
                    wts.append(ev_wtr_ev[half])
                    self.op("pe", (lambda par, sl, dc, pv: (lambda eng: eng.transpose(
                        pv[:, sl:sl + 128], self.w_stage[par][:, dc * 128:(dc + 1) * 128],
                        self.id_bf[:])))(par, sl, dc, pv), wts, [("pe", 1)], tag=f"wtr{g}_{j}_{dc}")
                    evt = ("pe", self.p.cnt["pe"])
                    lastev = self.one("act", (lambda dc, sl, pv, wTj=wT[j]: (lambda eng: eng.copy(
                        wTj[:, dc * 128:(dc + 1) * 128], pv[:, sl:sl + 128])))(dc, sl, pv),
                        [evt], tag=f"wtre{g}_{j}_{dc}")
                    ev_wtr_ev[half] = lastev
                ev_wT[j] = lastev
                ev_stage_free[par] = lastev
            self.relay_drip(-(-self.n_relay // c.NDC))
            ev_vev = [None, None]
            for kc in range(c.S // 128):
                par = kc % 2
                vp = self.Q[par][:, 0:128]
                for dc in range(c.NDC):
                    wts = []
                    if dc == 0:
                        wts = [ev_wT[2], ev_vev[par], ev_y_free if kc == 0 else None]
                        if kc < 2:
                            wts.append(ev_exp_q[par])
                    self.op("pe", (lambda kc, dc, vp: (lambda eng: eng.matmul(
                        vp, self.xn[:, dc * c.S + kc * 128: dc * c.S + (kc + 1) * 128],
                        self.wvT_g[:, dc * 128:(dc + 1) * 128],
                        start=(dc == 0), stop=False)))(kc, dc, vp), wts, (), tag=f"v_mm{g}_{kc}_{dc}")
                self.op("pe", (lambda g, vp: (lambda eng: eng.matmul(
                    vp, self.ones_f[:], self.bv_row[:, g * 128:(g + 1) * 128],
                    start=False, stop=True)))(g, vp), (), [("pe", 1)], tag=f"v_bv{g}_{kc}")
                evt = ("pe", self.p.cnt["pe"])
                wv = [evt]
                if kc < 2:
                    wv += ev_vtm_war
                self.one("dve", (lambda kc, vp: (lambda eng: eng.tensor_copy(
                    self.v_tm[0][:, kc * 65:kc * 65 + 64], vp[:, 0:64])))(kc, vp), wv, tag=f"v_evA{g}_{kc}")
                ev_vev[par] = self.one("dve", (lambda kc, vp: (lambda eng: eng.tensor_copy(
                    self.v_tm[1][:, kc * 65:kc * 65 + 64], vp[:, 64:128])))(kc, vp), [evt], tag=f"v_evB{g}_{kc}")
            ev_vready = ev_vev[0] if (c.S // 128) % 2 == 1 else ev_vev[1]
            QK = {0: (self.wqT_g, self.Q_g, 0), 1: (self.wkT_g, self.K_g, c.NDC)}
            nsl = c.S // 512
            ev_qk_ev = ev_qk_last_ev
            for j in (0, 1):
                wTj, dstT, bcol = QK[j]
                for hb in range((nsl + 3) // 4):
                    sls = list(range(hb * 4, min(nsl, hb * 4 + 4)))
                    for dc in range(c.NDC):
                        for i, s in enumerate(sls):
                            pq = self.Q[i // 2][:, (i % 2) * 512:(i % 2) * 512 + 512]
                            wts = []
                            if dc == 0 and i == 0:
                                wts = [ev_wT[j], ev_qk_ev, ev_exp_q[0], ev_exp_q[1],
                                       ev_vev[0], ev_vev[1]]
                            self.op("pe", (lambda j, dc, s, pq, wTj: (lambda eng: eng.matmul(
                                pq, wTj[:, dc * 128:(dc + 1) * 128],
                                self.xn[:, dc * c.S + s * 512: dc * c.S + (s + 1) * 512],
                                start=(dc == 0), stop=(dc == c.NDC - 1))))(j, dc, s, pq, wTj),
                                wts, [("pe", 1)] if (dc == c.NDC - 1 and i == len(sls) - 1) else (),
                                tag=f"qk{g}_{j}_{hb}_{dc}_{s}")
                    evt = ("pe", self.p.cnt["pe"])
                    for i in range(0, len(sls), 2):
                        n2 = min(2, len(sls) - i)
                        pq = self.Q[i // 2][:, 0:512 * n2]
                        dst = dstT[:, sls[i] * 512: (sls[i] + n2) * 512]
                        ev_qk_ev = self.one("act", (lambda pq, dst, g, bcol: (lambda eng: eng.activation(
                            dst, pq, AF.Identity, bias=self.bqkT[:, bcol + g:bcol + g + 1])))(pq, dst, g, bcol),
                            [evt], tag=f"qk_ev{g}_{j}_{hb}_{i}")
            ev_qk = ev_qk_ev
            ev_qk_last_ev = ev_qk_ev
            qm = self.scr[:, 16:17]
            km = self.scr[:, 17:18]
            r1 = self.scr1[:, 0:1]
            r2 = self.scr1[:, 1:2]
            eq1 = self.one("dve", lambda eng: eng.tensor_reduce(
                qm, self.Q_g[:], AX.X, ALU.max, apply_absolute_value=True), [ev_qk], tag=f"b_q{g}")
            ek1 = self.one("dve", lambda eng: eng.tensor_reduce(
                km, self.K_g[:], AX.X, ALU.max, apply_absolute_value=True), [eq1], tag=f"b_k{g}")
            et1 = self.op("pe", lambda eng: eng.transpose(
                self.Q[2][0:1, 0:128], qm, self.id_f[:]),
                [eq1, ev_y_free], [("pe", 1)], tag=f"b_tq{g}")[0]
            er1 = self.one("dve", lambda eng: eng.reduce_max(r1, self.Q[2][0:1, 0:128], axis=AX.X),
                           [et1], tag=f"b_rq{g}")
            et2 = self.op("pe", lambda eng: eng.transpose(
                self.Q[2][0:1, 128:256], km, self.id_f[:]), [ek1, er1], [("pe", 1)], tag=f"b_tk{g}")[0]
            er2 = self.one("dve", lambda eng: eng.reduce_max(r2, self.Q[2][0:1, 128:256], axis=AX.X),
                           [et2], tag=f"b_rk{g}")
            es1 = self.one("act", lambda eng: eng.activation(r1, r1, AF.Square), [er1], tag=f"b_sq{g}")
            es2 = self.one("act", lambda eng: eng.activation(r2, r2, AF.Square), [er2], tag=f"b_sk{g}")
            ea = self.one("dve", lambda eng: eng.tensor_add(r1, r1, r2), [es1, es2], tag=f"b_add{g}")
            en = self.one("act", lambda eng: eng.mul(r1, r1, -1.0), [ea], tag=f"b_neg{g}")
            eb = self.op("pe", lambda eng: eng.matmul(
                self.Q[2][0:128, 256:257], self.ones_f[:], r1[:, 0:1], start=True, stop=True),
                [en], [("pe", 1)], tag=f"b_bc{g}")[0]
            ebb = self.one("act", (lambda g: (lambda eng: eng.copy(
                self.negb[:, g:g + 1], self.Q[2][0:128, 256:257])))(g), [eb], tag=f"b_cp{g}")
            last_mul = ev_y_free
            for qt in range(c.NQT):
                kcm = 4 * (qt + 1)
                pend = []

                def emit_av(kc, ap, ee, kcm=None):
                    avw = [ee]
                    if kc == 0:
                        avw += [ev_vready, last_mul]
                    self.op("pe", (lambda kc, ap, kcm: (lambda eng: eng.matmul(
                        self.Q[2][0:65, 0:512], self.v_tm[0][:, kc * 65:(kc + 1) * 65],
                        self.att_sb[ap][:, 0:512], start=(kc == 0), stop=(kc == kcm - 1))))(kc, ap, kcm),
                        avw, (), tag=f"avA{g}_{kc}")
                    self.op("pe", (lambda kc, ap, kcm: (lambda eng: eng.matmul(
                        self.Q[2][0:65, 512:1024], self.v_tm[1][:, kc * 65:(kc + 1) * 65],
                        self.att_sb[ap][:, 512:1024], start=(kc == 0), stop=(kc == kcm - 1))))(kc, ap, kcm),
                        (), [("pe", 1)], tag=f"avB{g}_{kc}")
                    ev_attsb[ap] = ("pe", self.p.cnt["pe"])

                for kc in range(kcm):
                    kp = kc % 2
                    ap = kc % 4
                    sp_ = self.Q[kp]
                    w0 = [ev_exp_q[kp]]
                    if qt == 0 and kc == 0:
                        w0 += [ev_qk, ebb]
                    if qt > 0 and kc == 0:
                        w0.append(ev_epi_q0)
                    self.op("pe", (lambda qt, kc, sp_: (lambda eng: eng.matmul(
                        sp_[:, 0:512], self.K_g[0:64, kc * 128:(kc + 1) * 128],
                        self.Q_g[0:64, qt * 512:(qt + 1) * 512], start=True, stop=True,
                        tile_position=(0, 0))))(qt, kc, sp_), w0, (), tag=f"scA{g}_{qt}_{kc}")
                    self.op("pe", (lambda qt, kc, sp_: (lambda eng: eng.matmul(
                        sp_[:, 512:1024], self.K_g[64:128, kc * 128:(kc + 1) * 128],
                        self.Q_g[64:128, qt * 512:(qt + 1) * 512], start=True, stop=True,
                        tile_position=(64, 0))))(qt, kc, sp_), (), [("pe", 1)], tag=f"scB{g}_{qt}_{kc}")
                    ev_sc = ("pe", self.p.cnt["pe"])
                    wts = [ev_sc, ev_attsb[ap]]
                    if qt == 0 and kc == 0:
                        wts.append(ebb)
                    ee = self.one("act", (lambda g, ap, sp_: (lambda eng: eng.activation(
                        self.att_sb[ap][:], sp_[:], AF.Exp, bias=self.negb[:, g:g + 1],
                        scale=self.cfg.scale)))(g, ap, sp_), wts, tag=f"exp{g}_{qt}_{kc}")
                    ev_exp_q[kp] = ee
                    dnum = kc - 4 * qt
                    if dnum >= 0:
                        ee = self.one("dve", (lambda ap, dnum: (lambda eng: eng.tensor_tensor(
                            self.att_sb[ap][:].rearrange("p (h q) -> p h q", h=2),
                            self.att_sb[ap][:].rearrange("p (h q) -> p h q", h=2),
                            self.masks[:, dnum * 512:(dnum + 1) * 512].unsqueeze(1)
                                .broadcast_to([128, 2, 512]), ALU.mult)))(ap, dnum),
                            [ee], tag=f"msk{g}_{qt}_{kc}")
                    pend.append((kc, ap, ee))
                    if len(pend) > 2:
                        emit_av(*pend.pop(0), kcm=kcm)
                while pend:
                    emit_av(*pend.pop(0), kcm=kcm)
                ev_avB = ("pe", self.p.cnt["pe"])
                bslot = self.Q[0][0:64, 0:512]
                erA = self.one("act", lambda eng: eng.activation(
                    self.sumrow[:], self.Q[2][64:65, 0:512], AF.Ln), [ev_avB, last_mul], tag=f"nrA{g}_{qt}")
                ebcA = self.op("pe", lambda eng: eng.matmul(
                    bslot, self.ones_f[:, 0:64], self.sumrow[:], start=True, stop=True),
                    [erA, ev_exp_q[0]], [("pe", 1)], tag=f"nbA{g}_{qt}")[0]
                ecA = self.one("act", lambda eng: eng.activation(
                    self.recip64[:], bslot, AF.Exp, scale=-1.0), [ebcA], tag=f"ncA{g}_{qt}")
                mulA = self.one("dve", lambda eng: eng.tensor_mul(
                    self.y_stage[0][:], self.Q[2][0:64, 0:512], self.recip64[:]),
                    [ecA, ev_ystage[0]], tag=f"nmA{g}_{qt}")
                self.dma("sp", "yw0", (lambda g, qt: (lambda eng: eng.dma_start(
                    self.y_d_scr[0:64, g * c.S + qt * 512:g * c.S + (qt + 1) * 512],
                    self.y_stage[0][:])))(g, qt), [mulA], tag=f"ywA{g}_{qt}")
                ev_ystage[0] = ("yw0", self.p.cnt["yw0"])
                erB = self.one("act", lambda eng: eng.activation(
                    self.sumrow2[:], self.Q[2][64:65, 512:1024], AF.Ln), [ev_avB], tag=f"nrB{g}_{qt}")
                ebcB = self.op("pe", lambda eng: eng.matmul(
                    bslot, self.ones_f[:, 0:64], self.sumrow2[:], start=True, stop=True),
                    [erB, ecA], [("pe", 1)], tag=f"nbB{g}_{qt}")[0]
                ecB = self.one("act", lambda eng: eng.activation(
                    self.recip64[:], bslot, AF.Exp, scale=-1.0), [ebcB, mulA], tag=f"ncB{g}_{qt}")
                last_mul = self.one("dve", lambda eng: eng.tensor_mul(
                    self.y_stage[1][:], self.Q[2][0:64, 512:1024], self.recip64[:]),
                    [ecB, ev_ystage[1]], tag=f"nmB{g}_{qt}")
                self.dma("sp", "yw1", (lambda g, qt: (lambda eng: eng.dma_start(
                    self.y_d_scr[64:128, g * c.S + qt * 512:g * c.S + (qt + 1) * 512],
                    self.y_stage[1][:])))(g, qt), [last_mul], tag=f"ywB{g}_{qt}")
                ev_ystage[1] = ("yw1", self.p.cnt["yw1"])
                ev_epi_q0 = ecB
                ev_y_free = last_mul
            ev_vtm_war = [ev_attsb[0], ev_attsb[1]]

    def p3_proj(self):
        c = self.cfg
        ev_wtr_ev = [None, None]
        ev_stage_free = [None, None]
        for dcH in range(c.NDC):
            par = dcH % 2
            evs = self.dma("act", f"ws{par}", (lambda dcH, par: (lambda eng: eng.dma_start(
                self.w_stage[par][:], self.wpb_d[dcH * 128:(dcH + 1) * 128, :])))(dcH, par),
                [ev_stage_free[par], ("rlw", 64)], tag=f"wp_st{dcH}")
            pv = self.Q[2][:].bitcast(BF16)
            lastev = None
            for fc in range(c.NDC):
                half = fc % 2
                sl = half * 1024
                wts = [ev_wtr_ev[half]]
                if fc == 0:
                    wts.append(evs)
                self.op("pe", (lambda par, fc, sl, pv: (lambda eng: eng.transpose(
                    pv[:, sl:sl + 128], self.w_stage[par][:, fc * 128:(fc + 1) * 128],
                    self.id_bf[:])))(par, fc, sl, pv), wts, [("pe", 1)], tag=f"wp_tr{dcH}_{fc}")
                evt = ("pe", self.p.cnt["pe"])
                lastev = self.one("act", (lambda dcH, fc, sl, pv: (lambda eng: eng.copy(
                    self.wpT[:, fc * c.D + dcH * 128:fc * c.D + (dcH + 1) * 128],
                    pv[:, sl:sl + 128])))(dcH, fc, sl, pv), [evt], tag=f"wp_tre{dcH}_{fc}")
                ev_wtr_ev[half] = lastev
            ev_stage_free[par] = lastev
        ev_wp = lastev
        self.hst["qg"][0].append(ev_wp)
        ev_ev = [None, None]
        ev_ytc = [None, None]
        ev_xp_free = [None, None]
        self.ev_p3_xrd = [None, None]
        yv = self.y_d_scr[:].rearrange("p (fc s) -> p fc s", fc=c.NDC)
        for tc in range(c.NTC):
            par = tc % 2
            prev_aw = ev_ev[par]
            pp = self.Q[par]
            if tc % 2 == 0:
                self.pull_h_n(1)
            eyt = self.dma("sp", f"yt{par}", (lambda tc, par: (lambda eng: eng.dma_start(
                self.y_tc[par][:].rearrange("p (fc t) -> p fc t", fc=c.NDC),
                yv[:, :, tc * 128:(tc + 1) * 128])))(tc, par), [ev_ytc[par]], tag=f"ytd{tc}")
            for dh in range(c.NDH):
                po = pp[:, dh * 512:dh * 512 + c.DHW]
                for fc in range(c.NDC):
                    wts = []
                    if fc == 0 and dh == 0:
                        wts = [ev_wp, prev_aw, eyt]
                    self.op("pe", (lambda tc, dh, fc, po, par: (lambda eng: eng.matmul(
                        po, self.y_tc[par][:, fc * 128:(fc + 1) * 128],
                        self.wpT[:, fc * c.D + dh * c.DHW:fc * c.D + (dh + 1) * c.DHW],
                        start=(fc == 0), stop=False)))(tc, dh, fc, po, par), wts, (), tag=f"pj{tc}_{dh}_{fc}")
                self.op("pe", (lambda dh, po: (lambda eng: eng.matmul(
                    po, self.ones_f[:], self.bp_row[:, dh * c.DHW:(dh + 1) * c.DHW],
                    start=False, stop=False)))(dh, po), (), (), tag=f"pj_bp{tc}_{dh}")
                self.op("pe", (lambda tc, dh, po: (lambda eng: eng.matmul(
                    po, self.w_fm[:, tc * 128:(tc + 1) * 128],
                    self.b2_bf[:, dh * c.DHW:(dh + 1) * c.DHW],
                    start=False, stop=True)))(tc, dh, po), (),
                    [("pe", 1)] if dh == c.NDH - 1 else (), tag=f"pj_b2{tc}_{dh}")
            evt = ("pe", self.p.cnt["pe"])
            ev_ytc[par] = evt
            exl = self.dma("sp", f"xp{par}", (lambda tc, par: (lambda eng: eng.dma_start(
                self.x_rd[par][:], self.x_d[tc * 128:(tc + 1) * 128, :])))(tc, par),
                [ev_xp_free[par]], tag=f"pj_x{tc}")
            eev = self.one("dve", (lambda par, pp: (lambda eng: eng.tensor_add(
                self.attn_stage[par][:], pp[:, 0:c.D], self.x_rd[par][:])))(par, pp),
                [evt, prev_aw, exl], tag=f"pj_ev{tc}")
            ev_xp_free[par] = eev
            self.ev_p3_xrd[par] = eev
            self.dma("act", f"aw{par}", (lambda tc, par: (lambda eng: eng.dma_start(
                self.attn_d[tc * 128:(tc + 1) * 128, :], self.attn_stage[par][:])))(tc, par),
                [eev], tag=f"pj_wr{tc}")
            ev_ev[par] = (f"aw{par}", self.p.cnt[f"aw{par}"])

    def p4_ffn(self):
        c = self.cfg
        ev_w2_free = [None, None]
        ev_hrd_free = [None, None, None, None]
        ev_psum_q = [None] * 4
        ev_stop_q = [None] * 4
        ev_ostage = [None, None, None, None]
        hrd_ix = 0
        w2_ix = 0
        ost_ix = 0
        for sgi in range(c.NSG):
            e = sgi // c.NCH
            self.hst["qg"] = [[x for x in [ev_psum_q[2]] if x is not None],
                              [x for x in [ev_psum_q[3]] if x is not None]]
            cbi = sgi % 2
            self.load_chunk(sgi)
            wv3 = self.wsl[cbi][:].rearrange("p (a w) -> p a w", w=2 * c.E)
            ev_wslf = self.one("dve", (lambda e, cbi, wv3: (lambda eng: eng.tensor_add(
                self.wslf[cbi][:], wv3[:, :, 2 * e:2 * e + 1].squeeze(2),
                wv3[:, :, 2 * e + 1:2 * e + 2].squeeze(2))))(e, cbi, wv3),
                [self.hst["chunk_loaded"][sgi]], tag=f"wslf{sgi}")
            self.ev_wslf_ev[sgi] = ev_wslf
            self.pull_h_until(min(sgi + 1, c.NSG - 1))
            ge_last = [self.hst["ev_gelu"][0], self.hst["ev_gelu"][1]]
            for grp in range(0, c.NTCT, 4):
                ng = min(4, c.NTCT - grp)
                for hp2 in range(c.NP1):
                    bi = w2_ix % 2
                    self.dma("pool", f"w2{bi}", (lambda e, hp2, bi: (lambda eng: eng.dma_start(
                        self.w2buf[bi][:],
                        self.w2b_d[e, hp2].rearrange(
                            "p (hcl d) -> p hcl d", d=c.D))))(e, hp2, bi),
                        [ev_w2_free[bi], ("rl", 16 * self.n_relay)], tag=f"w2d{sgi}_{grp}_{hp2}")
                    evw2 = (f"w2{bi}", self.p.cnt[f"w2{bi}"])
                    w2_ix += 1
                    for hcl in range(4):
                        hc = hp2 * 4 + hcl
                        rbi = hrd_ix % 4
                        self.dma("sp", f"hr{rbi}", (lambda sgi, hc, rbi, grp, ng: (lambda eng: eng.dma_start(
                            self.h_rd[rbi][:, 0:ng * 128],
                            self.h_d[sgi, hc * 128:(hc + 1) * 128,
                                     grp * 128:(grp + ng) * 128])))(sgi, hc, rbi, grp, ng),
                            [self.ev_hwr_ev[(sgi, hc)], ev_hrd_free[rbi]], tag=f"hrd{sgi}_{grp}_{hc}")
                        evhr = (f"hr{rbi}", self.p.cnt[f"hr{rbi}"])
                        hrd_ix += 1
                        for tci in range(ng):
                            tc = grp + tci
                            QR = self.Q[tci]
                            wts = []
                            if tci == 0:
                                wts = [evw2, evhr]
                            if hc == 0:
                                wts.append(ev_psum_q[tci])
                                if tci >= 2:
                                    wts.append(ge_last[tci - 2])
                            for dh in range(c.NDH):
                                do_inc = ((tci == ng - 1) or (hc == c.HC - 1)) and dh == c.NDH - 1
                                self.op("pe", (lambda hcl, tci, rbi, bi, QR, hc, dh: (lambda eng: eng.matmul(
                                    QR[:, dh * c.DHW:(dh + 1) * c.DHW],
                                    self.h_rd[rbi][:, tci * 128:(tci + 1) * 128],
                                    self.w2buf[bi][:, hcl, dh * c.DHW:(dh + 1) * c.DHW],
                                    start=(hc == 0), stop=(hc == c.HC - 1))))(hcl, tci, rbi, bi, QR, hc, dh),
                                    wts if dh == 0 else (),
                                    [("pe", 1)] if do_inc else (),
                                    tag=f"o_mm{sgi}_{grp}_{hc}_{tci}_{dh}")
                            if hc == c.HC - 1:
                                ev_stop_q[tci] = ("pe", self.p.cnt["pe"])
                        evo = ("pe", self.p.cnt["pe"])
                        if hcl == 3:
                            ev_w2_free[bi] = evo
                        ev_hrd_free[rbi] = evo
                order = [2, 3, 0, 1] if ng == 4 else list(range(ng))
                for tci in order:
                    tc = grp + tci
                    QR = self.Q[tci]
                    wcol = self.wslf[cbi][:, tc:tc + 1]
                    evl = None
                    for dh in range(c.NDH):
                        o = ost_ix % 4
                        ost_ix += 1
                        ev = self.one("dve", (lambda QR, wcol, o, dh: (lambda eng: eng.tensor_scalar(
                            self.out_stage[o][:], QR[:, dh * c.DHW:(dh + 1) * c.DHW],
                            wcol, None, ALU.mult)))(QR, wcol, o, dh),
                            [ev_stop_q[tci], ev_ostage[o], ev_wslf], tag=f"stt{sgi}_{tc}_{dh}")
                        self.dma("act", f"ow{o}", (lambda sgi, dh, tc, o: (lambda eng: eng.dma_start(
                            self.outsl_d[sgi * c.TT + tc * 128: sgi * c.TT + (tc + 1) * 128,
                                         dh * c.DHW:(dh + 1) * c.DHW],
                            self.out_stage[o][:])))(sgi, dh, tc, o), [ev], tag=f"osw{sgi}_{tc}_{dh}")
                        ev_ostage[o] = (f"ow{o}", self.p.cnt[f"ow{o}"])
                        evl = ev
                    ev_psum_q[tci] = evl
        ow_all = [(s, self.p.cnt[s]) for s in ("ow0", "ow1", "ow2", "ow3") if s in self.p.cnt]
        aw_all = [(s, self.p.cnt[s]) for s in ("aw0", "aw1") if s in self.p.cnt]
        lanes = [self.attn_rd[0], self.attn_rd[1], self.x_rd[0], self.x_rd[1]]
        ev_lane = [None, None, self.ev_p3_xrd[0], self.ev_p3_xrd[1]]
        for tc in range(c.NTC):
            ln = tc % 4
            buf = lanes[ln]
            ea = self.dma("sp", f"al{ln}", (lambda tc, buf: (lambda eng: eng.dma_start(
                buf[:], self.attn_d[tc * 128:(tc + 1) * 128, :])))(tc, buf),
                aw_all + [ev_lane[ln]], tag=f"fa{tc}")
            evg = None
            for k2 in range(2):
                self.dma("pool", f"gp{ln}", (lambda tc, buf, k2: (lambda eng: eng.indirect_dma_start(
                    buf[:], None,
                    self.outsl_d[:],
                    IndirectOffsetOnAxis(self.s12_i[:, k2 * c.NTC + tc: k2 * c.NTC + tc + 1], 0),
                    bounds_check=self._bc_reg, oob_is_err=False,
                    compute_op=ALU.add)))(tc, buf, k2),
                    (ow_all + [ea]) if k2 == 0 else [evg], tag=f"gad{tc}_{k2}")
                evg = (f"gp{ln}", self.p.cnt[f"gp{ln}"])
            self.dma("sp", f"os{ln}", (lambda tc, buf: (lambda eng: eng.dma_start(
                self.y_d[tc * 128:(tc + 1) * 128, :], buf[:])))(tc, buf),
                [evg], tag=f"fout{tc}")
            ev_lane[ln] = (f"os{ln}", self.p.cnt[f"os{ln}"])

    def emit(self):
        nc = self.nc
        semh = self.semh = {s: nc.alloc_semaphore(s) for s in self.p.cnt}
        waited = {e: {} for e in ENGS}

        def run(eng_obj, ename):
            seen = waited[ename]
            for waits, emitfn, incs, tag in self.p.q[ename]:
                for s, v in waits:
                    if seen.get(s, -1) >= v:
                        continue
                    eng_obj.wait_ge(semh[s], v)
                    seen[s] = v
                inst = emitfn(eng_obj)
                for s, n in incs:
                    inst.then_inc(semh[s], n)

        with nc.Block() as blk:
            blk.tensor(lambda eng: run(eng, "pe"))
            blk.scalar(lambda eng: run(eng, "act"))
            blk.vector(lambda eng: run(eng, "dve"))
            blk.gpsimd(lambda eng: run(eng, "pool"))
            blk.sync(lambda eng: run(eng, "sp"))


def build_program(cfg: Cfg):
    return Builder(cfg).build()


_IN_KEYS = {
    "x": "L0_attn_x", "wq": "L0_attn_Wq", "bq": "L0_attn_bq", "wk": "L0_attn_Wk",
    "bk": "L0_attn_bk", "wv": "L0_attn_Wv", "bv": "L0_attn_bv", "wp": "L0_attn_Wp",
    "bp": "L0_attn_bp", "gw": "L0_ffn_gate_w", "gb": "L0_ffn_gate_b",
    "w1": "L0_ffn_w1", "b1": "L0_ffn_b1", "w2": "L0_ffn_w2", "b2": "L0_ffn_b2",
}


def kernel(**inputs):
    cfg = Cfg()
    n = 8
    x = np.ascontiguousarray(np.asarray(inputs["L0_attn_x"], dtype=np.float32))
    common = {k: np.ascontiguousarray(np.asarray(inputs[v], dtype=np.float32))
              for k, v in _IN_KEYS.items() if k != "x"}
    nc = build_program(cfg)
    in_maps = []
    for ci in range(n):
        m = dict(common)
        m["x"] = np.ascontiguousarray(x[ci])
        in_maps.append(m)
    res = run_bass_kernel_spmd(nc, in_maps, core_ids=list(range(n)))
    out = np.stack([res.results[ci]["y"] for ci in range(n)], axis=0)
    return out.astype(np.float32)
